# version 01
# baseline (speedup 1.0000x reference)
import numpy as np
import ml_dtypes
import concourse.bass as bass
import concourse.mybir as mybir
from concourse.bass_utils import run_bass_kernel_spmd
from contextlib import ExitStack

F32 = mybir.dt.float32
BF16 = mybir.dt.bfloat16
I16 = mybir.dt.int16
AF = mybir.ActivationFunctionType
ALU = mybir.AluOpType
AX = mybir.AxisListType

MAGIC = 12582912.0
NEGINF = -1.0e30
D = 1024
NCH = 8
NT = 4096
SEQ = 2048
HID = 3072
NHC = 24
QMAXA = 32767.0
EPS_LN = 1e-5
EPS_Q = 1e-12

ENG_ATTR = {"T": "tensor", "V": "vector", "S": "scalar", "G": "gpsimd", "Y": "sync"}


class Eng:

    def __init__(self, raw, w):
        self._raw = raw
        self._w = w
        self.last = None

    def __getattr__(self, name):
        v = getattr(self._raw, name)
        if not callable(v):
            return v
        def call(*a, **kw):
            r = v(*a, **kw)
            if hasattr(r, "then_inc"):
                self.last = r
            return r
        return call


class Sched:

    def __init__(self, nc, block, es):
        self.nc, self.block, self.es = nc, block, es
        self.cnt = {}
        self.hnd = {}
        self.real_cnt = {}
        self.free_pool = []
        self.free_pool_dma = []
        self.free_pool_dg = []
        self.persist = set()
        self.phase_names = []
        self.waited = {}
        self.ni = {k: 0 for k in ENG_ATTR}

    def sem(self, name):
        if name not in self.hnd:
            if name.startswith("dG"):
                pool = self.free_pool_dg
            elif name.startswith("d"):
                pool = self.free_pool_dma
            else:
                pool = self.free_pool
            if pool:
                h = pool.pop()
            else:
                h = self.es.enter_context(self.nc.semaphore(f"sm{len(self.real_cnt)}"))
                self.real_cnt[id(h)] = 0
            self.hnd[name] = h
            self.cnt[name] = self.real_cnt[id(h)]
            if self.phase_names is not None:
                self.phase_names.append(name)
        return self.hnd[name]

    def persist_name(self, name):
        self.sem(name)
        self.persist.add(name)

    def end_phase(self):
        for name in self.phase_names:
            if name in self.persist or name not in self.hnd:
                continue
            h = self.hnd.pop(name)
            self.real_cnt[id(h)] = self.cnt.pop(name)
            if name.startswith("dG"):
                self.free_pool_dg.append(h)
            elif name.startswith("d"):
                self.free_pool_dma.append(h)
            else:
                self.free_pool.append(h)
            self.waited = {k: v for k, v in self.waited.items() if k[1] != name}
        self.phase_names = []

    def inc(self, instr, name, k=1):
        h = self.sem(name)
        instr.then_inc(h, k)
        self.cnt[name] += k
        self.real_cnt[id(h)] = self.cnt[name]
        return (name, self.cnt[name])

    def sig(self, e, name, k=1):
        assert e.last is not None, f"sig({name}) with no prior instruction"
        r = self.inc(e.last, name, k)
        e.last = None
        return r

    def fence(self, e):
        name = "fn" + e._w
        assert e.last is not None
        tok = self.inc(e.last, name)
        e.last = None
        e._raw.wait_ge(self.hnd[name], tok[1])
        self.waited[(e._w, name)] = tok[1]
        return tok

    def bump(self, e, name, k=1):
        h = self.sem(name)
        e._raw.sem_inc(h, k)
        self.cnt[name] += k
        self.real_cnt[id(h)] = self.cnt[name]
        return (name, self.cnt[name])

    def tok(self, name):
        self.sem(name)
        return (name, self.cnt[name])

    def wait(self, e, tok):
        if tok is None:
            return
        name, val = tok
        if val <= 0:
            return
        key = (e._w, name)
        if self.waited.get(key, -1) >= val:
            return
        e._raw.wait_ge(self.hnd[name], val)
        self.waited[key] = val

    def on(self, w, f):
        self.ni[w] += 1
        getattr(self.block, ENG_ATTR[w])(lambda raw: f(Eng(raw, w)))


def build(upto=7, dumps=(), stage=99):
    nc = bass.Bass("TRN2", target_bir_lowering=False, detect_race_conditions=False)

    xin = nc.dram_tensor("xT", [NCH, 128, NT], F32, kind="ExternalInput").ap()
    w_up_raw = nc.dram_tensor("w_up", [HID, D], F32, kind="ExternalInput").ap()
    w_gate_raw = nc.dram_tensor("w_gate", [HID, D], F32, kind="ExternalInput").ap()
    w_down_raw = nc.dram_tensor("w_down", [D, HID], F32, kind="ExternalInput").ap()
    l0_w = {p: nc.dram_tensor(f"l0_w{p}T", [D, D], F32, kind="ExternalInput").ap()
            for p in ("q", "k", "v", "o")}
    l1_w = {p: nc.dram_tensor(f"l1_w{p}T", [D, D], F32, kind="ExternalInput").ap()
            for p in ("q", "k", "v", "o")}
    l1_b = {p: nc.dram_tensor(f"l1_b{p}", [128, 8], F32, kind="ExternalInput").ap()
            for p in ("q", "k", "v", "o")}
    wfT = nc.dram_tensor("wfT", [D, D], F32, kind="ExternalInput").ap()
    iden32 = nc.dram_tensor("iden32", [128, 128], F32, kind="ExternalInput").ap()
    idenb = nc.dram_tensor("idenb", [128, 128], BF16, kind="ExternalInput").ap()
    ones32 = nc.dram_tensor("ones32", [128, 1], F32, kind="ExternalInput").ap()
    masks = nc.dram_tensor("masks", [4, 128, 128], F32, kind="ExternalInput").ap()

    xout = nc.dram_tensor("outT", [NCH, 128, NT], F32, kind="ExternalOutput").ap()

    xres = {i: nc.dram_tensor(f"x{i}", [NCH, 128, NT], F32).ap() for i in range(1, 6)}
    pk_ug = nc.dram_tensor("pk_ug", [2, 2, NHC, 128, 8 * 128], BF16).ap()
    pk_d = nc.dram_tensor("pk_d", [2, NCH, 128, 24 * 128], BF16).ap()
    upraw = nc.dram_tensor("upraw", [NHC, 128, NT], F32).ap()
    gateraw = nc.dram_tensor("gateraw", [NHC, 128, NT], F32).ap()
    m2i = nc.dram_tensor("m2i", [NHC, 128, NT], I16).ap()
    prodb = nc.dram_tensor("prodb", [NHC, 128, NT], F32).ap()
    mmb = nc.dram_tensor("mmb", [NHC, 128, NT], F32).ap()
    rows = nc.dram_tensor("rows", [8, NT], F32).ap()
    dbg_sc = nc.dram_tensor("dbg_sc", [128, 16], F32).ap()
    dbg_mx = nc.dram_tensor("dbg_mx", [128, 384], F32).ap()
    dbg2 = nc.dram_tensor("dbg2", [128, 8], F32).ap()
    avh = nc.dram_tensor("avh", [8, 128, SEQ], F32).ap()
    cc_in = [nc.dram_tensor(f"cc_in{i}", [1, 2], F32) for i in range(8)]
    cc_out = [nc.dram_tensor(f"cc_out{i}", [1, 2], F32) for i in range(8)]

    named = {"x1": (xres[1], [NCH, 128, NT], F32), "x2": (xres[2], [NCH, 128, NT], F32),
             "x3": (xres[3], [NCH, 128, NT], F32), "x4": (xres[4], [NCH, 128, NT], F32),
             "x5": (xres[5], [NCH, 128, NT], F32),
             "pk_ug": (pk_ug, [2, 2, NHC, 128, 8 * 128], BF16),
             "pk_d": (pk_d, [2, NCH, 128, 24 * 128], BF16),
             "upraw": (upraw, [NHC, 128, NT], F32),
             "gateraw": (gateraw, [NHC, 128, NT], F32),
             "prodb": (prodb, [NHC, 128, NT], F32),
             "mmb": (mmb, [NHC, 128, NT], F32),
             "rows": (rows, [8, NT], F32),
             "dbg_sc": (dbg_sc, [128, 16], F32),
             "dbg_mx": (dbg_mx, [128, 384], F32),
             "dbg2": (dbg2, [128, 8], F32),
             "m2i": (m2i, [NHC, 128, NT], I16),
             "avh": (avh, [8, 128, SEQ], F32)}
    dump_outs = {nm: nc.dram_tensor("dump_" + nm, named[nm][1], named[nm][2],
                                    kind="ExternalOutput").ap() for nm in dumps}

    with ExitStack() as es:
        block = es.enter_context(nc.Block())
        S = Sched(nc, block, es)
        for w in ENG_ATTR:
            S.persist_name("bar" + w)
            S.persist_name("fn" + w)
        S.persist_name("dconst")

        c_iden = es.enter_context(nc.sbuf_tensor("c_iden", [128, 128], F32))
        c_idenb = es.enter_context(nc.sbuf_tensor("c_idenb", [128, 128], BF16))
        c_ones = es.enter_context(nc.sbuf_tensor("c_ones", [128, 1], F32))
        c_masks = es.enter_context(nc.sbuf_tensor("c_masks", [128, 4 * 128], F32))
        c_b = {p: es.enter_context(nc.sbuf_tensor(f"c_b{p}", [128, 8], F32))
               for p in ("q", "k", "v", "o")}
        c_sc = es.enter_context(nc.sbuf_tensor("c_sc", [128, 16], F32))
        rb1 = es.enter_context(nc.sbuf_tensor("rb1", [1, 2 * 512], F32))
        rb2 = es.enter_context(nc.sbuf_tensor("rb2", [1, 2 * 512], F32))
        rb3 = es.enter_context(nc.sbuf_tensor("rb3", [1, 2 * 512], F32))
        r32 = es.enter_context(nc.sbuf_tensor("r32", [32, 4 * 128], F32))
        s11 = es.enter_context(nc.sbuf_tensor("s11", [1, 2], F32))

        def y0(yq):
            for dst, src in [(c_iden, iden32), (c_idenb, idenb), (c_ones, ones32)]:
                S.inc(yq.dma_start(out=dst[:], in_=src), "dconst", 16)
            msrc = bass.AP(masks.tensor, 0, [[128, 128], [128 * 128, 4], [1, 128]])
            S.inc(yq.dma_start(out=c_masks[:].rearrange("p (m f) -> p m f", m=4),
                               in_=msrc), "dconst", 16)
            for p in ("q", "k", "v", "o"):
                S.inc(yq.dma_start(out=c_b[p][:], in_=l1_b[p]), "dconst", 16)
        S.on("Y", y0)
        tok_consts = S.tok("dconst")

        def barrier():
            for w in ENG_ATTR:
                S.on(w, (lambda w: lambda e: S.bump(e, "bar" + w))(w))
            toks = [S.tok("bar" + w) for w in ENG_ATTR]
            for w in ENG_ATTR:
                def body(e, w=w):
                    for tk in toks:
                        S.wait(e, tk)
                S.on(w, body)

        def par_max(vec_ap, out11, in_tok, tag, psb):
            def t(te):
                S.wait(te, in_tok)
                te.transpose(psb[0:1, 0:128], vec_ap, c_iden[:])
                S.sig(te, "pmt" + tag)
            def v(ve):
                S.wait(ve, S.tok("pmt" + tag))
                ve.tensor_reduce(out11, psb[0:1, 0:128], AX.X, ALU.max)
                S.sig(ve, "pmv" + tag)
            S.on("T", t)
            S.on("V", v)
            return S.tok("pmv" + tag)

        def collective_max(rnd, srcs, dst_tile, ncols, in_tok, tag):
            def g(ge):
                S.wait(ge, in_tok)
                for i in range(2):
                    s_ = srcs[min(i, len(srcs) - 1)]
                    S.inc(ge.dma_start(out=cc_in[rnd][0:1, i:i + 1], in_=s_), "dG", 16)
                S.wait(ge, S.tok("dG"))
                S.inc(ge.collective_compute(
                    "AllReduce", ALU.max, replica_groups=[list(range(8))],
                    ins=[cc_in[rnd].ap().opt()], outs=[cc_out[rnd].ap().opt()]), "cc")
                S.wait(ge, S.tok("cc"))
                rep = bass.AP(cc_out[rnd].ap().tensor, 0, [[0, 128], [1, ncols]])
                S.inc(ge.dma_start(out=dst_tile, in_=rep), "dG", 16)
                S.wait(ge, S.tok("dG"))
                S.bump(ge, "ccb" + tag)
            S.on("G", g)
            return S.tok("ccb" + tag)

        def stats_pass(xsrc, t0, T, tag, ph, psg, psh):
            xa = ph.enter_context(nc.sbuf_tensor(f"st_x_{tag}", [128, 2 * 512], F32))
            sq = ph.enter_context(nc.sbuf_tensor(f"st_q_{tag}", [128, 2 * 512], F32))
            nblk = T // 512
            slot_tok = [None, None]
            for blk in range(nblk):
                tb = t0 + blk * 512
                bsl = blk % 2
                bss = slice(bsl * 512, bsl * 512 + 512)
                for c in range(NCH):
                    sl = (blk * NCH + c) % 2
                    ss = slice(sl * 512, sl * 512 + 512)
                    def y(yq, c=c, tb=tb, sl=sl, ss=ss, st_=slot_tok[sl]):
                        S.wait(yq, st_)
                        S.inc(yq.dma_start(out=xa[:, ss], in_=xsrc[c, :, tb:tb + 512]),
                              f"d{tag}x{sl}", 16)
                    def s_(se, sl=sl, ss=ss, st_=slot_tok[sl]):
                        S.wait(se, S.tok(f"d{tag}x{sl}"))
                        S.wait(se, st_)
                        se.activation(sq[:, ss], xa[:, ss], AF.Square)
                        S.sig(se, f"{tag}sq")
                    def t(te, c=c, ss=ss):
                        S.wait(te, S.tok(f"{tag}sq"))
                        if c == 0:
                            S.wait(te, S.tok(f"{tag}ev"))
                        te.matmul(psg[0:1, 0:512], c_ones[:], xa[:, ss],
                                  start=(c == 0), stop=(c == NCH - 1))
                        te.matmul(psh[0:1, 0:512], c_ones[:], sq[:, ss],
                                  start=(c == 0), stop=(c == NCH - 1))
                        S.sig(te, f"{tag}tc")
                    S.on("Y", y)
                    S.on("S", s_)
                    S.on("T", t)
                    slot_tok[sl] = S.tok(f"{tag}tc")
                def v(ve, bss=bss, bsl=bsl):
                    S.wait(ve, S.tok(f"{tag}tc"))
                    S.wait(ve, S.tok(f"d{tag}mr{bsl}"))
                    ve.tensor_scalar(rb1[0:1, bss], psg[0:1, 0:512], 1.0 / D, 0.0,
                                     ALU.mult, ALU.add)
                    ve.tensor_scalar(rb3[0:1, bss], psh[0:1, 0:512], 1.0 / D, 0.0,
                                     ALU.mult, ALU.add)
                    S.sig(ve, f"{tag}ev")
                    ve.tensor_tensor(rb2[0:1, bss], rb1[0:1, bss], rb1[0:1, bss],
                                     ALU.mult)
                    ve.tensor_tensor(rb3[0:1, bss], rb3[0:1, bss], rb2[0:1, bss],
                                     ALU.subtract)
                    ve.tensor_scalar(rb3[0:1, bss], rb3[0:1, bss], EPS_LN, 0.0,
                                     ALU.add, ALU.add)
                    S.sig(ve, f"{tag}vv")
                def s2_(se, bss=bss):
                    S.wait(se, S.tok(f"{tag}vv"))
                    se.activation(rb2[0:1, bss], rb3[0:1, bss], AF.Sqrt)
                    S.sig(se, f"{tag}sr")
                def v3(ve, bss=bss):
                    S.wait(ve, S.tok(f"{tag}sr"))
                    ve.reciprocal(rb3[0:1, bss], rb2[0:1, bss])
                    S.fence(ve)
                    ve.tensor_copy(rb2[0:1, bss.start:bss.start + 1],
                                   rb3[0:1, bss.start:bss.start + 1])
                    S.sig(ve, f"{tag}rr")
                def y2(yq, tb=tb, bss=bss, bsl=bsl):
                    S.wait(yq, S.tok(f"{tag}rr"))
                    S.inc(yq.dma_start(out=rows[0:1, tb:tb + 512], in_=rb1[0:1, bss]),
                          f"d{tag}mr{bsl}", 16)
                    S.inc(yq.dma_start(out=rows[1:2, tb:tb + 512], in_=rb3[0:1, bss]),
                          f"d{tag}mr{bsl}", 16)
                S.on("V", v)
                S.on("S", s2_)
                S.on("V", v3)
                S.on("Y", y2)
            def yfin(yq):
                S.wait(yq, S.tok(f"d{tag}mr0"))
                S.wait(yq, S.tok(f"d{tag}mr1"))
                S.bump(yq, f"{tag}done")
            S.on("Y", yfin)
            rtok = S.tok(f"{tag}done")
            return rtok, rtok

        def bcast_row(yq, dst, row_idx, t0, T, semname):
            src = bass.AP(rows.tensor, row_idx * NT + t0, [[0, 128], [1, T]])
            return S.inc(yq.dma_start(out=dst, in_=src), semname, 16)

        def emit_dequant(ph):
            PSD = {nm: ph.enter_context(nc.psum_tensor(f"psd_{nm}", [128, 512], F32))
                   for nm in "ab"}
            swr = ph.enter_context(nc.sbuf_tensor("dq_wr", [128, HID], F32))
            sbm = ph.enter_context(nc.sbuf_tensor("dq_bm", [128, 96], F32))
            sinv = ph.enter_context(nc.sbuf_tensor("dq_inv", [128, 96], F32))
            st_ = ph.enter_context(nc.sbuf_tensor("dq_t", [128, HID], F32))
            shi = ph.enter_context(nc.sbuf_tensor("dq_hi", [128, 2 * 512], BF16))
            slo = ph.enter_context(nc.sbuf_tensor("dq_lo", [128, 2 * 512], BF16))
            jobs = [(w_up_raw, D, NHC, ("ug", 0)), (w_gate_raw, D, NHC, ("ug", 1)),
                    (w_down_raw, HID, NCH, ("d", 0))]
            tr_done = None
            v_done = None
            for raw, K, OC, dest in jobs:
                nb = K // 32
                nkc = K // 128
                for oc in range(OC):
                    def y(yq, raw=raw, oc=oc, K=K, v_done=v_done):
                        S.wait(yq, v_done)
                        S.inc(yq.dma_start(out=swr[:, 0:K],
                                           in_=raw[oc * 128:(oc + 1) * 128, :]), "dqx", 16)
                    def v(ve, K=K, nb=nb, tr_done=tr_done, wprev=v_done):
                        S.wait(ve, S.tok("dqx"))
                        S.wait(ve, tr_done)
                        S.wait(ve, wprev)
                        ve.tensor_reduce(sbm[:, 0:nb],
                                         swr[:, 0:K].rearrange("p (b k) -> p b k", k=32),
                                         AX.X, ALU.max, apply_absolute_value=True)
                        S.fence(ve)
                        ve.tensor_scalar(sbm[:, 0:nb], sbm[:, 0:nb], 1.0 / 7.0, EPS_Q,
                                         ALU.mult, ALU.max)
                        S.fence(ve)
                        ve.reciprocal(sinv[:, 0:nb], sbm[:, 0:nb])
                        S.fence(ve)
                        for b in range(nb):
                            ve.tensor_scalar(st_[:, b * 32:(b + 1) * 32],
                                             swr[:, b * 32:(b + 1) * 32],
                                             sinv[:, b:b + 1], MAGIC, ALU.mult, ALU.add)
                        S.fence(ve)
                        ve.tensor_scalar(st_[:, 0:K], st_[:, 0:K], MAGIC, 0.0,
                                         ALU.subtract, ALU.add)
                        for b in range(nb):
                            ve.tensor_scalar(st_[:, b * 32:(b + 1) * 32],
                                             st_[:, b * 32:(b + 1) * 32],
                                             sbm[:, b:b + 1], 0.0, ALU.mult, ALU.add)
                        S.fence(ve)
                        ve.tensor_copy(sbm[:, 0:1], sbm[:, 0:1])
                        S.sig(ve, "zqw")
                    S.on("Y", y)
                    S.on("V", v)
                    v_done = S.tok("zqw")
                    ng4 = (nkc + 3) // 4
                    for g4 in range(ng4):
                        ntile = min(4, nkc - g4 * 4)
                        pb = "ab"[g4 % 2]
                        off = 0 if pb == "a" else 512
                        def t(te, g4=g4, ntile=ntile, pb=pb, v_done=v_done):
                            S.wait(te, v_done)
                            S.wait(te, S.tok(f"zqpf{pb}"))
                            for i in range(ntile):
                                kc = g4 * 4 + i
                                te.transpose(PSD[pb][:, i * 128:(i + 1) * 128],
                                             st_[:, kc * 128:(kc + 1) * 128], c_iden[:])
                            S.sig(te, "zqt")
                        def sv(se, ntile=ntile, pb=pb, off=off, tcur=None):
                            S.wait(se, S.tok("zqt"))
                            S.wait(se, S.tok(f"zqhf{pb}"))
                            se.activation(shi[:, off:off + ntile * 128],
                                          PSD[pb][:, 0:ntile * 128], AF.Identity)
                            S.sig(se, f"zqh{pb}")
                        def vv(ve, ntile=ntile, pb=pb, off=off):
                            S.wait(ve, S.tok(f"zqh{pb}"))
                            ve.tensor_tensor(slo[:, off:off + ntile * 128],
                                             PSD[pb][:, 0:ntile * 128],
                                             shi[:, off:off + ntile * 128], ALU.subtract)
                            S.sig(ve, f"zql{pb}")
                        def g2(ge, g4=g4, ntile=ntile, pb=pb, off=off, dest=dest, oc=oc):
                            opb = "b" if pb == "a" else "a"
                            prev = S.tok(f"dGpk{opb}")
                            S.wait(ge, S.tok(f"zql{pb}"))
                            S.wait(ge, prev)
                            S.bump(ge, f"zqhf{opb}")
                            S.bump(ge, f"zqpf{opb}")
                            for i in range(ntile):
                                kc = g4 * 4 + i
                                for half, srcbuf in ((0, shi), (1, slo)):
                                    if dest[0] == "ug":
                                        base = (((dest[1] * 2 + half) * NHC + oc) * 128) * (8 * 128)
                                        dap = bass.AP(pk_ug.tensor, base + kc * 128,
                                                      [[8 * 128, 128], [1, 128]])
                                    else:
                                        base = ((half * NCH + oc) * 128) * (24 * 128)
                                        dap = bass.AP(pk_d.tensor, base + kc * 128,
                                                      [[24 * 128, 128], [1, 128]])
                                    S.inc(ge.dma_start(
                                        out=dap, in_=srcbuf[:, off + i * 128:off + (i + 1) * 128]),
                                        f"dGpk{pb}", 16)
                        S.on("T", t)
                        S.on("S", sv)
                        S.on("V", vv)
                        S.on("G", g2)
                    tr_done = S.tok("zqt")
            def gend(ge):
                S.wait(ge, S.tok("dGpka"))
                S.wait(ge, S.tok("dGpkb"))
            S.on("G", gend)

        def emit_ffn0(xsrc, xdst, rbase, P, ph):
            mb = ph.enter_context(nc.sbuf_tensor(f"f{P}_mb", [128, 2048], F32))
            rb = ph.enter_context(nc.sbuf_tensor(f"f{P}_rb", [128, 2048], F32))
            xt = ph.enter_context(nc.sbuf_tensor(f"f{P}_xt", [128, 2 * 2048], F32))
            t1 = ph.enter_context(nc.sbuf_tensor(f"f{P}_t1", [128, 2048], F32))
            mxs = ph.enter_context(nc.sbuf_tensor(f"f{P}_mx", [128, 18], F32))
            mxs2 = ph.enter_context(nc.sbuf_tensor(f"f{P}_mx2", [128, 384], F32))
            umx = ph.enter_context(nc.sbuf_tensor(f"f{P}_umx", [128, 2], F32))
            gv = ph.enter_context(nc.sbuf_tensor(f"f{P}_gv", [128, 2], F32))

            stsub = ExitStack()
            psg = stsub.enter_context(nc.psum_tensor(f"f{P}_psg", [128, 512], F32))
            psh = stsub.enter_context(nc.psum_tensor(f"f{P}_psh", [128, 512], F32))
            _, rtok = stats_pass(xsrc, 0, NT, f"{P}st", ph, psg, psh)
            stsub.close()
            PSF = {nm: ph.enter_context(nc.psum_tensor(f"f{P}_psb_{nm}", [128, 512], F32))
                   for nm in "abcdefgh"}
            if stage < 2:
                return

            tg = f"{P}L"
            xslot = [None, None]
            for half in range(2):
                tb = half * 2048
                bused = S.tok(f"{tg}v")
                def y(yq, tb=tb, bused=bused):
                    S.wait(yq, rtok)
                    S.wait(yq, bused)
                    bcast_row(yq, mb[:, :], 0, tb, 2048, f"d{tg}b")
                    bcast_row(yq, rb[:, :], 1, tb, 2048, f"d{tg}b")
                S.on("Y", y)
                btok = S.tok(f"d{tg}b")
                for c in range(NCH):
                    sl = c % 2
                    ss = slice(sl * 2048, sl * 2048 + 2048)
                    def y2(yq, c=c, tb=tb, sl=sl, ss=ss, xs=xslot[sl]):
                        S.wait(yq, xs)
                        S.inc(yq.dma_start(out=xt[:, ss], in_=xsrc[c, :, tb:tb + 2048]),
                              f"d{tg}x{sl}", 16)
                    def v(ve, c=c, half=half, sl=sl, ss=ss, btok=btok):
                        S.wait(ve, btok)
                        S.wait(ve, S.tok(f"d{tg}x{sl}"))
                        ve.tensor_tensor(t1[:], xt[:, ss], mb[:], ALU.subtract)
                        S.sig(ve, f"{tg}xf")
                        ve.tensor_tensor(t1[:], t1[:], rb[:], ALU.mult)
                        ve.tensor_reduce(mxs[:, half * NCH + c:half * NCH + c + 1],
                                         t1[:], AX.X, ALU.max, apply_absolute_value=True)
                        S.sig(ve, f"{tg}v")
                    S.on("Y", y2)
                    S.on("V", v)
                    xslot[sl] = S.tok(f"{tg}xf")
            def vm(ve):
                S.wait(ve, S.tok(f"{tg}v"))
                ve.tensor_reduce(mxs[:, 16:17], mxs[:, 0:16], AX.X, ALU.max)
                S.sig(ve, f"{tg}m")
            S.on("V", vm)
            pt = par_max(mxs[:, 16:17], s11[0:1, 0:1], S.tok(f"{tg}m"), tg, PSF["g"])
            ct1 = collective_max(rbase + 0, [s11[0:1, 0:1]], gv[:, 0:1], 1, pt, tg)
            def ya0(yq):
                S.wait(yq, ct1)
                mr_src = bass.AP(rows.tensor, 0 * NT, [[128, 32], [1, 128]])
                rr_src = bass.AP(rows.tensor, 1 * NT, [[128, 32], [1, 128]])
                S.inc(yq.dma_start(out=r32[0:32, 0:128], in_=mr_src), f"d{tg}mr32", 16)
                S.inc(yq.dma_start(out=r32[0:32, 128:256], in_=rr_src), f"d{tg}mr32", 16)
            def va(ve):
                S.wait(ve, ct1)
                S.wait(ve, S.tok(f"d{tg}mr32"))
                ve.tensor_scalar(c_sc[:, 0:1], gv[:, 0:1], 1.0 / QMAXA, EPS_Q,
                                 ALU.mult, ALU.max)
                S.fence(ve)
                ve.reciprocal(c_sc[:, 1:2], c_sc[:, 0:1])
                S.fence(ve)
                ve.tensor_scalar(r32[0:32, 256:384], r32[0:32, 128:256],
                                 c_sc[0:32, 1:2], 0.0, ALU.mult, ALU.add)
                S.fence(ve)
                ve.tensor_tensor(r32[0:32, 384:512], r32[0:32, 0:128],
                                 r32[0:32, 256:384], ALU.mult)
                S.sig(ve, f"{tg}ab")
            def ya(yq):
                S.wait(yq, S.tok(f"{tg}ab"))
                a_dst = bass.AP(rows.tensor, 2 * NT, [[128, 32], [1, 128]])
                b_dst = bass.AP(rows.tensor, 3 * NT, [[128, 32], [1, 128]])
                S.inc(yq.dma_start(out=a_dst, in_=r32[0:32, 256:384]), f"d{tg}ab", 16)
                S.inc(yq.dma_start(out=b_dst, in_=r32[0:32, 384:512]), f"d{tg}ab", 16)
            S.on("Y", ya0)
            S.on("V", va)
            S.on("Y", ya)
            abtok = S.tok(f"d{tg}ab")

            if stage < 3:
                return
            qt = f"{P}Q"
            ph1 = ExitStack()
            kh = ph1.enter_context(nc.sbuf_tensor(f"f{P}_kh", [128, NCH * 2048], BF16))
            kl = ph1.enter_context(nc.sbuf_tensor(f"f{P}_kl", [128, NCH * 2048], BF16))
            wsl = [ph1.enter_context(nc.sbuf_tensor(f"f{P}_w{i}", [128, 1024], BF16))
                   for i in range(8)]
            evb = [ph1.enter_context(nc.sbuf_tensor(f"f{P}_ev{i}", [128, 512], F32))
                   for i in range(4)]
            UPB, GAB = "abcd", "efgh"
            xslot2 = [None, None]
            wslot_free = [None, None]
            bank_free = {b: None for b in "abcdefgh"}
            ev_free = [None, None]
            mm_half_tok = None
            for half in range(2):
                tb = half * 2048
                bused = S.tok(f"{qt}k")
                def y(yq, tb=tb, bused=bused):
                    S.wait(yq, abtok)
                    S.wait(yq, bused)
                    bcast_row(yq, mb[:, :], 2, tb, 2048, f"d{qt}b")
                    bcast_row(yq, rb[:, :], 3, tb, 2048, f"d{qt}b")
                S.on("Y", y)
                btok = S.tok(f"d{qt}b")
                for c in range(NCH):
                    sl = c % 2
                    ss = slice(sl * 2048, sl * 2048 + 2048)
                    def y2(yq, c=c, tb=tb, sl=sl, ss=ss, xs=xslot2[sl]):
                        S.wait(yq, xs)
                        S.inc(yq.dma_start(out=xt[:, ss],
                                           in_=xsrc[c, :, tb:tb + 2048]), f"d{qt}x{sl}", 16)
                    def v(ve, c=c, sl=sl, ss=ss, half=half, btok=btok,
                          mht=mm_half_tok):
                        S.wait(ve, btok)
                        S.wait(ve, S.tok(f"d{qt}x{sl}"))
                        if half == 1:
                            S.wait(ve, mht)
                        ve.tensor_tensor(t1[:], xt[:, ss], mb[:], ALU.mult)
                        S.sig(ve, f"{qt}xf")
                        ve.tensor_tensor(t1[:], t1[:], rb[:], ALU.subtract)
                        ve.tensor_scalar(t1[:], t1[:], MAGIC, MAGIC, ALU.add, ALU.subtract)
                        cs = slice(c * 2048, (c + 1) * 2048)
                        ve.tensor_copy(kh[:, cs], t1[:])
                        ve.tensor_tensor(kl[:, cs], t1[:], kh[:, cs], ALU.subtract)
                        S.sig(ve, f"{qt}k")
                    S.on("Y", y2)
                    S.on("V", v)
                    xslot2[sl] = S.tok(f"{qt}xf")
                ktok = S.tok(f"{qt}k")
                for hc in range(NHC):
                    ws = hc % 2
                    def yw(yq, hc=hc, ws=ws, wf=wslot_free[ws]):
                        S.wait(yq, wf)
                        for j, (var, hl) in enumerate(((0, 0), (0, 1), (1, 0), (1, 1))):
                            S.inc(yq.dma_start(out=wsl[ws * 4 + j][:],
                                               in_=pk_ug[var, hl, hc]), f"d{qt}w{ws}", 16)
                    S.on("Y", yw)
                    wtok = S.tok(f"d{qt}w{ws}")
                    for tt in range(4):
                        pu, pg = UPB[tt], GAB[tt]
                        def t(te, hc=hc, tt=tt, pu=pu, pg=pg, ws=ws, ktok=ktok, wtok=wtok,
                              bfu=bank_free[pu], bfg=bank_free[pg]):
                            S.wait(te, ktok)
                            S.wait(te, wtok)
                            S.wait(te, bfu)
                            S.wait(te, bfg)
                            for var, pb in ((0, pu), (1, pg)):
                                whi = wsl[ws * 4 + var * 2]
                                wlo = wsl[ws * 4 + var * 2 + 1]
                                for kc in range(NCH):
                                    rs = slice(kc * 2048 + tt * 512, kc * 2048 + tt * 512 + 512)
                                    wk = slice(kc * 128, (kc + 1) * 128)
                                    te.matmul(PSF[pb][:, 0:512], whi[:, wk], kh[:, rs],
                                              start=(kc == 0), stop=False)
                                    te.matmul(PSF[pb][:, 0:512], whi[:, wk], kl[:, rs],
                                              start=False, stop=False)
                                    te.matmul(PSF[pb][:, 0:512], wlo[:, wk], kh[:, rs],
                                              start=False, stop=(kc == NCH - 1))
                            S.sig(te, f"{qt}mm")
                        S.on("T", t)
                        mmtok = S.tok(f"{qt}mm")
                        idx = (half * NHC + hc) * 4 + tt
                        def v2(ve, tt=tt, pu=pu, pg=pg, mmtok=mmtok, ef=ev_free[tt % 2],
                               idx=idx):
                            S.wait(ve, mmtok)
                            S.wait(ve, ef)
                            e0, e1 = evb[tt % 2 * 2], evb[tt % 2 * 2 + 1]
                            ve.tensor_scalar(e0[:], PSF[pu][:, 0:512], c_sc[:, 0:1], 0.0,
                                             ALU.mult, ALU.add)
                            S.sig(ve, f"{qt}ec")
                            ve.tensor_scalar(e1[:], PSF[pg][:, 0:512], c_sc[:, 0:1], 0.0,
                                             ALU.mult, ALU.add)
                            S.sig(ve, f"{qt}ec")
                            ve.tensor_reduce(mxs2[:, idx:idx + 1], PSF[pu][:, 0:512], AX.X,
                                             ALU.max, apply_absolute_value=True)
                            S.sig(ve, f"{qt}pv")
                            ve.tensor_reduce(mxs2[:, 192 + idx:193 + idx], PSF[pg][:, 0:512],
                                             AX.X, ALU.max, apply_absolute_value=True)
                            S.sig(ve, f"{qt}pv")
                        S.on("V", v2)
                        pvtok = S.tok(f"{qt}pv")
                        bank_free[pu] = pvtok
                        bank_free[pg] = pvtok
                        ectok = S.tok(f"{qt}ec")
                        def y3(yq, hc=hc, tt=tt, half=half, ectok=ectok):
                            S.wait(yq, ectok)
                            e0, e1 = evb[tt % 2 * 2], evb[tt % 2 * 2 + 1]
                            off = half * 2048 + tt * 512
                            S.inc(yq.dma_start(out=upraw[hc, :, off:off + 512],
                                               in_=e0[:]), f"d{qt}s{tt % 2}", 16)
                            S.inc(yq.dma_start(out=gateraw[hc, :, off:off + 512],
                                               in_=e1[:]), f"d{qt}s{tt % 2}", 16)
                        S.on("Y", y3)
                        ev_free[tt % 2] = S.tok(f"d{qt}s{tt % 2}")
                    wslot_free[ws] = S.tok(f"{qt}mm")
                mm_half_tok = S.tok(f"{qt}mm")
            if stage < 4:
                return
            def yrs(yq):
                S.wait(yq, S.tok(f"d{qt}s0"))
                S.wait(yq, S.tok(f"d{qt}s1"))
                S.bump(yq, f"{qt}rawdone")
            S.on("Y", yrs)
            rawtok = S.tok(f"{qt}rawdone")
            allpv = S.tok(f"{qt}pv")
            def vfin(ve):
                S.wait(ve, allpv)
                ve.tensor_reduce(umx[:, 0:1], mxs2[:, 0:192], AX.X, ALU.max)
                S.sig(ve, f"{qt}fin")
                ve.tensor_reduce(umx[:, 1:2], mxs2[:, 192:384], AX.X, ALU.max)
                S.sig(ve, f"{qt}fin")
            S.on("V", vfin)
            fintok = S.tok(f"{qt}fin")
            def tq(te):
                S.wait(te, fintok)
                te.transpose(PSF["a"][0:1, 0:128], umx[:, 0:1], c_iden[:])
                te.transpose(PSF["a"][0:1, 128:256], umx[:, 1:2], c_iden[:])
                S.sig(te, f"{qt}tp")
            def vq2(ve):
                S.wait(ve, S.tok(f"{qt}tp"))
                ve.tensor_reduce(s11[0:1, 0:1], PSF["a"][0:1, 0:128], AX.X, ALU.max)
                S.sig(ve, f"{qt}pm")
                ve.tensor_reduce(s11[0:1, 1:2], PSF["a"][0:1, 128:256], AX.X, ALU.max)
                S.sig(ve, f"{qt}pm")
            S.on("T", tq)
            S.on("V", vq2)
            ct23 = collective_max(rbase + 1, [s11[0:1, 0:1], s11[0:1, 1:2]],
                                  gv[:, 0:2], 2, S.tok(f"{qt}pm"), qt)
            def vsc(ve):
                S.wait(ve, ct23)
                ve.tensor_scalar(mxs[:, 0:2], gv[:, 0:2], c_sc[:, 0:1], 0.0,
                                 ALU.mult, ALU.add)
                S.fence(ve)
                ve.tensor_scalar(c_sc[:, 4:6], mxs[:, 0:2], 1.0 / QMAXA, EPS_Q,
                                 ALU.mult, ALU.max)
                S.fence(ve)
                ve.reciprocal(c_sc[:, 2:3], c_sc[:, 4:5])
                ve.reciprocal(c_sc[:, 3:4], c_sc[:, 5:6])
                ve.memset(umx[:, 0:1], 0.0)
                S.fence(ve)
                ve.tensor_copy(c_sc[:, 14:15], c_sc[:, 2:3])
                S.sig(ve, f"{qt}sc")
            S.on("V", vsc)
            sctok = S.tok(f"{qt}sc")

            if stage < 5:
                return
            ph1.close()
            at = f"{P}A"
            a_up = ph.enter_context(nc.sbuf_tensor(f"f{P}_aup", [128, 2 * 512], F32))
            a_ga = ph.enter_context(nc.sbuf_tensor(f"f{P}_aga", [128, 2 * 512], F32))
            a_t = ph.enter_context(nc.sbuf_tensor(f"f{P}_atA", [128, 2 * 512], F32))
            a_t2 = ph.enter_context(nc.sbuf_tensor(f"f{P}_atB", [128, 2 * 512], F32))
            a_i16 = ph.enter_context(nc.sbuf_tensor(f"f{P}_i16", [128, 2 * 512], I16))
            a_pr = ph.enter_context(nc.sbuf_tensor(f"f{P}_apr", [128, 2 * 512], F32))
            ntile = NHC * NT // 512
            in_free = [None, None]
            out_free = [None, None]
            for j in range(ntile):
                hc, off = divmod(j * 512, NT)
                sl = j % 2
                ss = slice(sl * 512, sl * 512 + 512)
                def y(yq, hc=hc, off=off, sl=sl, ss=ss, if_=in_free[sl]):
                    S.wait(yq, sctok)
                    S.wait(yq, rawtok)
                    S.wait(yq, if_)
                    S.inc(yq.dma_start(out=a_up[:, ss], in_=upraw[hc, :, off:off + 512]),
                          f"d{at}i{sl}", 16)
                    S.inc(yq.dma_start(out=a_ga[:, ss], in_=gateraw[hc, :, off:off + 512]),
                          f"d{at}i{sl}", 16)
                def v(ve, sl=sl, ss=ss, of_=out_free[sl]):
                    S.wait(ve, S.tok(f"d{at}i{sl}"))
                    S.wait(ve, sctok)
                    S.wait(ve, of_)
                    ve.tensor_scalar(a_t[:, ss], a_up[:, ss], c_sc[:, 2:3], MAGIC,
                                     ALU.mult, ALU.add)
                    S.fence(ve)
                    ve.tensor_scalar(a_t[:, ss], a_t[:, ss], MAGIC, 0.0,
                                     ALU.subtract, ALU.add)
                    ve.tensor_copy(a_i16[:, ss], a_t[:, ss])
                    S.fence(ve)
                    ve.tensor_scalar(a_t[:, ss], a_ga[:, ss], c_sc[:, 3:4], MAGIC,
                                     ALU.mult, ALU.add)
                    iftok = S.sig(ve, f"{at}if")
                    S.wait(ve, iftok)
                    ve.tensor_scalar(a_t[:, ss], a_t[:, ss], MAGIC, 0.0,
                                     ALU.subtract, ALU.add)
                    S.sig(ve, f"{at}gq")
                def s(se, sl=sl, ss=ss):
                    S.wait(se, S.tok(f"{at}gq"))
                    se.activation(a_t2[:, ss], a_t[:, ss], AF.Sigmoid,
                                  scale=c_sc[:, 5:6])
                    S.sig(se, f"{at}sg")
                def v2(ve, sl=sl, ss=ss, j=j):
                    S.wait(ve, S.tok(f"{at}sg"))
                    ve.tensor_scalar(a_t2[:, ss], a_t2[:, ss], 65536.0, MAGIC,
                                     ALU.mult, ALU.add)
                    ve.tensor_scalar(a_t2[:, ss], a_t2[:, ss], MAGIC, 65535.0,
                                     ALU.subtract, ALU.min)
                    ve.tensor_scalar(a_t2[:, ss], a_t2[:, ss], 1.0 / 65536.0, 0.0,
                                     ALU.mult, ALU.add)
                    ve.tensor_tensor(a_pr[:, ss], a_t[:, ss], a_t2[:, ss], ALU.mult)
                    S.sig(ve, f"{at}pr")
                    ve.tensor_reduce(mxs2[:, j:j + 1], a_pr[:, ss], AX.X, ALU.max,
                                     apply_absolute_value=True)
                    S.sig(ve, f"{at}prm")
                def y2(yq, hc=hc, off=off, sl=sl, ss=ss):
                    S.wait(yq, S.tok(f"{at}pr"))
                    S.inc(yq.dma_start(out=m2i[hc, :, off:off + 512], in_=a_i16[:, ss]),
                          f"d{at}o{sl}", 16)
                    S.inc(yq.dma_start(out=prodb[hc, :, off:off + 512], in_=a_pr[:, ss]),
                          f"d{at}o{sl}", 16)
                S.on("Y", y)
                S.on("V", v)
                S.on("S", s)
                S.on("V", v2)
                S.on("Y", y2)
                in_free[sl] = S.tok(f"{at}if")
                out_free[sl] = S.tok(f"d{at}o{sl}")
            def vfin4(ve):
                S.wait(ve, S.tok(f"{at}prm"))
                ve.tensor_reduce(umx[:, 0:1], mxs2[:, 0:192], AX.X, ALU.max)
                S.sig(ve, f"{at}fin")
            S.on("V", vfin4)
            pt4 = par_max(umx[:, 0:1], s11[0:1, 0:1], S.tok(f"{at}fin"), at, PSF["g"])
            ct4 = collective_max(rbase + 2, [s11[0:1, 0:1]], gv[:, 0:1], 1, pt4, at)
            def vs4(ve):
                S.wait(ve, ct4)
                ve.tensor_scalar(mxs[:, 0:1], gv[:, 0:1], c_sc[:, 5:6], 0.0,
                                 ALU.mult, ALU.add)
                S.fence(ve)
                ve.tensor_scalar(c_sc[:, 6:7], mxs[:, 0:1], 1.0 / QMAXA, EPS_Q,
                                 ALU.mult, ALU.max)
                S.fence(ve)
                ve.reciprocal(c_sc[:, 13:14], c_sc[:, 6:7])
                S.fence(ve)
                ve.tensor_tensor(c_sc[:, 7:8], c_sc[:, 13:14], c_sc[:, 5:6], ALU.mult)
                S.sig(ve, f"{at}s4")
            S.on("V", vs4)
            s4tok = S.tok(f"{at}s4")
            def yod(yq):
                S.wait(yq, S.tok(f"d{at}o0"))
                S.wait(yq, S.tok(f"d{at}o1"))
                S.bump(yq, f"{at}odone")
            S.on("Y", yod)
            aodone = S.tok(f"{at}odone")

            if stage < 6:
                return
            bt = f"{P}B"
            in_free = [None, None]
            out_free = [None, None]
            for j in range(ntile):
                hc, off = divmod(j * 512, NT)
                sl = j % 2
                ss = slice(sl * 512, sl * 512 + 512)
                def y(yq, hc=hc, off=off, sl=sl, ss=ss, if_=in_free[sl]):
                    S.wait(yq, s4tok)
                    S.wait(yq, aodone)
                    S.wait(yq, if_)
                    S.inc(yq.dma_start(out=a_up[:, ss], in_=prodb[hc, :, off:off + 512]),
                          f"d{bt}i{sl}", 16)
                    S.inc(yq.dma_start(out=a_i16[:, ss], in_=m2i[hc, :, off:off + 512]),
                          f"d{bt}i{sl}", 16)
                def v(ve, sl=sl, ss=ss, of_=out_free[sl], j=j):
                    S.wait(ve, S.tok(f"d{bt}i{sl}"))
                    S.wait(ve, s4tok)
                    S.wait(ve, of_)
                    ve.tensor_scalar(a_t[:, ss], a_up[:, ss], c_sc[:, 7:8], MAGIC,
                                     ALU.mult, ALU.add)
                    S.fence(ve)
                    ve.tensor_scalar(a_t[:, ss], a_t[:, ss], MAGIC, 0.0,
                                     ALU.subtract, ALU.add)
                    ve.tensor_copy(a_t2[:, ss], a_i16[:, ss])
                    S.sig(ve, f"{bt}if")
                    ve.tensor_tensor(a_pr[:, ss], a_t[:, ss], a_t2[:, ss], ALU.mult)
                    S.sig(ve, f"{bt}pr")
                    ve.tensor_reduce(mxs2[:, j:j + 1], a_pr[:, ss], AX.X, ALU.max,
                                     apply_absolute_value=True)
                    S.sig(ve, f"{bt}prm")
                def y2(yq, hc=hc, off=off, sl=sl, ss=ss):
                    S.wait(yq, S.tok(f"{bt}pr"))
                    S.inc(yq.dma_start(out=mmb[hc, :, off:off + 512], in_=a_pr[:, ss]),
                          f"d{bt}o{sl}", 16)
                S.on("Y", y)
                S.on("V", v)
                S.on("Y", y2)
                in_free[sl] = S.tok(f"{bt}if")
                out_free[sl] = S.tok(f"d{bt}o{sl}")
            def vfin5(ve):
                S.wait(ve, S.tok(f"{bt}prm"))
                ve.tensor_reduce(umx[:, 0:1], mxs2[:, 0:192], AX.X, ALU.max)
                S.sig(ve, f"{bt}fin")
            S.on("V", vfin5)
            pt5 = par_max(umx[:, 0:1], s11[0:1, 0:1], S.tok(f"{bt}fin"), bt, PSF["g"])
            ct5 = collective_max(rbase + 3, [s11[0:1, 0:1]], gv[:, 0:1], 1, pt5, bt)
            def vs5(ve):
                S.wait(ve, ct5)
                ve.tensor_tensor(c_sc[:, 8:9], c_sc[:, 6:7], c_sc[:, 4:5], ALU.mult)
                S.fence(ve)
                ve.tensor_scalar(mxs[:, 0:1], gv[:, 0:1], c_sc[:, 8:9], 0.0,
                                 ALU.mult, ALU.add)
                S.fence(ve)
                ve.tensor_scalar(c_sc[:, 9:10], mxs[:, 0:1], 1.0 / QMAXA, EPS_Q,
                                 ALU.mult, ALU.max)
                S.fence(ve)
                ve.reciprocal(c_sc[:, 10:11], c_sc[:, 9:10])
                S.fence(ve)
                ve.tensor_tensor(c_sc[:, 11:12], c_sc[:, 8:9], c_sc[:, 10:11], ALU.mult)
                ve.tensor_scalar(c_sc[:, 12:13], c_sc[:, 9:10], 0.5, 0.0, ALU.mult, ALU.add)
                S.sig(ve, f"{bt}s5")
            S.on("V", vs5)
            s5tok = S.tok(f"{bt}s5")
            def yod2(yq):
                S.wait(yq, S.tok(f"d{bt}o0"))
                S.wait(yq, S.tok(f"d{bt}o1"))
                S.bump(yq, f"{bt}odone")
            S.on("Y", yod2)
            bodone = S.tok(f"{bt}odone")

            if stage < 7:
                return
            dt = f"{P}D"
            m5h = ph.enter_context(nc.sbuf_tensor(f"f{P}_m5h", [128, NHC * 512], BF16))
            m5l = ph.enter_context(nc.sbuf_tensor(f"f{P}_m5l", [128, NHC * 512], BF16))
            dwb = [ph.enter_context(nc.sbuf_tensor(f"f{P}_dw{i}", [128, 24 * 128], BF16))
                   for i in range(4)]
            in_free = [None, None]
            w_free = [None, None]
            bank_free = {"a": None, "b": None}
            x_free = {"a": None, "b": None}
            y_free = {"a": None, "b": None}
            mm_tt_tok = None
            for tt in range(NT // 512):
                off = tt * 512
                for hc in range(NHC):
                    sl = hc % 2
                    ss = slice(sl * 512, sl * 512 + 512)
                    def y(yq, hc=hc, off=off, sl=sl, ss=ss, if_=in_free[sl]):
                        S.wait(yq, s5tok)
                        S.wait(yq, bodone)
                        S.wait(yq, if_)
                        S.inc(yq.dma_start(out=a_up[:, ss], in_=mmb[hc, :, off:off + 512]),
                              f"d{dt}i{sl}", 16)
                    def v(ve, hc=hc, sl=sl, ss=ss, tt=tt, mtt=mm_tt_tok):
                        S.wait(ve, S.tok(f"d{dt}i{sl}"))
                        S.wait(ve, s5tok)
                        if tt > 0 and hc == 0:
                            S.wait(ve, mtt)
                        ve.tensor_scalar(a_t[:, ss], a_up[:, ss], c_sc[:, 11:12], MAGIC,
                                         ALU.mult, ALU.add)
                        iftok = S.sig(ve, f"{dt}if")
                        S.wait(ve, iftok)
                        ve.tensor_scalar(a_t[:, ss], a_t[:, ss], MAGIC, 0.0,
                                         ALU.subtract, ALU.add)
                        hs = slice(hc * 512, (hc + 1) * 512)
                        ve.tensor_copy(m5h[:, hs], a_t[:, ss])
                        ve.tensor_tensor(m5l[:, hs], a_t[:, ss], m5h[:, hs], ALU.subtract)
                        S.sig(ve, f"{dt}m5")
                    S.on("Y", y)
                    S.on("V", v)
                    in_free[sl] = S.tok(f"{dt}if")
                m5tok = S.tok(f"{dt}m5")
                for dc in range(NCH):
                    ws = dc % 2
                    pb = "ab"[dc % 2]
                    def yw(yq, dc=dc, ws=ws, wf=w_free[ws]):
                        S.wait(yq, wf)
                        for j, hl in enumerate((0, 1)):
                            S.inc(yq.dma_start(out=dwb[ws * 2 + j][:], in_=pk_d[hl, dc]),
                                  f"d{dt}w{ws}", 16)
                    S.on("Y", yw)
                    wtok = S.tok(f"d{dt}w{ws}")
                    def t(te, dc=dc, ws=ws, pb=pb, m5tok=m5tok, wtok=wtok,
                          bf=bank_free[pb]):
                        S.wait(te, m5tok)
                        S.wait(te, wtok)
                        S.wait(te, bf)
                        whi, wlo = dwb[ws * 2], dwb[ws * 2 + 1]
                        for kc in range(NHC):
                            ks = slice(kc * 512, kc * 512 + 512)
                            wk = slice(kc * 128, (kc + 1) * 128)
                            te.matmul(PSF[pb][:, 0:512], whi[:, wk], m5h[:, ks],
                                      start=(kc == 0), stop=False)
                            te.matmul(PSF[pb][:, 0:512], whi[:, wk], m5l[:, ks],
                                      start=False, stop=False)
                            te.matmul(PSF[pb][:, 0:512], wlo[:, wk], m5h[:, ks],
                                      start=False, stop=(kc == NHC - 1))
                        S.sig(te, f"{dt}mm")
                    S.on("T", t)
                    w_free[ws] = S.tok(f"{dt}mm")
                    mmtok = S.tok(f"{dt}mm")
                    def y2(yq, dc=dc, off=off, pb=pb, xf=x_free[pb]):
                        S.wait(yq, xf)
                        xoff = 0 if pb == "a" else 512
                        S.inc(yq.dma_start(out=xt[:, xoff:xoff + 512],
                                           in_=xsrc[dc, :, off:off + 512]), f"d{dt}x{pb}", 16)
                    def v2(ve, dc=dc, off=off, pb=pb, mmtok=mmtok, yf=y_free[pb]):
                        S.wait(ve, mmtok)
                        S.wait(ve, S.tok(f"d{dt}x{pb}"))
                        S.wait(ve, yf)
                        xoff = 0 if pb == "a" else 512
                        ysl = t1[:, 0:512] if pb == "a" else t1[:, 512:1024]
                        ve.tensor_scalar(ysl, PSF[pb][:, 0:512], c_sc[:, 12:13], 0.0,
                                         ALU.mult, ALU.add)
                        pvt = S.sig(ve, f"{dt}pv")
                        S.wait(ve, pvt)
                        ve.tensor_tensor(ysl, ysl, xt[:, xoff:xoff + 512], ALU.add)
                        S.sig(ve, f"{dt}yv")
                    S.on("Y", y2)
                    S.on("V", v2)
                    bank_free[pb] = S.tok(f"{dt}pv")
                    x_free[pb] = S.tok(f"{dt}yv")
                    yvtok = S.tok(f"{dt}yv")
                    def y3(yq, dc=dc, off=off, pb=pb, yvtok=yvtok):
                        S.wait(yq, yvtok)
                        ysl = t1[:, 0:512] if pb == "a" else t1[:, 512:1024]
                        S.inc(yq.dma_start(out=xdst[dc, :, off:off + 512], in_=ysl),
                              f"d{dt}o", 16)
                    S.on("Y", y3)
                    y_free[pb] = S.tok(f"d{dt}o")
                mm_tt_tok = S.tok(f"{dt}mm")
            def yend(yq):
                S.wait(yq, S.tok(f"d{dt}o"))
            S.on("Y", yend)


        def attn_tiles(layer, qi_g):
            if layer == 1 or qi_g <= 3:
                return [(0, 0, qi_g + 1)], {qi_g: 0}
            if qi_g == 4:
                return [(0, 0, 5)], {0: 2, 4: 0}
            return [(0, qi_g - 4, 5), (8, 0, 1)], {0: 1, 4: 0, 8: 3}

        def emit_attn(xsrc, xdst, layer, P, ph):
            L1 = (layer == 1)
            wsrc = l1_w if L1 else l0_w
            stsub = ExitStack()
            psg = stsub.enter_context(nc.psum_tensor(f"a{P}_psg", [128, 512], F32))
            psh = stsub.enter_context(nc.psum_tensor(f"a{P}_psh", [128, 512], F32))
            _, rtok = stats_pass(xsrc, 0, NT, f"{P}st", ph, psg, psh)
            stsub.close()
            ps_sc = ph.enter_context(nc.psum_tensor(f"a{P}_sc", [128, 2048], F32))
            ps_tp = ph.enter_context(nc.psum_tensor(f"a{P}_tp", [128, 512], F32))
            ps_av = ph.enter_context(nc.psum_tensor(f"a{P}_av", [128, 512], F32))
            pj = [ph.enter_context(nc.psum_tensor(f"a{P}_pj{i}", [128, 512], F32))
                  for i in range(2)]
            mb = ph.enter_context(nc.sbuf_tensor(f"a{P}_mb", [128, 2048], F32))
            rb = ph.enter_context(nc.sbuf_tensor(f"a{P}_rb", [128, 2048], F32))
            lnh = ph.enter_context(nc.sbuf_tensor(f"a{P}_lnh", [128, NCH * 2048], BF16))
            lnl = ph.enter_context(nc.sbuf_tensor(f"a{P}_lnl", [128, NCH * 2048], BF16))
            xt = ph.enter_context(nc.sbuf_tensor(f"a{P}_xt", [128, 2048], F32))
            st32 = ph.enter_context(nc.sbuf_tensor(f"a{P}_st", [128, 2048], F32))
            dall = ph.enter_context(nc.sbuf_tensor(f"a{P}_da", [128, 32], F32))
            biasv = ph.enter_context(nc.sbuf_tensor(f"a{P}_bv", [128, 4], F32))
            xprev = [None]
            for s_ in range(2):
                t0 = s_ * 2048
                Q = f"{P}s{s_}"
                prev_omm = S.tok(f"{P}s{s_-1}omm") if s_ else None
                def yb(yq, t0=t0, lc=S.tok(f"{P}alnc")):
                    S.wait(yq, rtok)
                    if lc[1] > 0:
                        S.wait(yq, lc)
                    bcast_row(yq, mb[:, :], 0, t0, 2048, f"d{Q}b")
                    bcast_row(yq, rb[:, :], 1, t0, 2048, f"d{Q}b")
                S.on("Y", yb)
                btok = S.tok(f"d{Q}b")
                for c in range(NCH):
                    def y2(yq, c=c, t0=t0, xs=xprev[0]):
                        S.wait(yq, xs)
                        S.inc(yq.dma_start(out=xt[:, :], in_=xsrc[c, :, t0:t0 + 2048]),
                              f"d{Q}x", 16)
                    def v(ve, c=c, btok=btok, xd=None):
                        S.wait(ve, btok)
                        S.wait(ve, S.tok(f"d{Q}x"))
                        ve.tensor_tensor(st32[:], xt[:, :], mb[:], ALU.subtract)
                        S.sig(ve, f"{P}axf")
                        ve.tensor_tensor(st32[:], st32[:], rb[:], ALU.mult)
                        cs = slice(c * 2048, (c + 1) * 2048)
                        ve.tensor_copy(lnh[:, cs], st32[:])
                        ve.tensor_tensor(lnl[:, cs], st32[:], lnh[:, cs], ALU.subtract)
                        S.sig(ve, f"{P}alnc")
                    S.on("Y", y2)
                    S.on("V", v)
                    xprev[0] = S.tok(f"{P}axf")
                lntok = S.tok(f"{P}alnc")
                subA = ExitStack()
                w32 = subA.enter_context(nc.sbuf_tensor(f"a{P}{s_}_w32", [128, 1024], F32))
                whi = subA.enter_context(nc.sbuf_tensor(f"a{P}{s_}_whi", [128, 1024], BF16))
                wlo = subA.enter_context(nc.sbuf_tensor(f"a{P}{s_}_wlo", [128, 1024], BF16))
                qh = subA.enter_context(nc.sbuf_tensor(f"a{P}{s_}_qh", [128, 2048], BF16))
                ql = subA.enter_context(nc.sbuf_tensor(f"a{P}{s_}_ql", [128, 2048], BF16))
                kh2 = subA.enter_context(nc.sbuf_tensor(f"a{P}{s_}_kh", [128, 2048], BF16))
                kl2 = subA.enter_context(nc.sbuf_tensor(f"a{P}{s_}_kl", [128, 2048], BF16))
                v32 = subA.enter_context(nc.sbuf_tensor(f"a{P}{s_}_v32", [128, 2048], F32))
                vT = subA.enter_context(nc.sbuf_tensor(f"a{P}{s_}_vT", [128, 2048], F32))
                p32 = subA.enter_context(nc.sbuf_tensor(f"a{P}{s_}_p32", [128, 2048], F32))
                pT = subA.enter_context(nc.sbuf_tensor(f"a{P}{s_}_pT", [128, 2048], F32))
                av2 = subA.enter_context(nc.sbuf_tensor(f"a{P}{s_}_av2", [64, 2048], F32))
                ivd = subA.enter_context(nc.sbuf_tensor(f"a{P}{s_}_ivd", [64, 2048], F32))
                for hg in range(8):
                    lastqk = S.tok(f"{Q}qk")
                    lastav = S.tok(f"{Q}avmm")
                    for pi, pname in enumerate("qkv"):
                        def yw(yq, pname=pname, hg=hg, pm=S.tok(f"{Q}pmm"),
                               prev_omm=prev_omm, first=(hg == 0 and pi == 0)):
                            S.wait(yq, pm)
                            if first and prev_omm is not None:
                                S.wait(yq, prev_omm)
                            src = bass.AP(wsrc[pname].tensor, hg * 128,
                                          [[1024, 128], [128 * 1024, 8], [1, 128]])
                            S.inc(yq.dma_start(
                                out=w32[:].rearrange("p (k f) -> p k f", k=8), in_=src),
                                f"d{Q}w", 16)
                        def vw(ve, wtok=None):
                            S.wait(ve, S.tok(f"d{Q}w"))
                            ve.tensor_copy(whi[:], w32[:])
                            ve.tensor_tensor(wlo[:], w32[:], whi[:], ALU.subtract)
                            S.sig(ve, f"{Q}wsp")
                        S.on("Y", yw)
                        S.on("V", vw)
                        wsptok = S.tok(f"{Q}wsp")
                        for tq in range(4):
                            pjb = pj[tq % 2]
                            def t(te, tq=tq, pjb=pjb, wsptok=wsptok, lntok=lntok,
                                  pf=S.tok(f"{Q}pev"), pi=pi, lastqk=lastqk, lastav=lastav):
                                S.wait(te, wsptok)
                                S.wait(te, lntok)
                                if pf[1] >= 2:
                                    S.wait(te, (pf[0], pf[1] - 1))
                                if pi < 2:
                                    S.wait(te, lastqk)
                                else:
                                    S.wait(te, lastav)
                                for kc in range(NCH):
                                    rs = slice(kc * 2048 + tq * 512, kc * 2048 + tq * 512 + 512)
                                    wk = slice(kc * 128, (kc + 1) * 128)
                                    te.matmul(pjb[:, 0:512], whi[:, wk], lnh[:, rs],
                                              start=(kc == 0), stop=False)
                                    te.matmul(pjb[:, 0:512], wlo[:, wk], lnh[:, rs],
                                              start=False, stop=False)
                                    te.matmul(pjb[:, 0:512], whi[:, wk], lnl[:, rs],
                                              start=False, stop=(kc == NCH - 1))
                                S.sig(te, f"{Q}pmm")
                            S.on("T", t)
                            mmtok = S.tok(f"{Q}pmm")
                            def sev(se, tq=tq, pjb=pjb, pname=pname, hg=hg, mmtok=mmtok,
                                    sf=S.tok(f"{Q}spl")):
                                S.wait(se, mmtok)
                                if pname != "v":
                                    S.wait(se, sf)
                                dst = (v32 if pname == "v" else st32)[:, tq * 512:(tq + 1) * 512]
                                if L1:
                                    se.activation(dst, pjb[:, 0:512], AF.Identity,
                                                  bias=c_b[pname][:, hg:hg + 1])
                                else:
                                    se.activation(dst, pjb[:, 0:512], AF.Identity)
                                S.sig(se, f"{Q}pev")
                            S.on("S", sev)
                        evtok = S.tok(f"{Q}pev")
                        if pname in ("q", "k"):
                            dsth, dstl = (qh, ql) if pname == "q" else (kh2, kl2)
                            def vsp(ve, dsth=dsth, dstl=dstl, evtok=evtok):
                                S.wait(ve, evtok)
                                ve.tensor_copy(dsth[:], st32[:])
                                ve.tensor_tensor(dstl[:], st32[:], dsth[:], ALU.subtract)
                                S.sig(ve, f"{Q}spl")
                            S.on("V", vsp)
                        else:
                            for g4 in range(4):
                                def tv(te, g4=g4, evtok=evtok, tf=S.tok(f"{Q}tpf")):
                                    S.wait(te, evtok)
                                    S.wait(te, tf)
                                    for i in range(4):
                                        kt = g4 * 4 + i
                                        te.transpose(ps_tp[:, i * 128:(i + 1) * 128],
                                                     v32[:, kt * 128:(kt + 1) * 128],
                                                     c_iden[:])
                                    S.sig(te, f"{Q}vtt")
                                def vv2(ve, g4=g4, vt=S.tok(f"{Q}vtt")):
                                    S.wait(ve, (f"{Q}vtt", vt[1] + 1))
                                    ve.tensor_copy(vT[:, g4 * 512:(g4 + 1) * 512],
                                                   ps_tp[:, 0:512])
                                    S.sig(ve, f"{Q}tpf")
                                S.on("T", tv)
                                S.on("V", vv2)
                    vttok = S.tok(f"{Q}tpf")
                    for h in range(2):
                        hofs = 64 * h
                        avwr = S.tok(f"d{Q}avh")
                        for qi_g in range(16):
                            runs, mtiles = attn_tiles(layer, qi_g)
                            spans = [(s0, n) for (s0, _, n) in runs]
                            qsl = slice(qi_g * 128, (qi_g + 1) * 128)
                            def tqk(te, runs=runs, qsl=qsl, hofs=hofs,
                                    expprev=S.tok(f"{Q}exp"), vttok=vttok,
                                    lastsp=S.tok(f"{Q}spl")):
                                S.wait(te, lastsp)
                                S.wait(te, vttok)
                                S.wait(te, expprev)
                                for pss in range(3):
                                    lh = (qh, ql, qh)[pss]
                                    rh = (kh2, kh2, kl2)[pss]
                                    lt = lh[hofs:hofs + 64, qsl]
                                    for (slot0, kt0, n) in runs:
                                        done = 0
                                        while done < n:
                                            pos = slot0 + done
                                            take = min(4 - pos % 4, n - done)
                                            te.matmul(
                                                ps_sc[:, pos * 128:(pos + take) * 128],
                                                lt,
                                                rh[hofs:hofs + 64,
                                                   (kt0 + done) * 128:(kt0 + done + take) * 128],
                                                start=(pss == 0), stop=(pss == 2))
                                            done += take
                                S.sig(te, f"{Q}qk")
                            S.on("T", tqk)
                            qktok = S.tok(f"{Q}qk")
                            def vmx(ve, mtiles=mtiles, spans=spans, qktok=qktok,
                                    expprev=S.tok(f"{Q}exp")):
                                S.wait(ve, qktok)
                                S.wait(ve, expprev)
                                for slot, mi in mtiles.items():
                                    ve.tensor_tensor(ps_sc[:, slot * 128:(slot + 1) * 128],
                                                     ps_sc[:, slot * 128:(slot + 1) * 128],
                                                     c_masks[:, mi * 128:(mi + 1) * 128],
                                                     ALU.add)
                                S.fence(ve)
                                for si, (s0, n) in enumerate(spans):
                                    ve.tensor_reduce(biasv[:, 2 * si:2 * si + 1],
                                                     ps_sc[:, s0 * 128:(s0 + n) * 128],
                                                     AX.X, ALU.max)
                                    S.fence(ve)
                                if len(spans) > 1:
                                    ve.tensor_tensor(biasv[:, 0:1], biasv[:, 0:1],
                                                     biasv[:, 2:3], ALU.max)
                                ve.tensor_scalar(biasv[:, 1:2], biasv[:, 0:1], -0.125, 0.0,
                                                 ALU.mult, ALU.add)
                                S.sig(ve, f"{Q}bv")
                            S.on("V", vmx)
                            bvtok = S.tok(f"{Q}bv")
                            def sexp(se, spans=spans, qi_g=qi_g, bvtok=bvtok,
                                     ttprev=S.tok(f"{Q}ptt")):
                                S.wait(se, bvtok)
                                S.wait(se, ttprev)
                                for si, (s0, n) in enumerate(spans):
                                    cs = slice(s0 * 128, (s0 + n) * 128)
                                    se.activation(p32[:, cs], ps_sc[:, cs],
                                                  AF.Exp, bias=biasv[:, 1:2], scale=0.125,
                                                  accum_out=dall[:, 16 * si + qi_g:
                                                                 16 * si + qi_g + 1])
                                S.sig(se, f"{Q}exp")
                            S.on("S", sexp)
                            exptok = S.tok(f"{Q}exp")
                            groups = [(g0, min(4, s0 + n - g0))
                                      for (s0, n) in spans
                                      for g0 in range(s0, s0 + n, 4)]
                            for (g0, nt_) in groups:
                                def tpp(te, g0=g0, nt_=nt_, exptok=exptok,
                                        tf=S.tok(f"{Q}tpf")):
                                    S.wait(te, exptok)
                                    S.wait(te, tf)
                                    for i in range(nt_):
                                        slot = g0 + i
                                        te.transpose(ps_tp[:, i * 128:(i + 1) * 128],
                                                     p32[:, slot * 128:(slot + 1) * 128],
                                                     c_iden[:])
                                    S.sig(te, f"{Q}ptt")
                                def vpp(ve, g0=g0, nt_=nt_, tt=S.tok(f"{Q}ptt")):
                                    S.wait(ve, (f"{Q}ptt", tt[1] + 1))
                                    ve.tensor_copy(pT[:, g0 * 128:(g0 + nt_) * 128],
                                                   ps_tp[:, 0:nt_ * 128])
                                    S.sig(ve, f"{Q}tpf")
                                S.on("T", tpp)
                                S.on("V", vpp)
                            pttok = S.tok(f"{Q}tpf")
                            ntot = sum(n for (_, n) in spans)
                            def tav(te, hofs=hofs, runs=runs, ntot=ntot, pttok=pttok,
                                    avev=S.tok(f"{Q}avev")):
                                S.wait(te, pttok)
                                S.wait(te, avev)
                                j = 0
                                for (slot0, kt0, n) in runs:
                                    for i in range(n):
                                        slot, kt = slot0 + i, kt0 + i
                                        te.matmul(ps_av[0:64, 0:128],
                                                  vT[:, kt * 128 + hofs:kt * 128 + hofs + 64],
                                                  pT[:, slot * 128:(slot + 1) * 128],
                                                  start=(j == 0), stop=(j == ntot - 1))
                                        j += 1
                                S.sig(te, f"{Q}avmm")
                            S.on("T", tav)
                            avtok = S.tok(f"{Q}avmm")
                            def vav(ve, qi_g=qi_g, avtok=avtok, avwr=avwr):
                                S.wait(ve, avtok)
                                if qi_g == 0:
                                    S.wait(ve, avwr)
                                ve.tensor_copy(av2[0:64, qsl], ps_av[0:64, 0:128])
                                S.sig(ve, f"{Q}avev")
                            S.on("V", vav)
                        exptok = S.tok(f"{Q}exp")
                        if not L1:
                            def vcmb(ve, exptok=exptok):
                                S.wait(ve, exptok)
                                ve.tensor_tensor(dall[:, 5:16], dall[:, 5:16],
                                                 dall[:, 21:32], ALU.add)
                                S.sig(ve, f"{Q}dcm")
                            S.on("V", vcmb)
                        cmbtok = S.tok(f"{Q}dcm") if not L1 else None
                        def tdt(te, exptok=exptok, cmbtok=cmbtok, tf=S.tok(f"{Q}tpf")):
                            S.wait(te, exptok)
                            S.wait(te, cmbtok)
                            S.wait(te, tf)
                            te.transpose(ps_tp[0:16, 0:128], dall[:, 0:16], c_iden[:])
                            S.sig(te, f"{Q}dtt")
                        def vdt(ve, dt=S.tok(f"{Q}dtt")):
                            S.wait(ve, (f"{Q}dtt", dt[1] + 1))
                            ve.tensor_copy(st32[0:16, 0:128], ps_tp[0:16, 0:128])
                            ftok = S.sig(ve, f"{Q}tpf")
                            S.wait(ve, ftok)
                            ve.reciprocal(st32[0:16, 128:256], st32[0:16, 0:128])
                            S.sig(ve, f"{Q}div")
                        def ydt(yq, h=h, dv=S.tok(f"{Q}div")):
                            S.wait(yq, (f"{Q}div", dv[1] + 1))
                            dst = bass.AP(rows.tensor, (4 + h) * NT, [[128, 16], [1, 128]])
                            S.inc(yq.dma_start(out=dst, in_=st32[0:16, 128:256]),
                                  f"d{Q}iv", 16)
                        S.on("T", tdt)
                        S.on("V", vdt)
                        S.on("Y", ydt)
                        ivtok = S.tok(f"d{Q}iv")
                        def ybc(yq, h=h, ivtok=ivtok, avn=S.tok(f"{Q}avn")):
                            S.wait(yq, ivtok)
                            if avn[1] > 0:
                                S.wait(yq, avn)
                            src = bass.AP(rows.tensor, (4 + h) * NT, [[0, 64], [1, 2048]])
                            S.inc(yq.dma_start(out=ivd[0:64, :], in_=src),
                                  f"d{Q}ivb", 16)
                        def vnr(ve, bt_=S.tok(f"d{Q}ivb"), ave=S.tok(f"{Q}avev")):
                            S.wait(ve, (f"d{Q}ivb", bt_[1] + 16))
                            S.wait(ve, ave)
                            ve.tensor_tensor(av2[0:64, :], av2[0:64, :], ivd[0:64, :],
                                             ALU.mult)
                            S.sig(ve, f"{Q}avn")
                        def yst(yq, hg=hg, h=h, avn=S.tok(f"{Q}avn")):
                            S.wait(yq, (f"{Q}avn", avn[1] + 1))
                            S.inc(yq.dma_start(out=avh[hg, 64 * h:64 * h + 64, :],
                                               in_=av2[0:64, :]), f"d{Q}avh", 16)
                        S.on("Y", ybc)
                        S.on("V", vnr)
                        S.on("Y", yst)
                subA.close()
                avhtok = S.tok(f"d{Q}avh")
                subB = ExitStack()
                wo32 = subB.enter_context(nc.sbuf_tensor(f"a{P}{s_}_wo", [128, 1024], F32))
                woh = subB.enter_context(nc.sbuf_tensor(f"a{P}{s_}_woh", [128, 1024], BF16))
                wol = subB.enter_context(nc.sbuf_tensor(f"a{P}{s_}_wol", [128, 1024], BF16))
                avh_t = subB.enter_context(nc.sbuf_tensor(f"a{P}{s_}_avht", [128, 8 * 512], BF16))
                avl_t = subB.enter_context(nc.sbuf_tensor(f"a{P}{s_}_avlt", [128, 8 * 512], BF16))
                avt = subB.enter_context(nc.sbuf_tensor(f"a{P}{s_}_avt", [128, 2 * 512], F32))
                xr = subB.enter_context(nc.sbuf_tensor(f"a{P}{s_}_xr", [128, 2 * 512], F32))
                yo = subB.enter_context(nc.sbuf_tensor(f"a{P}{s_}_yo", [128, 2 * 512], F32))
                for tt in range(4):
                    for hg in range(8):
                        sl = hg % 2
                        def ya(yq, hg=hg, tt=tt, sl=sl, avhtok=avhtok,
                               af=S.tok(f"{Q}avsp")):
                            S.wait(yq, avhtok)
                            if af[1] >= 2:
                                S.wait(yq, (af[0], af[1] - 1))
                            S.inc(yq.dma_start(out=avt[:, sl * 512:(sl + 1) * 512],
                                               in_=avh[hg, :, tt * 512:(tt + 1) * 512]),
                                  f"d{Q}av{sl}", 16)
                        def vsp2(ve, hg=hg, sl=sl, omm=S.tok(f"{Q}omm")):
                            S.wait(ve, S.tok(f"d{Q}av{sl}"))
                            S.wait(ve, omm)
                            hs = slice(hg * 512, (hg + 1) * 512)
                            ve.tensor_copy(avh_t[:, hs], avt[:, sl * 512:(sl + 1) * 512])
                            ve.tensor_tensor(avl_t[:, hs], avt[:, sl * 512:(sl + 1) * 512],
                                             avh_t[:, hs], ALU.subtract)
                            S.sig(ve, f"{Q}avsp")
                        S.on("Y", ya)
                        S.on("V", vsp2)
                    asptok = S.tok(f"{Q}avsp")
                    for oc in range(NCH):
                        pjb = pj[oc % 2]
                        def ywo(yq, oc=oc, ow=S.tok(f"{Q}omm")):
                            S.wait(yq, ow)
                            src = bass.AP(wsrc["o"].tensor, oc * 128,
                                          [[1024, 128], [128 * 1024, 8], [1, 128]])
                            S.inc(yq.dma_start(
                                out=wo32[:].rearrange("p (k f) -> p k f", k=8), in_=src),
                                f"d{Q}wo", 16)
                        def vwo(ve):
                            S.wait(ve, S.tok(f"d{Q}wo"))
                            ve.tensor_copy(woh[:], wo32[:])
                            ve.tensor_tensor(wol[:], wo32[:], woh[:], ALU.subtract)
                            S.sig(ve, f"{Q}wosp")
                        S.on("Y", ywo)
                        S.on("V", vwo)
                        wotok = S.tok(f"{Q}wosp")
                        def t2(te, oc=oc, pjb=pjb, asptok=asptok, wotok=wotok,
                               of=S.tok(f"{Q}oev")):
                            S.wait(te, asptok)
                            S.wait(te, wotok)
                            if of[1] >= 2:
                                S.wait(te, (of[0], of[1] - 1))
                            for hg in range(NCH):
                                hs = slice(hg * 512, (hg + 1) * 512)
                                wk = slice(hg * 128, (hg + 1) * 128)
                                te.matmul(pjb[:, 0:512], woh[:, wk], avh_t[:, hs],
                                          start=(hg == 0), stop=False)
                                te.matmul(pjb[:, 0:512], woh[:, wk], avl_t[:, hs],
                                          start=False, stop=False)
                                te.matmul(pjb[:, 0:512], wol[:, wk], avh_t[:, hs],
                                          start=False, stop=(hg == NCH - 1))
                            S.sig(te, f"{Q}omm")
                        S.on("T", t2)
                        otok = S.tok(f"{Q}omm")
                        def so(se, oc=oc, pjb=pjb, otok=otok, syo=S.tok(f"d{Q}yo")):
                            S.wait(se, otok)
                            if syo[1] >= 32:
                                S.wait(se, (syo[0], syo[1] - 16))
                            dst = yo[:, (oc % 2) * 512:(oc % 2) * 512 + 512]
                            if L1:
                                se.activation(dst, pjb[:, 0:512], AF.Identity,
                                              bias=c_b["o"][:, oc:oc + 1])
                            else:
                                se.activation(dst, pjb[:, 0:512], AF.Identity)
                            S.sig(se, f"{Q}oev")
                        def yx(yq, oc=oc, tt=tt, t0=t0, xf=S.tok(f"{Q}yadd")):
                            if xf[1] >= 2:
                                S.wait(yq, (xf[0], xf[1] - 1))
                            S.inc(yq.dma_start(
                                out=xr[:, (oc % 2) * 512:(oc % 2) * 512 + 512],
                                in_=xsrc[oc, :, t0 + tt * 512:t0 + (tt + 1) * 512]),
                                f"d{Q}xr", 16)
                        def vy(ve, oc=oc, oevt=S.tok(f"{Q}oev"), syo=S.tok(f"d{Q}yo")):
                            S.wait(ve, (f"{Q}oev", oevt[1] + 1))
                            S.wait(ve, S.tok(f"d{Q}xr"))
                            if syo[1] >= 32:
                                S.wait(ve, (syo[0], syo[1] - 16))
                            sl2 = (oc % 2) * 512
                            ve.tensor_tensor(yo[:, sl2:sl2 + 512], yo[:, sl2:sl2 + 512],
                                             xr[:, sl2:sl2 + 512], ALU.add)
                            S.sig(ve, f"{Q}yadd")
                        def ys(yq, oc=oc, tt=tt, t0=t0, ya_=S.tok(f"{Q}yadd")):
                            S.wait(yq, (f"{Q}yadd", ya_[1] + 1))
                            sl2 = (oc % 2) * 512
                            S.inc(yq.dma_start(
                                out=xdst[oc, :, t0 + tt * 512:t0 + (tt + 1) * 512],
                                in_=yo[:, sl2:sl2 + 512]), f"d{Q}yo", 16)
                        S.on("S", so)
                        S.on("Y", yx)
                        S.on("V", vy)
                        S.on("Y", ys)
                subB.close()
                def yend(yq, Q=Q):
                    S.wait(yq, S.tok(f"d{Q}yo"))
                S.on("Y", yend)

        def emit_ffn1(xsrc, xdst, P, ph):
            stsub = ExitStack()
            psg = stsub.enter_context(nc.psum_tensor(f"g{P}_psg", [128, 512], F32))
            psh = stsub.enter_context(nc.psum_tensor(f"g{P}_psh", [128, 512], F32))
            _, rtok = stats_pass(xsrc, 0, NT, f"{P}st", ph, psg, psh)
            stsub.close()
            pj = [ph.enter_context(nc.psum_tensor(f"g{P}_pj{i}", [128, 512], F32))
                  for i in range(2)]
            sw = ph.enter_context(nc.sbuf_tensor(f"g{P}_sw", [128, NCH * 1024], BF16))
            swr = ph.enter_context(nc.sbuf_tensor(f"g{P}_swr", [128, 2 * 1024], F32))
            mb = ph.enter_context(nc.sbuf_tensor(f"g{P}_mb", [128, 2048], F32))
            rb = ph.enter_context(nc.sbuf_tensor(f"g{P}_rb", [128, 2048], F32))
            lnh = ph.enter_context(nc.sbuf_tensor(f"g{P}_lnh", [128, NCH * 2048], BF16))
            lnl = ph.enter_context(nc.sbuf_tensor(f"g{P}_lnl", [128, NCH * 2048], BF16))
            xt = ph.enter_context(nc.sbuf_tensor(f"g{P}_xt", [128, 2 * 2048], F32))
            st32 = ph.enter_context(nc.sbuf_tensor(f"g{P}_st", [128, 2048], F32))
            xr = ph.enter_context(nc.sbuf_tensor(f"g{P}_xr", [128, 2 * 512], F32))
            yo = ph.enter_context(nc.sbuf_tensor(f"g{P}_yo", [128, 2 * 512], F32))
            for kc in range(NCH):
                sl = kc % 2
                def yw(yq, kc=kc, sl=sl, sg=S.tok(f"{P}sg")):
                    if sg[1] >= 2:
                        S.wait(yq, (sg[0], sg[1] - 1))
                    S.inc(yq.dma_start(out=swr[:, sl * 1024:(sl + 1) * 1024],
                                       in_=wfT[kc * 128:(kc + 1) * 128, :]), f"d{P}w{sl}", 16)
                def ssg(se, kc=kc, sl=sl):
                    S.wait(se, S.tok(f"d{P}w{sl}"))
                    se.activation(sw[:, kc * 1024:(kc + 1) * 1024],
                                  swr[:, sl * 1024:(sl + 1) * 1024], AF.Sign)
                    S.sig(se, f"{P}sg")
                S.on("Y", yw)
                S.on("S", ssg)
            swtok = S.tok(f"{P}sg")
            for half in range(2):
                t0 = half * 2048
                Q = f"{P}h{half}"
                def yb(yq, t0=t0, lc=S.tok(f"{P}lnc")):
                    S.wait(yq, rtok)
                    if lc[1] > 0:
                        S.wait(yq, lc)
                    bcast_row(yq, mb[:, :], 0, t0, 2048, f"d{Q}b")
                    bcast_row(yq, rb[:, :], 1, t0, 2048, f"d{Q}b")
                S.on("Y", yb)
                btok = S.tok(f"d{Q}b")
                xslot = [None, None]
                for c in range(NCH):
                    sl = c % 2
                    ss = slice(sl * 2048, sl * 2048 + 2048)
                    def y2(yq, c=c, t0=t0, ss=ss, xs=xslot[sl]):
                        S.wait(yq, xs)
                        S.inc(yq.dma_start(out=xt[:, ss], in_=xsrc[c, :, t0:t0 + 2048]),
                              f"d{Q}x{sl}", 16)
                    def v(ve, c=c, sl=sl, ss=ss, btok=btok, mmprev=S.tok(f"{P}fmm")):
                        S.wait(ve, btok)
                        S.wait(ve, S.tok(f"d{Q}x{sl}"))
                        S.wait(ve, mmprev)
                        ve.tensor_tensor(st32[:], xt[:, ss], mb[:], ALU.subtract)
                        S.sig(ve, f"{Q}xf")
                        ve.tensor_tensor(st32[:], st32[:], rb[:], ALU.mult)
                        cs = slice(c * 2048, (c + 1) * 2048)
                        ve.tensor_copy(lnh[:, cs], st32[:])
                        ve.tensor_tensor(lnl[:, cs], st32[:], lnh[:, cs], ALU.subtract)
                        S.sig(ve, f"{P}lnc")
                    S.on("Y", y2)
                    S.on("V", v)
                    xslot[sl] = S.tok(f"{Q}xf")
                lntok = S.tok(f"{P}lnc")
                for oc in range(NCH):
                    for tq in range(4):
                        pjb = pj[tq % 2]
                        def t(te, oc=oc, tq=tq, pjb=pjb, lntok=lntok, swtok=swtok,
                              of=S.tok(f"{Q}fev")):
                            S.wait(te, lntok)
                            S.wait(te, swtok)
                            if of[1] >= 2:
                                S.wait(te, (of[0], of[1] - 1))
                            for kc in range(NCH):
                                rs = slice(kc * 2048 + tq * 512, kc * 2048 + tq * 512 + 512)
                                wk = slice(kc * 1024 + oc * 128, kc * 1024 + (oc + 1) * 128)
                                te.matmul(pjb[:, 0:512], sw[:, wk], lnh[:, rs],
                                          start=(kc == 0), stop=False)
                                te.matmul(pjb[:, 0:512], sw[:, wk], lnl[:, rs],
                                          start=False, stop=(kc == NCH - 1))
                            S.sig(te, f"{P}fmm")
                        S.on("T", t)
                        mmtok = S.tok(f"{P}fmm")
                        def yx(yq, oc=oc, tq=tq, t0=t0, xf=S.tok(f"{Q}yadd")):
                            if xf[1] >= 2:
                                S.wait(yq, (xf[0], xf[1] - 1))
                            S.inc(yq.dma_start(
                                out=xr[:, (tq % 2) * 512:(tq % 2) * 512 + 512],
                                in_=xsrc[oc, :, t0 + tq * 512:t0 + (tq + 1) * 512]),
                                f"d{Q}xr", 16)
                        def vy(ve, oc=oc, tq=tq, pjb=pjb, mmtok=mmtok,
                               yst_=S.tok(f"d{Q}yo")):
                            S.wait(ve, mmtok)
                            S.wait(ve, S.tok(f"d{Q}xr"))
                            if yst_[1] >= 32:
                                S.wait(ve, (yst_[0], yst_[1] - 16))
                            sl2 = (tq % 2) * 512
                            ve.tensor_scalar(yo[:, sl2:sl2 + 512], pjb[:, 0:512], 0.5, 0.0,
                                             ALU.mult, ALU.add)
                            S.sig(ve, f"{Q}fev")
                            ve.tensor_tensor(yo[:, sl2:sl2 + 512], yo[:, sl2:sl2 + 512],
                                             xr[:, sl2:sl2 + 512], ALU.add)
                            S.sig(ve, f"{Q}yadd")
                        def ys(yq, oc=oc, tq=tq, t0=t0, ya_=S.tok(f"{Q}yadd")):
                            S.wait(yq, (f"{Q}yadd", ya_[1] + 1))
                            sl2 = (tq % 2) * 512
                            S.inc(yq.dma_start(
                                out=xdst[oc, :, t0 + tq * 512:t0 + (tq + 1) * 512],
                                in_=yo[:, sl2:sl2 + 512]), f"d{Q}yo", 16)
                        S.on("Y", yx)
                        S.on("V", vy)
                        S.on("Y", ys)
                def yend(yq, Q=Q):
                    S.wait(yq, S.tok(f"d{Q}yo"))
                S.on("Y", yend)

        def run_phase(fn, *args):
            with ExitStack() as ph:
                fn(*args, ph)
            barrier()
            S.end_phase()

        run_phase(emit_dequant)
        seqp = [(emit_ffn0, (xin, 0, "A")), (emit_attn, (None, 0, "M")),
                (emit_ffn0, (None, 4, "B")), (emit_ffn1, (None, "C")),
                (emit_attn, (None, 1, "N")), (emit_ffn1, (None, "E"))]
        srcs = [xin, xres[1], xres[2], xres[3], xres[4], xres[5]]
        nph = min(upto, 6)
        for i in range(nph):
            fn, args = seqp[i]
            src = srcs[i]
            dst = xout if i == nph - 1 else srcs[i + 1]
            if fn is emit_ffn0:
                run_phase(fn, src, dst, args[1], args[2])
            elif fn is emit_attn:
                run_phase(fn, src, dst, args[1], args[2])
            else:
                run_phase(fn, src, dst, args[1])

        if dump_outs:
            barrier()
            def ydump(yq):
                for nm, dst in dump_outs.items():
                    S.inc(yq.dma_start(out=dst, in_=named[nm][0]), "ddump", 16)
                S.wait(yq, S.tok("ddump"))
            S.on("Y", ydump)
        barrier()

    print("instr-sections per engine:", S.ni)
    return nc



def _mask_tiles():
    r = np.arange(128)[:, None]
    c = np.arange(128)[None, :]
    vis_d = c <= r
    vis_e = c >= r + 1
    vis_es = vis_e | (c < 4)
    vis_s = np.broadcast_to(c < 4, (128, 128))
    m = np.zeros((4, 128, 128), np.float32)
    for i, vis in enumerate((vis_d, vis_e, vis_es, vis_s)):
        m[i] = np.where(vis, 0.0, NEGINF).astype(np.float32)
    return m


def _prep_inputs(inputs):
    common = {
        "w_up": np.ascontiguousarray(np.asarray(inputs["L0_ffn_w_up"], np.float32)),
        "w_gate": np.ascontiguousarray(np.asarray(inputs["L0_ffn_w_gate"], np.float32)),
        "w_down": np.ascontiguousarray(np.asarray(inputs["L0_ffn_w_down"], np.float32)),
        "l0_wqT": np.ascontiguousarray(np.asarray(inputs["L0_attn_Wq"], np.float32).T),
        "l0_wkT": np.ascontiguousarray(np.asarray(inputs["L0_attn_Wk"], np.float32).T),
        "l0_wvT": np.ascontiguousarray(np.asarray(inputs["L0_attn_Wv"], np.float32).T),
        "l0_woT": np.ascontiguousarray(np.asarray(inputs["L0_attn_Wo"], np.float32).T),
        "l1_wqT": np.ascontiguousarray(
            np.transpose(np.asarray(inputs["L1_attn_W_Q"], np.float32), (1, 0, 2)).reshape(D, D)),
        "l1_wkT": np.ascontiguousarray(
            np.transpose(np.asarray(inputs["L1_attn_W_K"], np.float32), (1, 0, 2)).reshape(D, D)),
        "l1_wvT": np.ascontiguousarray(
            np.transpose(np.asarray(inputs["L1_attn_W_V"], np.float32), (1, 0, 2)).reshape(D, D)),
        "l1_woT": np.ascontiguousarray(np.asarray(inputs["L1_attn_W_O"], np.float32)),
        "wfT": np.ascontiguousarray(np.asarray(inputs["L1_ffn_weight"], np.float32).T),
        "iden32": np.eye(128, dtype=np.float32),
        "idenb": np.eye(128, dtype=ml_dtypes.bfloat16),
        "ones32": np.ones((128, 1), np.float32),
        "masks": _mask_tiles(),
    }
    for p, key in (("q", "L1_attn_b_Q"), ("k", "L1_attn_b_K"), ("v", "L1_attn_b_V")):
        b = np.asarray(inputs[key], np.float32).reshape(16, 64)
        bb = np.zeros((128, 8), np.float32)
        for hg in range(8):
            bb[0:64, hg] = b[2 * hg]
            bb[64:128, hg] = b[2 * hg + 1]
        common[f"l1_b{p}"] = bb
    bo = np.asarray(inputs["L1_attn_b_O"], np.float32).reshape(8, 128)
    common["l1_bo"] = np.ascontiguousarray(bo.T)
    x = np.asarray(inputs["L0_attn_x"], np.float32)
    in_maps = []
    for core in range(8):
        xc = x[2 * core:2 * core + 2]
        xt = np.transpose(xc, (2, 0, 1)).reshape(D, NT)
        m = dict(common)
        m["xT"] = np.ascontiguousarray(xt.reshape(NCH, 128, NT))
        in_maps.append(m)
    return in_maps


def kernel(**inputs):
    in_maps = _prep_inputs(inputs)
    nc = build(upto=7)
    res = run_bass_kernel_spmd(nc, in_maps, core_ids=list(range(8)))
    outs = []
    for core in range(8):
        o = np.asarray(res.results[core]["outT"]).reshape(D, 2, SEQ)
        outs.append(np.transpose(o, (1, 2, 0)))
    return np.ascontiguousarray(np.concatenate(outs, axis=0)).astype(np.float32)


# version 02
# speedup vs baseline: 1.0273x; 1.0273x over previous
import numpy as np
import ml_dtypes
import concourse.bass as bass
import concourse.mybir as mybir
from concourse.bass_utils import run_bass_kernel_spmd
from contextlib import ExitStack

F32 = mybir.dt.float32
BF16 = mybir.dt.bfloat16
I16 = mybir.dt.int16
AF = mybir.ActivationFunctionType
ALU = mybir.AluOpType
AX = mybir.AxisListType

MAGIC = 12582912.0
NEGINF = -1.0e30
D = 1024
NCH = 8
NT = 4096
SEQ = 2048
HID = 3072
NHC = 24
QMAXA = 32767.0
EPS_LN = 1e-5
EPS_Q = 1e-12

ENG_ATTR = {"T": "tensor", "V": "vector", "S": "scalar", "G": "gpsimd", "Y": "sync"}


class Eng:

    def __init__(self, raw, w):
        self._raw = raw
        self._w = w
        self.last = None

    def __getattr__(self, name):
        v = getattr(self._raw, name)
        if not callable(v):
            return v
        def call(*a, **kw):
            r = v(*a, **kw)
            if hasattr(r, "then_inc"):
                self.last = r
            return r
        return call


class Sched:

    def __init__(self, nc, block, es):
        self.nc, self.block, self.es = nc, block, es
        self.cnt = {}
        self.hnd = {}
        self.real_cnt = {}
        self.free_pool = []
        self.free_pool_dma = []
        self.free_pool_dg = []
        self.persist = set()
        self.phase_names = []
        self.waited = {}
        self.ni = {k: 0 for k in ENG_ATTR}

    def sem(self, name):
        if name not in self.hnd:
            if name.startswith("dG"):
                pool = self.free_pool_dg
            elif name.startswith("d"):
                pool = self.free_pool_dma
            else:
                pool = self.free_pool
            if pool:
                h = pool.pop()
            else:
                h = self.es.enter_context(self.nc.semaphore(f"sm{len(self.real_cnt)}"))
                self.real_cnt[id(h)] = 0
            self.hnd[name] = h
            self.cnt[name] = self.real_cnt[id(h)]
            if self.phase_names is not None:
                self.phase_names.append(name)
        return self.hnd[name]

    def persist_name(self, name):
        self.sem(name)
        self.persist.add(name)

    def end_phase(self):
        for name in self.phase_names:
            if name in self.persist or name not in self.hnd:
                continue
            h = self.hnd.pop(name)
            self.real_cnt[id(h)] = self.cnt.pop(name)
            if name.startswith("dG"):
                self.free_pool_dg.append(h)
            elif name.startswith("d"):
                self.free_pool_dma.append(h)
            else:
                self.free_pool.append(h)
            self.waited = {k: v for k, v in self.waited.items() if k[1] != name}
        self.phase_names = []

    def inc(self, instr, name, k=1):
        h = self.sem(name)
        instr.then_inc(h, k)
        self.cnt[name] += k
        self.real_cnt[id(h)] = self.cnt[name]
        return (name, self.cnt[name])

    def sig(self, e, name, k=1):
        assert e.last is not None, f"sig({name}) with no prior instruction"
        r = self.inc(e.last, name, k)
        e.last = None
        return r

    def fence(self, e):
        name = "fn" + e._w
        assert e.last is not None
        tok = self.inc(e.last, name)
        e.last = None
        e._raw.wait_ge(self.hnd[name], tok[1])
        self.waited[(e._w, name)] = tok[1]
        return tok

    def bump(self, e, name, k=1):
        h = self.sem(name)
        e._raw.sem_inc(h, k)
        self.cnt[name] += k
        self.real_cnt[id(h)] = self.cnt[name]
        return (name, self.cnt[name])

    def tok(self, name):
        self.sem(name)
        return (name, self.cnt[name])

    def wait(self, e, tok):
        if tok is None:
            return
        name, val = tok
        if val <= 0:
            return
        key = (e._w, name)
        if self.waited.get(key, -1) >= val:
            return
        e._raw.wait_ge(self.hnd[name], val)
        self.waited[key] = val

    def on(self, w, f):
        self.ni[w] += 1
        getattr(self.block, ENG_ATTR[w])(lambda raw: f(Eng(raw, w)))


def build(upto=7, dumps=(), stage=99):
    nc = bass.Bass("TRN2", target_bir_lowering=False, detect_race_conditions=False)

    xin = nc.dram_tensor("xT", [NCH, 128, NT], F32, kind="ExternalInput").ap()
    w_up_raw = nc.dram_tensor("w_up", [HID, D], F32, kind="ExternalInput").ap()
    w_gate_raw = nc.dram_tensor("w_gate", [HID, D], F32, kind="ExternalInput").ap()
    w_down_raw = nc.dram_tensor("w_down", [D, HID], F32, kind="ExternalInput").ap()
    l0_w = {p: nc.dram_tensor(f"l0_w{p}T", [D, D], F32, kind="ExternalInput").ap()
            for p in ("q", "k", "v", "o")}
    l1_w = {p: nc.dram_tensor(f"l1_w{p}T", [D, D], F32, kind="ExternalInput").ap()
            for p in ("q", "k", "v", "o")}
    l1_b = {p: nc.dram_tensor(f"l1_b{p}", [128, 8], F32, kind="ExternalInput").ap()
            for p in ("q", "k", "v", "o")}
    wfT = nc.dram_tensor("wfT", [D, D], F32, kind="ExternalInput").ap()
    iden32 = nc.dram_tensor("iden32", [128, 128], F32, kind="ExternalInput").ap()
    idenb = nc.dram_tensor("idenb", [128, 128], BF16, kind="ExternalInput").ap()
    ones32 = nc.dram_tensor("ones32", [128, 1], F32, kind="ExternalInput").ap()
    masks = nc.dram_tensor("masks", [4, 128, 128], F32, kind="ExternalInput").ap()

    xout = nc.dram_tensor("outT", [NCH, 128, NT], F32, kind="ExternalOutput").ap()

    xres = {i: nc.dram_tensor(f"x{i}", [NCH, 128, NT], F32).ap() for i in range(1, 6)}
    pk_ug = nc.dram_tensor("pk_ug", [2, 2, NHC, 128, 8 * 128], BF16).ap()
    pk_d = nc.dram_tensor("pk_d", [2, NCH, 128, 24 * 128], BF16).ap()
    upraw = nc.dram_tensor("upraw", [NHC, 128, NT], F32).ap()
    gateraw = nc.dram_tensor("gateraw", [NHC, 128, NT], F32).ap()
    m2i = nc.dram_tensor("m2i", [NHC, 128, NT], I16).ap()
    prodb = nc.dram_tensor("prodb", [NHC, 128, NT], F32).ap()
    mmb = nc.dram_tensor("mmb", [NHC, 128, NT], F32).ap()
    rows = nc.dram_tensor("rows", [8, NT], F32).ap()
    dbg_sc = nc.dram_tensor("dbg_sc", [128, 16], F32).ap()
    dbg_mx = nc.dram_tensor("dbg_mx", [128, 384], F32).ap()
    dbg2 = nc.dram_tensor("dbg2", [128, 8], F32).ap()
    avh = nc.dram_tensor("avh", [8, 128, SEQ], F32).ap()
    cc_in = [nc.dram_tensor(f"cc_in{i}", [1, 2], F32) for i in range(8)]
    cc_out = [nc.dram_tensor(f"cc_out{i}", [1, 2], F32) for i in range(8)]

    named = {"x1": (xres[1], [NCH, 128, NT], F32), "x2": (xres[2], [NCH, 128, NT], F32),
             "x3": (xres[3], [NCH, 128, NT], F32), "x4": (xres[4], [NCH, 128, NT], F32),
             "x5": (xres[5], [NCH, 128, NT], F32),
             "pk_ug": (pk_ug, [2, 2, NHC, 128, 8 * 128], BF16),
             "pk_d": (pk_d, [2, NCH, 128, 24 * 128], BF16),
             "upraw": (upraw, [NHC, 128, NT], F32),
             "gateraw": (gateraw, [NHC, 128, NT], F32),
             "prodb": (prodb, [NHC, 128, NT], F32),
             "mmb": (mmb, [NHC, 128, NT], F32),
             "rows": (rows, [8, NT], F32),
             "dbg_sc": (dbg_sc, [128, 16], F32),
             "dbg_mx": (dbg_mx, [128, 384], F32),
             "dbg2": (dbg2, [128, 8], F32),
             "m2i": (m2i, [NHC, 128, NT], I16),
             "avh": (avh, [8, 128, SEQ], F32)}
    dump_outs = {nm: nc.dram_tensor("dump_" + nm, named[nm][1], named[nm][2],
                                    kind="ExternalOutput").ap() for nm in dumps}

    with ExitStack() as es:
        block = es.enter_context(nc.Block())
        S = Sched(nc, block, es)
        for w in ENG_ATTR:
            S.persist_name("bar" + w)
            S.persist_name("fn" + w)
        S.persist_name("dconst")

        c_iden = es.enter_context(nc.sbuf_tensor("c_iden", [128, 128], F32))
        c_idenb = es.enter_context(nc.sbuf_tensor("c_idenb", [128, 128], BF16))
        c_ones = es.enter_context(nc.sbuf_tensor("c_ones", [128, 1], F32))
        c_masks = es.enter_context(nc.sbuf_tensor("c_masks", [128, 4 * 128], F32))
        c_b = {p: es.enter_context(nc.sbuf_tensor(f"c_b{p}", [128, 8], F32))
               for p in ("q", "k", "v", "o")}
        c_sc = es.enter_context(nc.sbuf_tensor("c_sc", [128, 16], F32))
        rb1 = es.enter_context(nc.sbuf_tensor("rb1", [1, 2 * 512], F32))
        rb2 = es.enter_context(nc.sbuf_tensor("rb2", [1, 2 * 512], F32))
        rb3 = es.enter_context(nc.sbuf_tensor("rb3", [1, 2 * 512], F32))
        r32 = es.enter_context(nc.sbuf_tensor("r32", [32, 4 * 128], F32))
        s11 = es.enter_context(nc.sbuf_tensor("s11", [1, 2], F32))

        def y0(yq):
            for dst, src in [(c_iden, iden32), (c_idenb, idenb), (c_ones, ones32)]:
                S.inc(yq.dma_start(out=dst[:], in_=src), "dconst", 16)
            msrc = bass.AP(masks.tensor, 0, [[128, 128], [128 * 128, 4], [1, 128]])
            S.inc(yq.dma_start(out=c_masks[:].rearrange("p (m f) -> p m f", m=4),
                               in_=msrc), "dconst", 16)
            for p in ("q", "k", "v", "o"):
                S.inc(yq.dma_start(out=c_b[p][:], in_=l1_b[p]), "dconst", 16)
        S.on("Y", y0)
        tok_consts = S.tok("dconst")

        def barrier():
            for w in ENG_ATTR:
                S.on(w, (lambda w: lambda e: S.bump(e, "bar" + w))(w))
            toks = [S.tok("bar" + w) for w in ENG_ATTR]
            for w in ENG_ATTR:
                def body(e, w=w):
                    for tk in toks:
                        S.wait(e, tk)
                S.on(w, body)

        def par_max(vec_ap, out11, in_tok, tag, psb):
            def t(te):
                S.wait(te, in_tok)
                te.transpose(psb[0:1, 0:128], vec_ap, c_iden[:])
                S.sig(te, "pmt" + tag)
            def v(ve):
                S.wait(ve, S.tok("pmt" + tag))
                ve.tensor_reduce(out11, psb[0:1, 0:128], AX.X, ALU.max)
                S.sig(ve, "pmv" + tag)
            S.on("T", t)
            S.on("V", v)
            return S.tok("pmv" + tag)

        def collective_max(rnd, srcs, dst_tile, ncols, in_tok, tag):
            def g(ge):
                S.wait(ge, in_tok)
                for i in range(2):
                    s_ = srcs[min(i, len(srcs) - 1)]
                    S.inc(ge.dma_start(out=cc_in[rnd][0:1, i:i + 1], in_=s_), "dG", 16)
                S.wait(ge, S.tok("dG"))
                S.inc(ge.collective_compute(
                    "AllReduce", ALU.max, replica_groups=[list(range(8))],
                    ins=[cc_in[rnd].ap().opt()], outs=[cc_out[rnd].ap().opt()]), "cc")
                S.wait(ge, S.tok("cc"))
                rep = bass.AP(cc_out[rnd].ap().tensor, 0, [[0, 128], [1, ncols]])
                S.inc(ge.dma_start(out=dst_tile, in_=rep), "dG", 16)
                S.wait(ge, S.tok("dG"))
                S.bump(ge, "ccb" + tag)
            S.on("G", g)
            return S.tok("ccb" + tag)

        def stats_pass(xsrc, t0, T, tag, ph, psg, psh):
            xa = ph.enter_context(nc.sbuf_tensor(f"st_x_{tag}", [128, 2 * 512], F32))
            sq = ph.enter_context(nc.sbuf_tensor(f"st_q_{tag}", [128, 2 * 512], F32))
            nblk = T // 512
            slot_tok = [None, None]
            for blk in range(nblk):
                tb = t0 + blk * 512
                bsl = blk % 2
                bss = slice(bsl * 512, bsl * 512 + 512)
                for c in range(NCH):
                    sl = (blk * NCH + c) % 2
                    ss = slice(sl * 512, sl * 512 + 512)
                    def y(yq, c=c, tb=tb, sl=sl, ss=ss, st_=slot_tok[sl]):
                        S.wait(yq, st_)
                        S.inc(yq.dma_start(out=xa[:, ss], in_=xsrc[c, :, tb:tb + 512]),
                              f"d{tag}x{sl}", 16)
                    def s_(se, sl=sl, ss=ss, st_=slot_tok[sl]):
                        S.wait(se, S.tok(f"d{tag}x{sl}"))
                        S.wait(se, st_)
                        se.activation(sq[:, ss], xa[:, ss], AF.Square)
                        S.sig(se, f"{tag}sq")
                    def t(te, c=c, ss=ss):
                        S.wait(te, S.tok(f"{tag}sq"))
                        if c == 0:
                            S.wait(te, S.tok(f"{tag}ev"))
                        te.matmul(psg[0:1, 0:512], c_ones[:], xa[:, ss],
                                  start=(c == 0), stop=(c == NCH - 1))
                        te.matmul(psh[0:1, 0:512], c_ones[:], sq[:, ss],
                                  start=(c == 0), stop=(c == NCH - 1))
                        S.sig(te, f"{tag}tc")
                    S.on("Y", y)
                    S.on("S", s_)
                    S.on("T", t)
                    slot_tok[sl] = S.tok(f"{tag}tc")
                def v(ve, bss=bss, bsl=bsl):
                    S.wait(ve, S.tok(f"{tag}tc"))
                    S.wait(ve, S.tok(f"d{tag}mr{bsl}"))
                    ve.tensor_scalar(rb1[0:1, bss], psg[0:1, 0:512], 1.0 / D, 0.0,
                                     ALU.mult, ALU.add)
                    ve.tensor_scalar(rb3[0:1, bss], psh[0:1, 0:512], 1.0 / D, 0.0,
                                     ALU.mult, ALU.add)
                    S.sig(ve, f"{tag}ev")
                    ve.tensor_tensor(rb2[0:1, bss], rb1[0:1, bss], rb1[0:1, bss],
                                     ALU.mult)
                    ve.tensor_tensor(rb3[0:1, bss], rb3[0:1, bss], rb2[0:1, bss],
                                     ALU.subtract)
                    ve.tensor_scalar(rb3[0:1, bss], rb3[0:1, bss], EPS_LN, 0.0,
                                     ALU.add, ALU.add)
                    S.sig(ve, f"{tag}vv")
                def s2_(se, bss=bss):
                    S.wait(se, S.tok(f"{tag}vv"))
                    se.activation(rb2[0:1, bss], rb3[0:1, bss], AF.Sqrt)
                    S.sig(se, f"{tag}sr")
                def v3(ve, bss=bss):
                    S.wait(ve, S.tok(f"{tag}sr"))
                    ve.reciprocal(rb3[0:1, bss], rb2[0:1, bss])
                    S.fence(ve)
                    ve.tensor_copy(rb2[0:1, bss.start:bss.start + 1],
                                   rb3[0:1, bss.start:bss.start + 1])
                    S.sig(ve, f"{tag}rr")
                def y2(yq, tb=tb, bss=bss, bsl=bsl):
                    S.wait(yq, S.tok(f"{tag}rr"))
                    S.inc(yq.dma_start(out=rows[0:1, tb:tb + 512], in_=rb1[0:1, bss]),
                          f"d{tag}mr{bsl}", 16)
                    S.inc(yq.dma_start(out=rows[1:2, tb:tb + 512], in_=rb3[0:1, bss]),
                          f"d{tag}mr{bsl}", 16)
                S.on("V", v)
                S.on("S", s2_)
                S.on("V", v3)
                S.on("Y", y2)
            def yfin(yq):
                S.wait(yq, S.tok(f"d{tag}mr0"))
                S.wait(yq, S.tok(f"d{tag}mr1"))
                S.bump(yq, f"{tag}done")
            S.on("Y", yfin)
            rtok = S.tok(f"{tag}done")
            return rtok, rtok

        def bcast_row(yq, dst, row_idx, t0, T, semname):
            src = bass.AP(rows.tensor, row_idx * NT + t0, [[0, 128], [1, T]])
            return S.inc(yq.dma_start(out=dst, in_=src), semname, 16)

        def emit_dequant(ph):
            PSD = {nm: ph.enter_context(nc.psum_tensor(f"psd_{nm}", [128, 512], F32))
                   for nm in "ab"}
            swr = ph.enter_context(nc.sbuf_tensor("dq_wr", [128, HID], F32))
            sbm = ph.enter_context(nc.sbuf_tensor("dq_bm", [128, 96], F32))
            sinv = ph.enter_context(nc.sbuf_tensor("dq_inv", [128, 96], F32))
            st_ = ph.enter_context(nc.sbuf_tensor("dq_t", [128, HID], F32))
            shi = ph.enter_context(nc.sbuf_tensor("dq_hi", [128, 2 * 512], BF16))
            slo = ph.enter_context(nc.sbuf_tensor("dq_lo", [128, 2 * 512], BF16))
            jobs = [(w_up_raw, D, NHC, ("ug", 0)), (w_gate_raw, D, NHC, ("ug", 1)),
                    (w_down_raw, HID, NCH, ("d", 0))]
            tr_done = None
            v_done = None
            for raw, K, OC, dest in jobs:
                nb = K // 32
                nkc = K // 128
                for oc in range(OC):
                    def y(yq, raw=raw, oc=oc, K=K, v_done=v_done):
                        S.wait(yq, v_done)
                        S.inc(yq.dma_start(out=swr[:, 0:K],
                                           in_=raw[oc * 128:(oc + 1) * 128, :]), "dqx", 16)
                    def v(ve, K=K, nb=nb, tr_done=tr_done, wprev=v_done):
                        S.wait(ve, S.tok("dqx"))
                        S.wait(ve, tr_done)
                        S.wait(ve, wprev)
                        ve.tensor_reduce(sbm[:, 0:nb],
                                         swr[:, 0:K].rearrange("p (b k) -> p b k", k=32),
                                         AX.X, ALU.max, apply_absolute_value=True)
                        S.fence(ve)
                        ve.tensor_scalar(sbm[:, 0:nb], sbm[:, 0:nb], 1.0 / 7.0, EPS_Q,
                                         ALU.mult, ALU.max)
                        S.fence(ve)
                        ve.reciprocal(sinv[:, 0:nb], sbm[:, 0:nb])
                        S.fence(ve)
                        for b in range(nb):
                            ve.tensor_scalar(st_[:, b * 32:(b + 1) * 32],
                                             swr[:, b * 32:(b + 1) * 32],
                                             sinv[:, b:b + 1], MAGIC, ALU.mult, ALU.add)
                        S.fence(ve)
                        ve.tensor_scalar(st_[:, 0:K], st_[:, 0:K], MAGIC, 0.0,
                                         ALU.subtract, ALU.add)
                        for b in range(nb):
                            ve.tensor_scalar(st_[:, b * 32:(b + 1) * 32],
                                             st_[:, b * 32:(b + 1) * 32],
                                             sbm[:, b:b + 1], 0.0, ALU.mult, ALU.add)
                        S.fence(ve)
                        ve.tensor_copy(sbm[:, 0:1], sbm[:, 0:1])
                        S.sig(ve, "zqw")
                    S.on("Y", y)
                    S.on("V", v)
                    v_done = S.tok("zqw")
                    ng4 = (nkc + 3) // 4
                    for g4 in range(ng4):
                        ntile = min(4, nkc - g4 * 4)
                        pb = "ab"[g4 % 2]
                        off = 0 if pb == "a" else 512
                        def t(te, g4=g4, ntile=ntile, pb=pb, v_done=v_done):
                            S.wait(te, v_done)
                            S.wait(te, S.tok(f"zqpf{pb}"))
                            for i in range(ntile):
                                kc = g4 * 4 + i
                                te.transpose(PSD[pb][:, i * 128:(i + 1) * 128],
                                             st_[:, kc * 128:(kc + 1) * 128], c_iden[:])
                            S.sig(te, "zqt")
                        def sv(se, ntile=ntile, pb=pb, off=off, tcur=None):
                            S.wait(se, S.tok("zqt"))
                            S.wait(se, S.tok(f"zqhf{pb}"))
                            se.activation(shi[:, off:off + ntile * 128],
                                          PSD[pb][:, 0:ntile * 128], AF.Identity)
                            S.sig(se, f"zqh{pb}")
                        def vv(ve, ntile=ntile, pb=pb, off=off):
                            S.wait(ve, S.tok(f"zqh{pb}"))
                            ve.tensor_tensor(slo[:, off:off + ntile * 128],
                                             PSD[pb][:, 0:ntile * 128],
                                             shi[:, off:off + ntile * 128], ALU.subtract)
                            S.sig(ve, f"zql{pb}")
                        def g2(ge, g4=g4, ntile=ntile, pb=pb, off=off, dest=dest, oc=oc):
                            opb = "b" if pb == "a" else "a"
                            prev = S.tok(f"dGpk{opb}")
                            S.wait(ge, S.tok(f"zql{pb}"))
                            S.wait(ge, prev)
                            S.bump(ge, f"zqhf{opb}")
                            S.bump(ge, f"zqpf{opb}")
                            for i in range(ntile):
                                kc = g4 * 4 + i
                                for half, srcbuf in ((0, shi), (1, slo)):
                                    if dest[0] == "ug":
                                        base = (((dest[1] * 2 + half) * NHC + oc) * 128) * (8 * 128)
                                        dap = bass.AP(pk_ug.tensor, base + kc * 128,
                                                      [[8 * 128, 128], [1, 128]])
                                    else:
                                        base = ((half * NCH + oc) * 128) * (24 * 128)
                                        dap = bass.AP(pk_d.tensor, base + kc * 128,
                                                      [[24 * 128, 128], [1, 128]])
                                    S.inc(ge.dma_start(
                                        out=dap, in_=srcbuf[:, off + i * 128:off + (i + 1) * 128]),
                                        f"dGpk{pb}", 16)
                        S.on("T", t)
                        S.on("S", sv)
                        S.on("V", vv)
                        S.on("G", g2)
                    tr_done = S.tok("zqt")
            def gend(ge):
                S.wait(ge, S.tok("dGpka"))
                S.wait(ge, S.tok("dGpkb"))
            S.on("G", gend)

        def emit_ffn0(xsrc, xdst, rbase, P, ph):
            mb = ph.enter_context(nc.sbuf_tensor(f"f{P}_mb", [128, 2048], F32))
            rb = ph.enter_context(nc.sbuf_tensor(f"f{P}_rb", [128, 2048], F32))
            xt = ph.enter_context(nc.sbuf_tensor(f"f{P}_xt", [128, 2 * 2048], F32))
            t1 = ph.enter_context(nc.sbuf_tensor(f"f{P}_t1", [128, 2048], F32))
            mxs = ph.enter_context(nc.sbuf_tensor(f"f{P}_mx", [128, 18], F32))
            mxs2 = ph.enter_context(nc.sbuf_tensor(f"f{P}_mx2", [128, 384], F32))
            umx = ph.enter_context(nc.sbuf_tensor(f"f{P}_umx", [128, 2], F32))
            gv = ph.enter_context(nc.sbuf_tensor(f"f{P}_gv", [128, 2], F32))

            stsub = ExitStack()
            psg = stsub.enter_context(nc.psum_tensor(f"f{P}_psg", [128, 512], F32))
            psh = stsub.enter_context(nc.psum_tensor(f"f{P}_psh", [128, 512], F32))
            _, rtok = stats_pass(xsrc, 0, NT, f"{P}st", ph, psg, psh)
            stsub.close()
            PSF = {nm: ph.enter_context(nc.psum_tensor(f"f{P}_psb_{nm}", [128, 512], F32))
                   for nm in "abcdefgh"}
            if stage < 2:
                return

            tg = f"{P}L"
            xslot = [None, None]
            for half in range(2):
                tb = half * 2048
                bused = S.tok(f"{tg}v")
                def y(yq, tb=tb, bused=bused):
                    S.wait(yq, rtok)
                    S.wait(yq, bused)
                    bcast_row(yq, mb[:, :], 0, tb, 2048, f"d{tg}b")
                    bcast_row(yq, rb[:, :], 1, tb, 2048, f"d{tg}b")
                S.on("Y", y)
                btok = S.tok(f"d{tg}b")
                for c in range(NCH):
                    sl = c % 2
                    ss = slice(sl * 2048, sl * 2048 + 2048)
                    def y2(yq, c=c, tb=tb, sl=sl, ss=ss, xs=xslot[sl]):
                        S.wait(yq, xs)
                        S.inc(yq.dma_start(out=xt[:, ss], in_=xsrc[c, :, tb:tb + 2048]),
                              f"d{tg}x{sl}", 16)
                    def v(ve, c=c, half=half, sl=sl, ss=ss, btok=btok):
                        S.wait(ve, btok)
                        S.wait(ve, S.tok(f"d{tg}x{sl}"))
                        ve.tensor_tensor(t1[:], xt[:, ss], mb[:], ALU.subtract)
                        S.sig(ve, f"{tg}xf")
                        ve.tensor_tensor(t1[:], t1[:], rb[:], ALU.mult)
                        ve.tensor_reduce(mxs[:, half * NCH + c:half * NCH + c + 1],
                                         t1[:], AX.X, ALU.max, apply_absolute_value=True)
                        S.sig(ve, f"{tg}v")
                    S.on("Y", y2)
                    S.on("V", v)
                    xslot[sl] = S.tok(f"{tg}xf")
            def vm(ve):
                S.wait(ve, S.tok(f"{tg}v"))
                ve.tensor_reduce(mxs[:, 16:17], mxs[:, 0:16], AX.X, ALU.max)
                S.sig(ve, f"{tg}m")
            S.on("V", vm)
            pt = par_max(mxs[:, 16:17], s11[0:1, 0:1], S.tok(f"{tg}m"), tg, PSF["g"])
            ct1 = collective_max(rbase + 0, [s11[0:1, 0:1]], gv[:, 0:1], 1, pt, tg)
            def ya0(yq):
                S.wait(yq, ct1)
                mr_src = bass.AP(rows.tensor, 0 * NT, [[128, 32], [1, 128]])
                rr_src = bass.AP(rows.tensor, 1 * NT, [[128, 32], [1, 128]])
                S.inc(yq.dma_start(out=r32[0:32, 0:128], in_=mr_src), f"d{tg}mr32", 16)
                S.inc(yq.dma_start(out=r32[0:32, 128:256], in_=rr_src), f"d{tg}mr32", 16)
            def va(ve):
                S.wait(ve, ct1)
                S.wait(ve, S.tok(f"d{tg}mr32"))
                ve.tensor_scalar(c_sc[:, 0:1], gv[:, 0:1], 1.0 / QMAXA, EPS_Q,
                                 ALU.mult, ALU.max)
                S.fence(ve)
                ve.reciprocal(c_sc[:, 1:2], c_sc[:, 0:1])
                S.fence(ve)
                ve.tensor_scalar(r32[0:32, 256:384], r32[0:32, 128:256],
                                 c_sc[0:32, 1:2], 0.0, ALU.mult, ALU.add)
                S.fence(ve)
                ve.tensor_tensor(r32[0:32, 384:512], r32[0:32, 0:128],
                                 r32[0:32, 256:384], ALU.mult)
                S.sig(ve, f"{tg}ab")
            def ya(yq):
                S.wait(yq, S.tok(f"{tg}ab"))
                a_dst = bass.AP(rows.tensor, 2 * NT, [[128, 32], [1, 128]])
                b_dst = bass.AP(rows.tensor, 3 * NT, [[128, 32], [1, 128]])
                S.inc(yq.dma_start(out=a_dst, in_=r32[0:32, 256:384]), f"d{tg}ab", 16)
                S.inc(yq.dma_start(out=b_dst, in_=r32[0:32, 384:512]), f"d{tg}ab", 16)
            S.on("Y", ya0)
            S.on("V", va)
            S.on("Y", ya)
            abtok = S.tok(f"d{tg}ab")

            if stage < 3:
                return
            qt = f"{P}Q"
            ph1 = ExitStack()
            kh = ph1.enter_context(nc.sbuf_tensor(f"f{P}_kh", [128, NCH * 2048], BF16))
            kl = ph1.enter_context(nc.sbuf_tensor(f"f{P}_kl", [128, NCH * 2048], BF16))
            wsl = [ph1.enter_context(nc.sbuf_tensor(f"f{P}_w{i}", [128, 1024], BF16))
                   for i in range(8)]
            evb = [ph1.enter_context(nc.sbuf_tensor(f"f{P}_ev{i}", [128, 512], F32))
                   for i in range(4)]
            UPB, GAB = "abcd", "efgh"
            xslot2 = [None, None]
            wslot_free = [None, None]
            bank_free = {b: None for b in "abcdefgh"}
            ev_free = [None, None]
            mm_half_tok = None
            for half in range(2):
                tb = half * 2048
                bused = S.tok(f"{qt}k")
                def y(yq, tb=tb, bused=bused):
                    S.wait(yq, abtok)
                    S.wait(yq, bused)
                    bcast_row(yq, mb[:, :], 2, tb, 2048, f"d{qt}b")
                    bcast_row(yq, rb[:, :], 3, tb, 2048, f"d{qt}b")
                S.on("Y", y)
                btok = S.tok(f"d{qt}b")
                for c in range(NCH):
                    sl = c % 2
                    ss = slice(sl * 2048, sl * 2048 + 2048)
                    def y2(yq, c=c, tb=tb, sl=sl, ss=ss, xs=xslot2[sl]):
                        S.wait(yq, xs)
                        S.inc(yq.dma_start(out=xt[:, ss],
                                           in_=xsrc[c, :, tb:tb + 2048]), f"d{qt}x{sl}", 16)
                    def v(ve, c=c, sl=sl, ss=ss, half=half, btok=btok,
                          mht=mm_half_tok):
                        S.wait(ve, btok)
                        S.wait(ve, S.tok(f"d{qt}x{sl}"))
                        if half == 1:
                            S.wait(ve, mht)
                        ve.tensor_tensor(t1[:], xt[:, ss], mb[:], ALU.mult)
                        S.sig(ve, f"{qt}xf")
                        ve.tensor_tensor(t1[:], t1[:], rb[:], ALU.subtract)
                        ve.tensor_scalar(t1[:], t1[:], MAGIC, MAGIC, ALU.add, ALU.subtract)
                        cs = slice(c * 2048, (c + 1) * 2048)
                        ve.tensor_copy(kh[:, cs], t1[:])
                        ve.tensor_tensor(kl[:, cs], t1[:], kh[:, cs], ALU.subtract)
                        S.sig(ve, f"{qt}k")
                    S.on("Y", y2)
                    S.on("V", v)
                    xslot2[sl] = S.tok(f"{qt}xf")
                ktok = S.tok(f"{qt}k")
                for hc in range(NHC):
                    ws = hc % 2
                    def yw(yq, hc=hc, ws=ws, wf=wslot_free[ws]):
                        S.wait(yq, wf)
                        for j, (var, hl) in enumerate(((0, 0), (0, 1), (1, 0), (1, 1))):
                            S.inc(yq.dma_start(out=wsl[ws * 4 + j][:],
                                               in_=pk_ug[var, hl, hc]), f"d{qt}w{ws}", 16)
                    S.on("Y", yw)
                    wtok = S.tok(f"d{qt}w{ws}")
                    for tt in range(4):
                        pu, pg = UPB[tt], GAB[tt]
                        def t(te, hc=hc, tt=tt, pu=pu, pg=pg, ws=ws, ktok=ktok, wtok=wtok,
                              bfu=bank_free[pu], bfg=bank_free[pg]):
                            S.wait(te, ktok)
                            S.wait(te, wtok)
                            S.wait(te, bfu)
                            S.wait(te, bfg)
                            for var, pb in ((0, pu), (1, pg)):
                                whi = wsl[ws * 4 + var * 2]
                                wlo = wsl[ws * 4 + var * 2 + 1]
                                for kc in range(NCH):
                                    rs = slice(kc * 2048 + tt * 512, kc * 2048 + tt * 512 + 512)
                                    wk = slice(kc * 128, (kc + 1) * 128)
                                    te.matmul(PSF[pb][:, 0:512], whi[:, wk], kh[:, rs],
                                              start=(kc == 0), stop=False)
                                    te.matmul(PSF[pb][:, 0:512], whi[:, wk], kl[:, rs],
                                              start=False, stop=False)
                                    te.matmul(PSF[pb][:, 0:512], wlo[:, wk], kh[:, rs],
                                              start=False, stop=(kc == NCH - 1))
                            S.sig(te, f"{qt}mm")
                        S.on("T", t)
                        mmtok = S.tok(f"{qt}mm")
                        idx = (half * NHC + hc) * 4 + tt
                        def v2(ve, tt=tt, pu=pu, pg=pg, mmtok=mmtok, ef=ev_free[tt % 2],
                               idx=idx):
                            S.wait(ve, mmtok)
                            S.wait(ve, ef)
                            e0, e1 = evb[tt % 2 * 2], evb[tt % 2 * 2 + 1]
                            ve.tensor_scalar(e0[:], PSF[pu][:, 0:512], c_sc[:, 0:1], 0.0,
                                             ALU.mult, ALU.add)
                            S.sig(ve, f"{qt}ec")
                            ve.tensor_scalar(e1[:], PSF[pg][:, 0:512], c_sc[:, 0:1], 0.0,
                                             ALU.mult, ALU.add)
                            S.sig(ve, f"{qt}ec")
                            ve.tensor_reduce(mxs2[:, idx:idx + 1], PSF[pu][:, 0:512], AX.X,
                                             ALU.max, apply_absolute_value=True)
                            S.sig(ve, f"{qt}pv")
                            ve.tensor_reduce(mxs2[:, 192 + idx:193 + idx], PSF[pg][:, 0:512],
                                             AX.X, ALU.max, apply_absolute_value=True)
                            S.sig(ve, f"{qt}pv")
                        S.on("V", v2)
                        pvtok = S.tok(f"{qt}pv")
                        bank_free[pu] = pvtok
                        bank_free[pg] = pvtok
                        ectok = S.tok(f"{qt}ec")
                        def y3(yq, hc=hc, tt=tt, half=half, ectok=ectok):
                            S.wait(yq, ectok)
                            e0, e1 = evb[tt % 2 * 2], evb[tt % 2 * 2 + 1]
                            off = half * 2048 + tt * 512
                            S.inc(yq.dma_start(out=upraw[hc, :, off:off + 512],
                                               in_=e0[:]), f"d{qt}s{tt % 2}", 16)
                            S.inc(yq.dma_start(out=gateraw[hc, :, off:off + 512],
                                               in_=e1[:]), f"d{qt}s{tt % 2}", 16)
                        S.on("Y", y3)
                        ev_free[tt % 2] = S.tok(f"d{qt}s{tt % 2}")
                    wslot_free[ws] = S.tok(f"{qt}mm")
                mm_half_tok = S.tok(f"{qt}mm")
            if stage < 4:
                return
            def yrs(yq):
                S.wait(yq, S.tok(f"d{qt}s0"))
                S.wait(yq, S.tok(f"d{qt}s1"))
                S.bump(yq, f"{qt}rawdone")
            S.on("Y", yrs)
            rawtok = S.tok(f"{qt}rawdone")
            allpv = S.tok(f"{qt}pv")
            def vfin(ve):
                S.wait(ve, allpv)
                ve.tensor_reduce(umx[:, 0:1], mxs2[:, 0:192], AX.X, ALU.max)
                S.sig(ve, f"{qt}fin")
                ve.tensor_reduce(umx[:, 1:2], mxs2[:, 192:384], AX.X, ALU.max)
                S.sig(ve, f"{qt}fin")
            S.on("V", vfin)
            fintok = S.tok(f"{qt}fin")
            def tq(te):
                S.wait(te, fintok)
                te.transpose(PSF["a"][0:1, 0:128], umx[:, 0:1], c_iden[:])
                te.transpose(PSF["a"][0:1, 128:256], umx[:, 1:2], c_iden[:])
                S.sig(te, f"{qt}tp")
            def vq2(ve):
                S.wait(ve, S.tok(f"{qt}tp"))
                ve.tensor_reduce(s11[0:1, 0:1], PSF["a"][0:1, 0:128], AX.X, ALU.max)
                S.sig(ve, f"{qt}pm")
                ve.tensor_reduce(s11[0:1, 1:2], PSF["a"][0:1, 128:256], AX.X, ALU.max)
                S.sig(ve, f"{qt}pm")
            S.on("T", tq)
            S.on("V", vq2)
            ct23 = collective_max(rbase + 1, [s11[0:1, 0:1], s11[0:1, 1:2]],
                                  gv[:, 0:2], 2, S.tok(f"{qt}pm"), qt)
            def vsc(ve):
                S.wait(ve, ct23)
                ve.tensor_scalar(mxs[:, 0:2], gv[:, 0:2], c_sc[:, 0:1], 0.0,
                                 ALU.mult, ALU.add)
                S.fence(ve)
                ve.tensor_scalar(c_sc[:, 4:6], mxs[:, 0:2], 1.0 / QMAXA, EPS_Q,
                                 ALU.mult, ALU.max)
                S.fence(ve)
                ve.reciprocal(c_sc[:, 2:3], c_sc[:, 4:5])
                ve.reciprocal(c_sc[:, 3:4], c_sc[:, 5:6])
                ve.memset(umx[:, 0:1], 0.0)
                S.fence(ve)
                ve.tensor_copy(c_sc[:, 14:15], c_sc[:, 2:3])
                S.sig(ve, f"{qt}sc")
            S.on("V", vsc)
            sctok = S.tok(f"{qt}sc")

            if stage < 5:
                return
            ph1.close()
            at = f"{P}A"
            a_up = ph.enter_context(nc.sbuf_tensor(f"f{P}_aup", [128, 2 * 512], F32))
            a_ga = ph.enter_context(nc.sbuf_tensor(f"f{P}_aga", [128, 2 * 512], F32))
            a_t = ph.enter_context(nc.sbuf_tensor(f"f{P}_atA", [128, 2 * 512], F32))
            a_t2 = ph.enter_context(nc.sbuf_tensor(f"f{P}_atB", [128, 2 * 512], F32))
            a_i16 = ph.enter_context(nc.sbuf_tensor(f"f{P}_i16", [128, 2 * 512], I16))
            a_pr = ph.enter_context(nc.sbuf_tensor(f"f{P}_apr", [128, 2 * 512], F32))
            ntile = NHC * NT // 512
            in_free = [None, None]
            out_free = [None, None]
            for j in range(ntile):
                hc, off = divmod(j * 512, NT)
                sl = j % 2
                ss = slice(sl * 512, sl * 512 + 512)
                def y(yq, hc=hc, off=off, sl=sl, ss=ss, if_=in_free[sl]):
                    S.wait(yq, sctok)
                    S.wait(yq, rawtok)
                    S.wait(yq, if_)
                    S.inc(yq.dma_start(out=a_up[:, ss], in_=upraw[hc, :, off:off + 512]),
                          f"d{at}i{sl}", 16)
                    S.inc(yq.dma_start(out=a_ga[:, ss], in_=gateraw[hc, :, off:off + 512]),
                          f"d{at}i{sl}", 16)
                def v(ve, sl=sl, ss=ss, of_=out_free[sl]):
                    S.wait(ve, S.tok(f"d{at}i{sl}"))
                    S.wait(ve, sctok)
                    S.wait(ve, of_)
                    ve.tensor_scalar(a_t[:, ss], a_up[:, ss], c_sc[:, 2:3], MAGIC,
                                     ALU.mult, ALU.add)
                    S.fence(ve)
                    ve.tensor_scalar(a_t[:, ss], a_t[:, ss], MAGIC, 0.0,
                                     ALU.subtract, ALU.add)
                    ve.tensor_copy(a_i16[:, ss], a_t[:, ss])
                    S.fence(ve)
                    ve.tensor_scalar(a_t[:, ss], a_ga[:, ss], c_sc[:, 3:4], MAGIC,
                                     ALU.mult, ALU.add)
                    iftok = S.sig(ve, f"{at}if")
                    S.wait(ve, iftok)
                    ve.tensor_scalar(a_t[:, ss], a_t[:, ss], MAGIC, 0.0,
                                     ALU.subtract, ALU.add)
                    S.sig(ve, f"{at}gq")
                def s(se, sl=sl, ss=ss):
                    S.wait(se, S.tok(f"{at}gq"))
                    se.activation(a_t2[:, ss], a_t[:, ss], AF.Sigmoid,
                                  scale=c_sc[:, 5:6])
                    S.sig(se, f"{at}sg")
                def v2(ve, sl=sl, ss=ss, j=j):
                    S.wait(ve, S.tok(f"{at}sg"))
                    ve.tensor_scalar(a_t2[:, ss], a_t2[:, ss], 65536.0, MAGIC,
                                     ALU.mult, ALU.add)
                    ve.tensor_scalar(a_t2[:, ss], a_t2[:, ss], MAGIC, 65535.0,
                                     ALU.subtract, ALU.min)
                    ve.tensor_scalar(a_t2[:, ss], a_t2[:, ss], 1.0 / 65536.0, 0.0,
                                     ALU.mult, ALU.add)
                    ve.tensor_tensor(a_pr[:, ss], a_t[:, ss], a_t2[:, ss], ALU.mult)
                    S.sig(ve, f"{at}pr")
                    ve.tensor_reduce(mxs2[:, j:j + 1], a_pr[:, ss], AX.X, ALU.max,
                                     apply_absolute_value=True)
                    S.sig(ve, f"{at}prm")
                def y2(yq, hc=hc, off=off, sl=sl, ss=ss):
                    S.wait(yq, S.tok(f"{at}pr"))
                    S.inc(yq.dma_start(out=m2i[hc, :, off:off + 512], in_=a_i16[:, ss]),
                          f"d{at}o{sl}", 16)
                    S.inc(yq.dma_start(out=prodb[hc, :, off:off + 512], in_=a_pr[:, ss]),
                          f"d{at}o{sl}", 16)
                S.on("Y", y)
                S.on("V", v)
                S.on("S", s)
                S.on("V", v2)
                S.on("Y", y2)
                in_free[sl] = S.tok(f"{at}if")
                out_free[sl] = S.tok(f"d{at}o{sl}")
            def vfin4(ve):
                S.wait(ve, S.tok(f"{at}prm"))
                ve.tensor_reduce(umx[:, 0:1], mxs2[:, 0:192], AX.X, ALU.max)
                S.sig(ve, f"{at}fin")
            S.on("V", vfin4)
            pt4 = par_max(umx[:, 0:1], s11[0:1, 0:1], S.tok(f"{at}fin"), at, PSF["g"])
            ct4 = collective_max(rbase + 2, [s11[0:1, 0:1]], gv[:, 0:1], 1, pt4, at)
            def vs4(ve):
                S.wait(ve, ct4)
                ve.tensor_scalar(mxs[:, 0:1], gv[:, 0:1], c_sc[:, 5:6], 0.0,
                                 ALU.mult, ALU.add)
                S.fence(ve)
                ve.tensor_scalar(c_sc[:, 6:7], mxs[:, 0:1], 1.0 / QMAXA, EPS_Q,
                                 ALU.mult, ALU.max)
                S.fence(ve)
                ve.reciprocal(c_sc[:, 13:14], c_sc[:, 6:7])
                S.fence(ve)
                ve.tensor_tensor(c_sc[:, 7:8], c_sc[:, 13:14], c_sc[:, 5:6], ALU.mult)
                S.sig(ve, f"{at}s4")
            S.on("V", vs4)
            s4tok = S.tok(f"{at}s4")
            def yod(yq):
                S.wait(yq, S.tok(f"d{at}o0"))
                S.wait(yq, S.tok(f"d{at}o1"))
                S.bump(yq, f"{at}odone")
            S.on("Y", yod)
            aodone = S.tok(f"{at}odone")

            if stage < 6:
                return
            bt = f"{P}B"
            in_free = [None, None]
            out_free = [None, None]
            for j in range(ntile):
                hc, off = divmod(j * 512, NT)
                sl = j % 2
                ss = slice(sl * 512, sl * 512 + 512)
                def y(yq, hc=hc, off=off, sl=sl, ss=ss, if_=in_free[sl]):
                    S.wait(yq, s4tok)
                    S.wait(yq, aodone)
                    S.wait(yq, if_)
                    S.inc(yq.dma_start(out=a_up[:, ss], in_=prodb[hc, :, off:off + 512]),
                          f"d{bt}i{sl}", 16)
                    S.inc(yq.dma_start(out=a_i16[:, ss], in_=m2i[hc, :, off:off + 512]),
                          f"d{bt}i{sl}", 16)
                def v(ve, sl=sl, ss=ss, of_=out_free[sl], j=j):
                    S.wait(ve, S.tok(f"d{bt}i{sl}"))
                    S.wait(ve, s4tok)
                    S.wait(ve, of_)
                    ve.tensor_scalar(a_t[:, ss], a_up[:, ss], c_sc[:, 7:8], MAGIC,
                                     ALU.mult, ALU.add)
                    S.fence(ve)
                    ve.tensor_scalar(a_t[:, ss], a_t[:, ss], MAGIC, 0.0,
                                     ALU.subtract, ALU.add)
                    ve.tensor_copy(a_t2[:, ss], a_i16[:, ss])
                    S.sig(ve, f"{bt}if")
                    ve.tensor_tensor(a_pr[:, ss], a_t[:, ss], a_t2[:, ss], ALU.mult)
                    S.sig(ve, f"{bt}pr")
                    ve.tensor_reduce(mxs2[:, j:j + 1], a_pr[:, ss], AX.X, ALU.max,
                                     apply_absolute_value=True)
                    S.sig(ve, f"{bt}prm")
                def y2(yq, hc=hc, off=off, sl=sl, ss=ss):
                    S.wait(yq, S.tok(f"{bt}pr"))
                    S.inc(yq.dma_start(out=mmb[hc, :, off:off + 512], in_=a_pr[:, ss]),
                          f"d{bt}o{sl}", 16)
                S.on("Y", y)
                S.on("V", v)
                S.on("Y", y2)
                in_free[sl] = S.tok(f"{bt}if")
                out_free[sl] = S.tok(f"d{bt}o{sl}")
            def vfin5(ve):
                S.wait(ve, S.tok(f"{bt}prm"))
                ve.tensor_reduce(umx[:, 0:1], mxs2[:, 0:192], AX.X, ALU.max)
                S.sig(ve, f"{bt}fin")
            S.on("V", vfin5)
            pt5 = par_max(umx[:, 0:1], s11[0:1, 0:1], S.tok(f"{bt}fin"), bt, PSF["g"])
            ct5 = collective_max(rbase + 3, [s11[0:1, 0:1]], gv[:, 0:1], 1, pt5, bt)
            def vs5(ve):
                S.wait(ve, ct5)
                ve.tensor_tensor(c_sc[:, 8:9], c_sc[:, 6:7], c_sc[:, 4:5], ALU.mult)
                S.fence(ve)
                ve.tensor_scalar(mxs[:, 0:1], gv[:, 0:1], c_sc[:, 8:9], 0.0,
                                 ALU.mult, ALU.add)
                S.fence(ve)
                ve.tensor_scalar(c_sc[:, 9:10], mxs[:, 0:1], 1.0 / QMAXA, EPS_Q,
                                 ALU.mult, ALU.max)
                S.fence(ve)
                ve.reciprocal(c_sc[:, 10:11], c_sc[:, 9:10])
                S.fence(ve)
                ve.tensor_tensor(c_sc[:, 11:12], c_sc[:, 8:9], c_sc[:, 10:11], ALU.mult)
                ve.tensor_scalar(c_sc[:, 12:13], c_sc[:, 9:10], 0.5, 0.0, ALU.mult, ALU.add)
                S.sig(ve, f"{bt}s5")
            S.on("V", vs5)
            s5tok = S.tok(f"{bt}s5")
            def yod2(yq):
                S.wait(yq, S.tok(f"d{bt}o0"))
                S.wait(yq, S.tok(f"d{bt}o1"))
                S.bump(yq, f"{bt}odone")
            S.on("Y", yod2)
            bodone = S.tok(f"{bt}odone")

            if stage < 7:
                return
            dt = f"{P}D"
            m5h = ph.enter_context(nc.sbuf_tensor(f"f{P}_m5h", [128, NHC * 512], BF16))
            m5l = ph.enter_context(nc.sbuf_tensor(f"f{P}_m5l", [128, NHC * 512], BF16))
            dwb = [ph.enter_context(nc.sbuf_tensor(f"f{P}_dw{i}", [128, 24 * 128], BF16))
                   for i in range(4)]
            in_free = [None, None]
            w_free = [None, None]
            bank_free = {"a": None, "b": None}
            x_free = {"a": None, "b": None}
            y_free = {"a": None, "b": None}
            mm_tt_tok = None
            for tt in range(NT // 512):
                off = tt * 512
                for hc in range(NHC):
                    sl = hc % 2
                    ss = slice(sl * 512, sl * 512 + 512)
                    def y(yq, hc=hc, off=off, sl=sl, ss=ss, if_=in_free[sl]):
                        S.wait(yq, s5tok)
                        S.wait(yq, bodone)
                        S.wait(yq, if_)
                        S.inc(yq.dma_start(out=a_up[:, ss], in_=mmb[hc, :, off:off + 512]),
                              f"d{dt}i{sl}", 16)
                    def v(ve, hc=hc, sl=sl, ss=ss, tt=tt, mtt=mm_tt_tok):
                        S.wait(ve, S.tok(f"d{dt}i{sl}"))
                        S.wait(ve, s5tok)
                        if tt > 0 and hc == 0:
                            S.wait(ve, mtt)
                        ve.tensor_scalar(a_t[:, ss], a_up[:, ss], c_sc[:, 11:12], MAGIC,
                                         ALU.mult, ALU.add)
                        iftok = S.sig(ve, f"{dt}if")
                        S.wait(ve, iftok)
                        ve.tensor_scalar(a_t[:, ss], a_t[:, ss], MAGIC, 0.0,
                                         ALU.subtract, ALU.add)
                        hs = slice(hc * 512, (hc + 1) * 512)
                        ve.tensor_copy(m5h[:, hs], a_t[:, ss])
                        ve.tensor_tensor(m5l[:, hs], a_t[:, ss], m5h[:, hs], ALU.subtract)
                        S.sig(ve, f"{dt}m5")
                    S.on("Y", y)
                    S.on("V", v)
                    in_free[sl] = S.tok(f"{dt}if")
                m5tok = S.tok(f"{dt}m5")
                for dc in range(NCH):
                    ws = dc % 2
                    pb = "ab"[dc % 2]
                    def yw(yq, dc=dc, ws=ws, wf=w_free[ws]):
                        S.wait(yq, wf)
                        for j, hl in enumerate((0, 1)):
                            S.inc(yq.dma_start(out=dwb[ws * 2 + j][:], in_=pk_d[hl, dc]),
                                  f"d{dt}w{ws}", 16)
                    S.on("Y", yw)
                    wtok = S.tok(f"d{dt}w{ws}")
                    def t(te, dc=dc, ws=ws, pb=pb, m5tok=m5tok, wtok=wtok,
                          bf=bank_free[pb]):
                        S.wait(te, m5tok)
                        S.wait(te, wtok)
                        S.wait(te, bf)
                        whi, wlo = dwb[ws * 2], dwb[ws * 2 + 1]
                        for kc in range(NHC):
                            ks = slice(kc * 512, kc * 512 + 512)
                            wk = slice(kc * 128, (kc + 1) * 128)
                            te.matmul(PSF[pb][:, 0:512], whi[:, wk], m5h[:, ks],
                                      start=(kc == 0), stop=False)
                            te.matmul(PSF[pb][:, 0:512], whi[:, wk], m5l[:, ks],
                                      start=False, stop=False)
                            te.matmul(PSF[pb][:, 0:512], wlo[:, wk], m5h[:, ks],
                                      start=False, stop=(kc == NHC - 1))
                        S.sig(te, f"{dt}mm")
                    S.on("T", t)
                    w_free[ws] = S.tok(f"{dt}mm")
                    mmtok = S.tok(f"{dt}mm")
                    def y2(yq, dc=dc, off=off, pb=pb, xf=x_free[pb]):
                        S.wait(yq, xf)
                        xoff = 0 if pb == "a" else 512
                        S.inc(yq.dma_start(out=xt[:, xoff:xoff + 512],
                                           in_=xsrc[dc, :, off:off + 512]), f"d{dt}x{pb}", 16)
                    def v2(ve, dc=dc, off=off, pb=pb, mmtok=mmtok, yf=y_free[pb]):
                        S.wait(ve, mmtok)
                        S.wait(ve, S.tok(f"d{dt}x{pb}"))
                        S.wait(ve, yf)
                        xoff = 0 if pb == "a" else 512
                        ysl = t1[:, 0:512] if pb == "a" else t1[:, 512:1024]
                        ve.tensor_scalar(ysl, PSF[pb][:, 0:512], c_sc[:, 12:13], 0.0,
                                         ALU.mult, ALU.add)
                        pvt = S.sig(ve, f"{dt}pv")
                        S.wait(ve, pvt)
                        ve.tensor_tensor(ysl, ysl, xt[:, xoff:xoff + 512], ALU.add)
                        S.sig(ve, f"{dt}yv")
                    S.on("Y", y2)
                    S.on("V", v2)
                    bank_free[pb] = S.tok(f"{dt}pv")
                    x_free[pb] = S.tok(f"{dt}yv")
                    yvtok = S.tok(f"{dt}yv")
                    def y3(yq, dc=dc, off=off, pb=pb, yvtok=yvtok):
                        S.wait(yq, yvtok)
                        ysl = t1[:, 0:512] if pb == "a" else t1[:, 512:1024]
                        S.inc(yq.dma_start(out=xdst[dc, :, off:off + 512], in_=ysl),
                              f"d{dt}o", 16)
                    S.on("Y", y3)
                    y_free[pb] = S.tok(f"d{dt}o")
                mm_tt_tok = S.tok(f"{dt}mm")
            def yend(yq):
                S.wait(yq, S.tok(f"d{dt}o"))
            S.on("Y", yend)


        def attn_tiles(layer, qi_g):
            if layer == 1 or qi_g <= 3:
                return [(0, 0, qi_g + 1)], {qi_g: 0}
            if qi_g == 4:
                return [(0, 0, 5)], {0: 2, 4: 0}
            return [(0, qi_g - 4, 5), (5, 0, 1)], {0: 1, 4: 0, 5: 3}

        def emit_attn(xsrc, xdst, layer, P, ph):
            L1 = (layer == 1)
            wsrc = l1_w if L1 else l0_w
            stsub = ExitStack()
            psg = stsub.enter_context(nc.psum_tensor(f"a{P}_psg", [128, 512], F32))
            psh = stsub.enter_context(nc.psum_tensor(f"a{P}_psh", [128, 512], F32))
            _, rtok = stats_pass(xsrc, 0, NT, f"{P}st", ph, psg, psh)
            stsub.close()
            ps_sc = ph.enter_context(nc.psum_tensor(f"a{P}_sc", [128, 2048], F32))
            ps_tp = ph.enter_context(nc.psum_tensor(f"a{P}_tp", [128, 512], F32))
            ps_av = ph.enter_context(nc.psum_tensor(f"a{P}_av", [128, 512], F32))
            pj = [ph.enter_context(nc.psum_tensor(f"a{P}_pj{i}", [128, 512], F32))
                  for i in range(2)]
            mb = ph.enter_context(nc.sbuf_tensor(f"a{P}_mb", [128, 2048], F32))
            rb = ph.enter_context(nc.sbuf_tensor(f"a{P}_rb", [128, 2048], F32))
            lnh = ph.enter_context(nc.sbuf_tensor(f"a{P}_lnh", [128, NCH * 2048], BF16))
            lnl = ph.enter_context(nc.sbuf_tensor(f"a{P}_lnl", [128, NCH * 2048], BF16))
            xt = ph.enter_context(nc.sbuf_tensor(f"a{P}_xt", [128, 2048], F32))
            st32 = ph.enter_context(nc.sbuf_tensor(f"a{P}_st", [128, 2048], F32))
            dall = ph.enter_context(nc.sbuf_tensor(f"a{P}_da", [128, 16], F32))
            biasv = ph.enter_context(nc.sbuf_tensor(f"a{P}_bv", [128, 8], F32))
            xprev = [None]
            for s_ in range(2):
                t0 = s_ * 2048
                Q = f"{P}s{s_}"
                prev_omm = S.tok(f"{P}s{s_-1}omm") if s_ else None
                def yb(yq, t0=t0, lc=S.tok(f"{P}alnc")):
                    S.wait(yq, rtok)
                    if lc[1] > 0:
                        S.wait(yq, lc)
                    bcast_row(yq, mb[:, :], 0, t0, 2048, f"d{Q}b")
                    bcast_row(yq, rb[:, :], 1, t0, 2048, f"d{Q}b")
                S.on("Y", yb)
                btok = S.tok(f"d{Q}b")
                for c in range(NCH):
                    def y2(yq, c=c, t0=t0, xs=xprev[0]):
                        S.wait(yq, xs)
                        S.inc(yq.dma_start(out=xt[:, :], in_=xsrc[c, :, t0:t0 + 2048]),
                              f"d{Q}x", 16)
                    def v(ve, c=c, btok=btok, xd=None):
                        S.wait(ve, btok)
                        S.wait(ve, S.tok(f"d{Q}x"))
                        ve.tensor_tensor(st32[:], xt[:, :], mb[:], ALU.subtract)
                        S.sig(ve, f"{P}axf")
                        ve.tensor_tensor(st32[:], st32[:], rb[:], ALU.mult)
                        cs = slice(c * 2048, (c + 1) * 2048)
                        ve.tensor_copy(lnh[:, cs], st32[:])
                        ve.tensor_tensor(lnl[:, cs], st32[:], lnh[:, cs], ALU.subtract)
                        S.sig(ve, f"{P}alnc")
                    S.on("Y", y2)
                    S.on("V", v)
                    xprev[0] = S.tok(f"{P}axf")
                lntok = S.tok(f"{P}alnc")
                subA = ExitStack()
                w32 = subA.enter_context(nc.sbuf_tensor(f"a{P}{s_}_w32", [128, 1024], F32))
                whi = subA.enter_context(nc.sbuf_tensor(f"a{P}{s_}_whi", [128, 1024], BF16))
                wlo = subA.enter_context(nc.sbuf_tensor(f"a{P}{s_}_wlo", [128, 1024], BF16))
                qh = subA.enter_context(nc.sbuf_tensor(f"a{P}{s_}_qh", [128, 2048], BF16))
                ql = subA.enter_context(nc.sbuf_tensor(f"a{P}{s_}_ql", [128, 2048], BF16))
                kh2 = subA.enter_context(nc.sbuf_tensor(f"a{P}{s_}_kh", [128, 2048], BF16))
                kl2 = subA.enter_context(nc.sbuf_tensor(f"a{P}{s_}_kl", [128, 2048], BF16))
                v32 = subA.enter_context(nc.sbuf_tensor(f"a{P}{s_}_v32", [128, 2048], F32))
                vT = subA.enter_context(nc.sbuf_tensor(f"a{P}{s_}_vT", [128, 2048], F32))
                p32 = subA.enter_context(nc.sbuf_tensor(f"a{P}{s_}_p32", [128, 2048], F32))
                pT = subA.enter_context(nc.sbuf_tensor(f"a{P}{s_}_pT", [128, 2048], F32))
                av2 = subA.enter_context(nc.sbuf_tensor(f"a{P}{s_}_av2", [64, 2048], F32))
                ivd = subA.enter_context(nc.sbuf_tensor(f"a{P}{s_}_ivd", [64, 2048], F32))
                exp_par = [None, None]
                exp_full_box = [None]
                for hg in range(8):
                    lastqk = S.tok(f"{Q}qk")
                    lastav = S.tok(f"{Q}avmm")
                    for pi, pname in enumerate("qkv"):
                        def yw(yq, pname=pname, hg=hg, pm=S.tok(f"{Q}pmm"),
                               prev_omm=prev_omm, first=(hg == 0 and pi == 0)):
                            S.wait(yq, pm)
                            if first and prev_omm is not None:
                                S.wait(yq, prev_omm)
                            src = bass.AP(wsrc[pname].tensor, hg * 128,
                                          [[1024, 128], [128 * 1024, 8], [1, 128]])
                            S.inc(yq.dma_start(
                                out=w32[:].rearrange("p (k f) -> p k f", k=8), in_=src),
                                f"d{Q}w", 16)
                        def vw(ve, wtok=None):
                            S.wait(ve, S.tok(f"d{Q}w"))
                            ve.tensor_copy(whi[:], w32[:])
                            ve.tensor_tensor(wlo[:], w32[:], whi[:], ALU.subtract)
                            S.sig(ve, f"{Q}wsp")
                        S.on("Y", yw)
                        S.on("V", vw)
                        wsptok = S.tok(f"{Q}wsp")
                        for tq in range(4):
                            pjb = pj[tq % 2]
                            def t(te, tq=tq, pjb=pjb, wsptok=wsptok, lntok=lntok,
                                  pf=S.tok(f"{Q}pev"), pi=pi, lastqk=lastqk, lastav=lastav):
                                S.wait(te, wsptok)
                                S.wait(te, lntok)
                                if pf[1] >= 2:
                                    S.wait(te, (pf[0], pf[1] - 1))
                                if pi < 2:
                                    S.wait(te, lastqk)
                                else:
                                    S.wait(te, lastav)
                                for kc in range(NCH):
                                    rs = slice(kc * 2048 + tq * 512, kc * 2048 + tq * 512 + 512)
                                    wk = slice(kc * 128, (kc + 1) * 128)
                                    te.matmul(pjb[:, 0:512], whi[:, wk], lnh[:, rs],
                                              start=(kc == 0), stop=False)
                                    te.matmul(pjb[:, 0:512], wlo[:, wk], lnh[:, rs],
                                              start=False, stop=False)
                                    te.matmul(pjb[:, 0:512], whi[:, wk], lnl[:, rs],
                                              start=False, stop=(kc == NCH - 1))
                                S.sig(te, f"{Q}pmm")
                            S.on("T", t)
                            mmtok = S.tok(f"{Q}pmm")
                            def sev(se, tq=tq, pjb=pjb, pname=pname, hg=hg, mmtok=mmtok,
                                    sf=S.tok(f"{Q}spl")):
                                S.wait(se, mmtok)
                                if pname != "v":
                                    S.wait(se, sf)
                                dst = (v32 if pname == "v" else st32)[:, tq * 512:(tq + 1) * 512]
                                if L1:
                                    se.activation(dst, pjb[:, 0:512], AF.Identity,
                                                  bias=c_b[pname][:, hg:hg + 1])
                                else:
                                    se.activation(dst, pjb[:, 0:512], AF.Identity)
                                S.sig(se, f"{Q}pev")
                            S.on("S", sev)
                        evtok = S.tok(f"{Q}pev")
                        if pname in ("q", "k"):
                            dsth, dstl = (qh, ql) if pname == "q" else (kh2, kl2)
                            def vsp(ve, dsth=dsth, dstl=dstl, evtok=evtok):
                                S.wait(ve, evtok)
                                ve.tensor_copy(dsth[:], st32[:])
                                ve.tensor_tensor(dstl[:], st32[:], dsth[:], ALU.subtract)
                                S.sig(ve, f"{Q}spl")
                            S.on("V", vsp)
                        else:
                            for g4 in range(4):
                                def tv(te, g4=g4, evtok=evtok, tf=S.tok(f"{Q}tpf")):
                                    S.wait(te, evtok)
                                    S.wait(te, tf)
                                    for i in range(4):
                                        kt = g4 * 4 + i
                                        te.transpose(ps_tp[:, i * 128:(i + 1) * 128],
                                                     v32[:, kt * 128:(kt + 1) * 128],
                                                     c_iden[:])
                                    S.sig(te, f"{Q}vtt")
                                def vv2(ve, g4=g4, vt=S.tok(f"{Q}vtt")):
                                    S.wait(ve, (f"{Q}vtt", vt[1] + 1))
                                    ve.tensor_copy(vT[:, g4 * 512:(g4 + 1) * 512],
                                                   ps_tp[:, 0:512])
                                    S.sig(ve, f"{Q}tpf")
                                S.on("T", tv)
                                S.on("V", vv2)
                    vttok = S.tok(f"{Q}tpf")
                    for h in range(2):
                        hofs = 64 * h
                        avwr = S.tok(f"d{Q}avh")
                        for qi_g in range(16):
                            runs, mtiles = attn_tiles(layer, qi_g)
                            ntot = runs[-1][0] + runs[-1][2]
                            par = qi_g % 2
                            if ntot <= 8:
                                base = 8 * par
                                scfree = [exp_par[par], exp_full_box[0]]
                            else:
                                base = 0
                                scfree = [exp_par[0], exp_par[1], exp_full_box[0]]
                            qsl = slice(qi_g * 128, (qi_g + 1) * 128)
                            def tqk(te, runs=runs, qsl=qsl, hofs=hofs, base=base,
                                    scfree=tuple(scfree), vttok=vttok,
                                    lastsp=S.tok(f"{Q}spl")):
                                S.wait(te, lastsp)
                                S.wait(te, vttok)
                                for tk in scfree:
                                    S.wait(te, tk)
                                for (slot0, kt0, n) in runs:
                                    for pss in range(3):
                                        lh = (qh, ql, qh)[pss]
                                        rh = (kh2, kh2, kl2)[pss]
                                        lt = lh[hofs:hofs + 64, qsl]
                                        done = 0
                                        while done < n:
                                            pos = base + slot0 + done
                                            take = min(4 - pos % 4, n - done)
                                            te.matmul(
                                                ps_sc[:, pos * 128:(pos + take) * 128],
                                                lt,
                                                rh[hofs:hofs + 64,
                                                   (kt0 + done) * 128:(kt0 + done + take) * 128],
                                                start=(pss == 0), stop=(pss == 2))
                                            done += take
                                S.sig(te, f"{Q}qk")
                            S.on("T", tqk)
                            qktok = S.tok(f"{Q}qk")
                            def vmx(ve, mtiles=mtiles, ntot=ntot, base=base, par=par,
                                    qktok=qktok, bvfree=exp_par[par]):
                                S.wait(ve, qktok)
                                S.wait(ve, bvfree)
                                for slot, mi in mtiles.items():
                                    cs = slice((base + slot) * 128, (base + slot + 1) * 128)
                                    ve.tensor_tensor(ps_sc[:, cs], ps_sc[:, cs],
                                                     c_masks[:, mi * 128:(mi + 1) * 128],
                                                     ALU.add)
                                S.fence(ve)
                                ve.tensor_reduce(biasv[:, 4 * par:4 * par + 1],
                                                 ps_sc[:, base * 128:(base + ntot) * 128],
                                                 AX.X, ALU.max)
                                S.fence(ve)
                                ve.tensor_scalar(biasv[:, 4 * par + 1:4 * par + 2],
                                                 biasv[:, 4 * par:4 * par + 1], -0.125, 0.0,
                                                 ALU.mult, ALU.add)
                                S.sig(ve, f"{Q}bv")
                            S.on("V", vmx)
                            bvtok = S.tok(f"{Q}bv")
                            def sexp(se, ntot=ntot, base=base, par=par, qi_g=qi_g,
                                     bvtok=bvtok, ttprev=S.tok(f"{Q}ptt")):
                                S.wait(se, bvtok)
                                S.wait(se, ttprev)
                                cs = slice(base * 128, (base + ntot) * 128)
                                se.activation(p32[:, cs], ps_sc[:, cs],
                                              AF.Exp, bias=biasv[:, 4 * par + 1:4 * par + 2],
                                              scale=0.125,
                                              accum_out=dall[:, qi_g:qi_g + 1])
                                S.sig(se, f"{Q}exp")
                            S.on("S", sexp)
                            exptok = S.tok(f"{Q}exp")
                            if ntot <= 8:
                                exp_par[par] = exptok
                            else:
                                exp_full_box[0] = exp_par[0] = exp_par[1] = exptok
                            for g0 in range(0, ntot, 4):
                                nt_ = min(4, ntot - g0)
                                def tpp(te, g0=g0, nt_=nt_, base=base, exptok=exptok,
                                        tf=S.tok(f"{Q}tpf")):
                                    S.wait(te, exptok)
                                    S.wait(te, tf)
                                    for i in range(nt_):
                                        slot = base + g0 + i
                                        te.transpose(ps_tp[:, i * 128:(i + 1) * 128],
                                                     p32[:, slot * 128:(slot + 1) * 128],
                                                     c_iden[:])
                                    S.sig(te, f"{Q}ptt")
                                def vpp(ve, g0=g0, nt_=nt_, base=base,
                                        tt=S.tok(f"{Q}ptt")):
                                    S.wait(ve, (f"{Q}ptt", tt[1] + 1))
                                    ve.tensor_copy(
                                        pT[:, (base + g0) * 128:(base + g0 + nt_) * 128],
                                        ps_tp[:, 0:nt_ * 128])
                                    S.sig(ve, f"{Q}tpf")
                                S.on("T", tpp)
                                S.on("V", vpp)
                            pttok = S.tok(f"{Q}tpf")
                            def tav(te, hofs=hofs, runs=runs, ntot=ntot, base=base,
                                    pttok=pttok, avev=S.tok(f"{Q}avev")):
                                S.wait(te, pttok)
                                S.wait(te, avev)
                                j = 0
                                for (slot0, kt0, n) in runs:
                                    for i in range(n):
                                        slot, kt = base + slot0 + i, kt0 + i
                                        te.matmul(ps_av[0:64, 0:128],
                                                  vT[:, kt * 128 + hofs:kt * 128 + hofs + 64],
                                                  pT[:, slot * 128:(slot + 1) * 128],
                                                  start=(j == 0), stop=(j == ntot - 1))
                                        j += 1
                                S.sig(te, f"{Q}avmm")
                            S.on("T", tav)
                            avtok = S.tok(f"{Q}avmm")
                            def vav(ve, qi_g=qi_g, avtok=avtok, avwr=avwr):
                                S.wait(ve, avtok)
                                if qi_g == 0:
                                    S.wait(ve, avwr)
                                ve.tensor_copy(av2[0:64, qsl], ps_av[0:64, 0:128])
                                S.sig(ve, f"{Q}avev")
                            S.on("V", vav)
                        exptok = S.tok(f"{Q}exp")
                        def tdt(te, exptok=exptok, tf=S.tok(f"{Q}tpf")):
                            S.wait(te, exptok)
                            S.wait(te, tf)
                            te.transpose(ps_tp[0:16, 0:128], dall[:, 0:16], c_iden[:])
                            S.sig(te, f"{Q}dtt")
                        def vdt(ve, dt=S.tok(f"{Q}dtt")):
                            S.wait(ve, (f"{Q}dtt", dt[1] + 1))
                            ve.tensor_copy(st32[0:16, 0:128], ps_tp[0:16, 0:128])
                            ftok = S.sig(ve, f"{Q}tpf")
                            S.wait(ve, ftok)
                            ve.reciprocal(st32[0:16, 128:256], st32[0:16, 0:128])
                            S.sig(ve, f"{Q}div")
                        def ydt(yq, h=h, dv=S.tok(f"{Q}div")):
                            S.wait(yq, (f"{Q}div", dv[1] + 1))
                            dst = bass.AP(rows.tensor, (4 + h) * NT, [[128, 16], [1, 128]])
                            S.inc(yq.dma_start(out=dst, in_=st32[0:16, 128:256]),
                                  f"d{Q}iv", 16)
                        S.on("T", tdt)
                        S.on("V", vdt)
                        S.on("Y", ydt)
                        ivtok = S.tok(f"d{Q}iv")
                        def ybc(yq, h=h, ivtok=ivtok, avn=S.tok(f"{Q}avn")):
                            S.wait(yq, ivtok)
                            if avn[1] > 0:
                                S.wait(yq, avn)
                            src = bass.AP(rows.tensor, (4 + h) * NT, [[0, 64], [1, 2048]])
                            S.inc(yq.dma_start(out=ivd[0:64, :], in_=src),
                                  f"d{Q}ivb", 16)
                        def vnr(ve, bt_=S.tok(f"d{Q}ivb"), ave=S.tok(f"{Q}avev")):
                            S.wait(ve, (f"d{Q}ivb", bt_[1] + 16))
                            S.wait(ve, ave)
                            ve.tensor_tensor(av2[0:64, :], av2[0:64, :], ivd[0:64, :],
                                             ALU.mult)
                            S.sig(ve, f"{Q}avn")
                        def yst(yq, hg=hg, h=h, avn=S.tok(f"{Q}avn")):
                            S.wait(yq, (f"{Q}avn", avn[1] + 1))
                            S.inc(yq.dma_start(out=avh[hg, 64 * h:64 * h + 64, :],
                                               in_=av2[0:64, :]), f"d{Q}avh", 16)
                        S.on("Y", ybc)
                        S.on("V", vnr)
                        S.on("Y", yst)
                subA.close()
                avhtok = S.tok(f"d{Q}avh")
                subB = ExitStack()
                wo32 = subB.enter_context(nc.sbuf_tensor(f"a{P}{s_}_wo", [128, 1024], F32))
                woh = subB.enter_context(nc.sbuf_tensor(f"a{P}{s_}_woh", [128, 1024], BF16))
                wol = subB.enter_context(nc.sbuf_tensor(f"a{P}{s_}_wol", [128, 1024], BF16))
                avh_t = subB.enter_context(nc.sbuf_tensor(f"a{P}{s_}_avht", [128, 32 * 512], BF16))
                avl_t = subB.enter_context(nc.sbuf_tensor(f"a{P}{s_}_avlt", [128, 32 * 512], BF16))
                avt = subB.enter_context(nc.sbuf_tensor(f"a{P}{s_}_avt", [128, 2 * 512], F32))
                xr = subB.enter_context(nc.sbuf_tensor(f"a{P}{s_}_xr", [128, 2 * 512], F32))
                yo = subB.enter_context(nc.sbuf_tensor(f"a{P}{s_}_yo", [128, 2 * 512], F32))
                for hg in range(8):
                    for tt in range(4):
                        sl = (hg * 4 + tt) % 2
                        def ya(yq, hg=hg, tt=tt, sl=sl, avhtok=avhtok,
                               af=S.tok(f"{Q}avsp")):
                            S.wait(yq, avhtok)
                            if af[1] >= 2:
                                S.wait(yq, (af[0], af[1] - 1))
                            S.inc(yq.dma_start(out=avt[:, sl * 512:(sl + 1) * 512],
                                               in_=avh[hg, :, tt * 512:(tt + 1) * 512]),
                                  f"d{Q}av{sl}", 16)
                        def vsp2(ve, hg=hg, tt=tt, sl=sl):
                            S.wait(ve, S.tok(f"d{Q}av{sl}"))
                            hs = slice((hg * 4 + tt) * 512, (hg * 4 + tt + 1) * 512)
                            ve.tensor_copy(avh_t[:, hs], avt[:, sl * 512:(sl + 1) * 512])
                            ve.tensor_tensor(avl_t[:, hs], avt[:, sl * 512:(sl + 1) * 512],
                                             avh_t[:, hs], ALU.subtract)
                            S.sig(ve, f"{Q}avsp")
                        S.on("Y", ya)
                        S.on("V", vsp2)
                asptok = S.tok(f"{Q}avsp")
                for oc in range(NCH):
                    def ywo(yq, oc=oc, ow=S.tok(f"{Q}omm")):
                        S.wait(yq, ow)
                        src = bass.AP(wsrc["o"].tensor, oc * 128,
                                      [[1024, 128], [128 * 1024, 8], [1, 128]])
                        S.inc(yq.dma_start(
                            out=wo32[:].rearrange("p (k f) -> p k f", k=8), in_=src),
                            f"d{Q}wo", 16)
                    def vwo(ve):
                        S.wait(ve, S.tok(f"d{Q}wo"))
                        ve.tensor_copy(woh[:], wo32[:])
                        ve.tensor_tensor(wol[:], wo32[:], woh[:], ALU.subtract)
                        S.sig(ve, f"{Q}wosp")
                    S.on("Y", ywo)
                    S.on("V", vwo)
                    wotok = S.tok(f"{Q}wosp")
                    for tt in range(4):
                        it = oc * 4 + tt
                        pjb = pj[it % 2]
                        def t2(te, oc=oc, tt=tt, pjb=pjb, asptok=asptok, wotok=wotok,
                               of=S.tok(f"{Q}oev")):
                            S.wait(te, asptok)
                            S.wait(te, wotok)
                            if of[1] >= 2:
                                S.wait(te, (of[0], of[1] - 1))
                            for hg in range(NCH):
                                hs = slice((hg * 4 + tt) * 512, (hg * 4 + tt + 1) * 512)
                                wk = slice(hg * 128, (hg + 1) * 128)
                                te.matmul(pjb[:, 0:512], woh[:, wk], avh_t[:, hs],
                                          start=(hg == 0), stop=False)
                                te.matmul(pjb[:, 0:512], woh[:, wk], avl_t[:, hs],
                                          start=False, stop=False)
                                te.matmul(pjb[:, 0:512], wol[:, wk], avh_t[:, hs],
                                          start=False, stop=(hg == NCH - 1))
                            S.sig(te, f"{Q}omm")
                        S.on("T", t2)
                        otok = S.tok(f"{Q}omm")
                        def so(se, it=it, oc=oc, pjb=pjb, otok=otok,
                               syo=S.tok(f"d{Q}yo")):
                            S.wait(se, otok)
                            if syo[1] >= 32:
                                S.wait(se, (syo[0], syo[1] - 16))
                            dst = yo[:, (it % 2) * 512:(it % 2) * 512 + 512]
                            if L1:
                                se.activation(dst, pjb[:, 0:512], AF.Identity,
                                              bias=c_b["o"][:, oc:oc + 1])
                            else:
                                se.activation(dst, pjb[:, 0:512], AF.Identity)
                            S.sig(se, f"{Q}oev")
                        def yx(yq, it=it, oc=oc, tt=tt, t0=t0, xf=S.tok(f"{Q}yadd")):
                            if xf[1] >= 2:
                                S.wait(yq, (xf[0], xf[1] - 1))
                            S.inc(yq.dma_start(
                                out=xr[:, (it % 2) * 512:(it % 2) * 512 + 512],
                                in_=xsrc[oc, :, t0 + tt * 512:t0 + (tt + 1) * 512]),
                                f"d{Q}xr", 16)
                        def vy(ve, it=it, oevt=S.tok(f"{Q}oev"), syo=S.tok(f"d{Q}yo")):
                            S.wait(ve, (f"{Q}oev", oevt[1] + 1))
                            S.wait(ve, S.tok(f"d{Q}xr"))
                            if syo[1] >= 32:
                                S.wait(ve, (syo[0], syo[1] - 16))
                            sl2 = (it % 2) * 512
                            ve.tensor_tensor(yo[:, sl2:sl2 + 512], yo[:, sl2:sl2 + 512],
                                             xr[:, sl2:sl2 + 512], ALU.add)
                            S.sig(ve, f"{Q}yadd")
                        def ys(yq, it=it, oc=oc, tt=tt, t0=t0, ya_=S.tok(f"{Q}yadd")):
                            S.wait(yq, (f"{Q}yadd", ya_[1] + 1))
                            sl2 = (it % 2) * 512
                            S.inc(yq.dma_start(
                                out=xdst[oc, :, t0 + tt * 512:t0 + (tt + 1) * 512],
                                in_=yo[:, sl2:sl2 + 512]), f"d{Q}yo", 16)
                        S.on("S", so)
                        S.on("Y", yx)
                        S.on("V", vy)
                        S.on("Y", ys)
                subB.close()
                def yend(yq, Q=Q):
                    S.wait(yq, S.tok(f"d{Q}yo"))
                S.on("Y", yend)

        def emit_ffn1(xsrc, xdst, P, ph):
            stsub = ExitStack()
            psg = stsub.enter_context(nc.psum_tensor(f"g{P}_psg", [128, 512], F32))
            psh = stsub.enter_context(nc.psum_tensor(f"g{P}_psh", [128, 512], F32))
            _, rtok = stats_pass(xsrc, 0, NT, f"{P}st", ph, psg, psh)
            stsub.close()
            pj = [ph.enter_context(nc.psum_tensor(f"g{P}_pj{i}", [128, 512], F32))
                  for i in range(2)]
            sw = ph.enter_context(nc.sbuf_tensor(f"g{P}_sw", [128, NCH * 1024], BF16))
            swr = ph.enter_context(nc.sbuf_tensor(f"g{P}_swr", [128, 2 * 1024], F32))
            mb = ph.enter_context(nc.sbuf_tensor(f"g{P}_mb", [128, 2048], F32))
            rb = ph.enter_context(nc.sbuf_tensor(f"g{P}_rb", [128, 2048], F32))
            lnh = ph.enter_context(nc.sbuf_tensor(f"g{P}_lnh", [128, NCH * 2048], BF16))
            lnl = ph.enter_context(nc.sbuf_tensor(f"g{P}_lnl", [128, NCH * 2048], BF16))
            xt = ph.enter_context(nc.sbuf_tensor(f"g{P}_xt", [128, 2 * 2048], F32))
            st32 = ph.enter_context(nc.sbuf_tensor(f"g{P}_st", [128, 2048], F32))
            xr = ph.enter_context(nc.sbuf_tensor(f"g{P}_xr", [128, 2 * 512], F32))
            yo = ph.enter_context(nc.sbuf_tensor(f"g{P}_yo", [128, 2 * 512], F32))
            for kc in range(NCH):
                sl = kc % 2
                def yw(yq, kc=kc, sl=sl, sg=S.tok(f"{P}sg")):
                    if sg[1] >= 2:
                        S.wait(yq, (sg[0], sg[1] - 1))
                    S.inc(yq.dma_start(out=swr[:, sl * 1024:(sl + 1) * 1024],
                                       in_=wfT[kc * 128:(kc + 1) * 128, :]), f"d{P}w{sl}", 16)
                def ssg(se, kc=kc, sl=sl):
                    S.wait(se, S.tok(f"d{P}w{sl}"))
                    se.activation(sw[:, kc * 1024:(kc + 1) * 1024],
                                  swr[:, sl * 1024:(sl + 1) * 1024], AF.Sign)
                    S.sig(se, f"{P}sg")
                S.on("Y", yw)
                S.on("S", ssg)
            swtok = S.tok(f"{P}sg")
            for half in range(2):
                t0 = half * 2048
                Q = f"{P}h{half}"
                def yb(yq, t0=t0, lc=S.tok(f"{P}lnc")):
                    S.wait(yq, rtok)
                    if lc[1] > 0:
                        S.wait(yq, lc)
                    bcast_row(yq, mb[:, :], 0, t0, 2048, f"d{Q}b")
                    bcast_row(yq, rb[:, :], 1, t0, 2048, f"d{Q}b")
                S.on("Y", yb)
                btok = S.tok(f"d{Q}b")
                xslot = [None, None]
                for c in range(NCH):
                    sl = c % 2
                    ss = slice(sl * 2048, sl * 2048 + 2048)
                    def y2(yq, c=c, t0=t0, ss=ss, xs=xslot[sl]):
                        S.wait(yq, xs)
                        S.inc(yq.dma_start(out=xt[:, ss], in_=xsrc[c, :, t0:t0 + 2048]),
                              f"d{Q}x{sl}", 16)
                    def v(ve, c=c, sl=sl, ss=ss, btok=btok, mmprev=S.tok(f"{P}fmm")):
                        S.wait(ve, btok)
                        S.wait(ve, S.tok(f"d{Q}x{sl}"))
                        S.wait(ve, mmprev)
                        ve.tensor_tensor(st32[:], xt[:, ss], mb[:], ALU.subtract)
                        S.sig(ve, f"{Q}xf")
                        ve.tensor_tensor(st32[:], st32[:], rb[:], ALU.mult)
                        cs = slice(c * 2048, (c + 1) * 2048)
                        ve.tensor_copy(lnh[:, cs], st32[:])
                        ve.tensor_tensor(lnl[:, cs], st32[:], lnh[:, cs], ALU.subtract)
                        S.sig(ve, f"{P}lnc")
                    S.on("Y", y2)
                    S.on("V", v)
                    xslot[sl] = S.tok(f"{Q}xf")
                lntok = S.tok(f"{P}lnc")
                for oc in range(NCH):
                    for tq in range(4):
                        pjb = pj[tq % 2]
                        def t(te, oc=oc, tq=tq, pjb=pjb, lntok=lntok, swtok=swtok,
                              of=S.tok(f"{Q}fev")):
                            S.wait(te, lntok)
                            S.wait(te, swtok)
                            if of[1] >= 2:
                                S.wait(te, (of[0], of[1] - 1))
                            for kc in range(NCH):
                                rs = slice(kc * 2048 + tq * 512, kc * 2048 + tq * 512 + 512)
                                wk = slice(kc * 1024 + oc * 128, kc * 1024 + (oc + 1) * 128)
                                te.matmul(pjb[:, 0:512], sw[:, wk], lnh[:, rs],
                                          start=(kc == 0), stop=False)
                                te.matmul(pjb[:, 0:512], sw[:, wk], lnl[:, rs],
                                          start=False, stop=(kc == NCH - 1))
                            S.sig(te, f"{P}fmm")
                        S.on("T", t)
                        mmtok = S.tok(f"{P}fmm")
                        def yx(yq, oc=oc, tq=tq, t0=t0, xf=S.tok(f"{Q}yadd")):
                            if xf[1] >= 2:
                                S.wait(yq, (xf[0], xf[1] - 1))
                            S.inc(yq.dma_start(
                                out=xr[:, (tq % 2) * 512:(tq % 2) * 512 + 512],
                                in_=xsrc[oc, :, t0 + tq * 512:t0 + (tq + 1) * 512]),
                                f"d{Q}xr", 16)
                        def vy(ve, oc=oc, tq=tq, pjb=pjb, mmtok=mmtok,
                               yst_=S.tok(f"d{Q}yo")):
                            S.wait(ve, mmtok)
                            S.wait(ve, S.tok(f"d{Q}xr"))
                            if yst_[1] >= 32:
                                S.wait(ve, (yst_[0], yst_[1] - 16))
                            sl2 = (tq % 2) * 512
                            ve.tensor_scalar(yo[:, sl2:sl2 + 512], pjb[:, 0:512], 0.5, 0.0,
                                             ALU.mult, ALU.add)
                            S.sig(ve, f"{Q}fev")
                            ve.tensor_tensor(yo[:, sl2:sl2 + 512], yo[:, sl2:sl2 + 512],
                                             xr[:, sl2:sl2 + 512], ALU.add)
                            S.sig(ve, f"{Q}yadd")
                        def ys(yq, oc=oc, tq=tq, t0=t0, ya_=S.tok(f"{Q}yadd")):
                            S.wait(yq, (f"{Q}yadd", ya_[1] + 1))
                            sl2 = (tq % 2) * 512
                            S.inc(yq.dma_start(
                                out=xdst[oc, :, t0 + tq * 512:t0 + (tq + 1) * 512],
                                in_=yo[:, sl2:sl2 + 512]), f"d{Q}yo", 16)
                        S.on("Y", yx)
                        S.on("V", vy)
                        S.on("Y", ys)
                def yend(yq, Q=Q):
                    S.wait(yq, S.tok(f"d{Q}yo"))
                S.on("Y", yend)

        def run_phase(fn, *args):
            with ExitStack() as ph:
                fn(*args, ph)
            barrier()
            S.end_phase()

        run_phase(emit_dequant)
        seqp = [(emit_ffn0, (xin, 0, "A")), (emit_attn, (None, 0, "M")),
                (emit_ffn0, (None, 4, "B")), (emit_ffn1, (None, "C")),
                (emit_attn, (None, 1, "N")), (emit_ffn1, (None, "E"))]
        srcs = [xin, xres[1], xres[2], xres[3], xres[4], xres[5]]
        nph = min(upto, 6)
        for i in range(nph):
            fn, args = seqp[i]
            src = srcs[i]
            dst = xout if i == nph - 1 else srcs[i + 1]
            if fn is emit_ffn0:
                run_phase(fn, src, dst, args[1], args[2])
            elif fn is emit_attn:
                run_phase(fn, src, dst, args[1], args[2])
            else:
                run_phase(fn, src, dst, args[1])

        if dump_outs:
            barrier()
            def ydump(yq):
                for nm, dst in dump_outs.items():
                    S.inc(yq.dma_start(out=dst, in_=named[nm][0]), "ddump", 16)
                S.wait(yq, S.tok("ddump"))
            S.on("Y", ydump)
        barrier()

    print("instr-sections per engine:", S.ni)
    return nc



def _mask_tiles():
    r = np.arange(128)[:, None]
    c = np.arange(128)[None, :]
    vis_d = c <= r
    vis_e = c >= r + 1
    vis_es = vis_e | (c < 4)
    vis_s = np.broadcast_to(c < 4, (128, 128))
    m = np.zeros((4, 128, 128), np.float32)
    for i, vis in enumerate((vis_d, vis_e, vis_es, vis_s)):
        m[i] = np.where(vis, 0.0, NEGINF).astype(np.float32)
    return m


def _prep_inputs(inputs):
    common = {
        "w_up": np.ascontiguousarray(np.asarray(inputs["L0_ffn_w_up"], np.float32)),
        "w_gate": np.ascontiguousarray(np.asarray(inputs["L0_ffn_w_gate"], np.float32)),
        "w_down": np.ascontiguousarray(np.asarray(inputs["L0_ffn_w_down"], np.float32)),
        "l0_wqT": np.ascontiguousarray(np.asarray(inputs["L0_attn_Wq"], np.float32).T),
        "l0_wkT": np.ascontiguousarray(np.asarray(inputs["L0_attn_Wk"], np.float32).T),
        "l0_wvT": np.ascontiguousarray(np.asarray(inputs["L0_attn_Wv"], np.float32).T),
        "l0_woT": np.ascontiguousarray(np.asarray(inputs["L0_attn_Wo"], np.float32).T),
        "l1_wqT": np.ascontiguousarray(
            np.transpose(np.asarray(inputs["L1_attn_W_Q"], np.float32), (1, 0, 2)).reshape(D, D)),
        "l1_wkT": np.ascontiguousarray(
            np.transpose(np.asarray(inputs["L1_attn_W_K"], np.float32), (1, 0, 2)).reshape(D, D)),
        "l1_wvT": np.ascontiguousarray(
            np.transpose(np.asarray(inputs["L1_attn_W_V"], np.float32), (1, 0, 2)).reshape(D, D)),
        "l1_woT": np.ascontiguousarray(np.asarray(inputs["L1_attn_W_O"], np.float32)),
        "wfT": np.ascontiguousarray(np.asarray(inputs["L1_ffn_weight"], np.float32).T),
        "iden32": np.eye(128, dtype=np.float32),
        "idenb": np.eye(128, dtype=ml_dtypes.bfloat16),
        "ones32": np.ones((128, 1), np.float32),
        "masks": _mask_tiles(),
    }
    for p, key in (("q", "L1_attn_b_Q"), ("k", "L1_attn_b_K"), ("v", "L1_attn_b_V")):
        b = np.asarray(inputs[key], np.float32).reshape(16, 64)
        bb = np.zeros((128, 8), np.float32)
        for hg in range(8):
            bb[0:64, hg] = b[2 * hg]
            bb[64:128, hg] = b[2 * hg + 1]
        common[f"l1_b{p}"] = bb
    bo = np.asarray(inputs["L1_attn_b_O"], np.float32).reshape(8, 128)
    common["l1_bo"] = np.ascontiguousarray(bo.T)
    x = np.asarray(inputs["L0_attn_x"], np.float32)
    in_maps = []
    for core in range(8):
        xc = x[2 * core:2 * core + 2]
        xt = np.transpose(xc, (2, 0, 1)).reshape(D, NT)
        m = dict(common)
        m["xT"] = np.ascontiguousarray(xt.reshape(NCH, 128, NT))
        in_maps.append(m)
    return in_maps


def kernel(**inputs):
    in_maps = _prep_inputs(inputs)
    nc = build(upto=7)
    res = run_bass_kernel_spmd(nc, in_maps, core_ids=list(range(8)))
    outs = []
    for core in range(8):
        o = np.asarray(res.results[core]["outT"]).reshape(D, 2, SEQ)
        outs.append(np.transpose(o, (1, 2, 0)))
    return np.ascontiguousarray(np.concatenate(outs, axis=0)).astype(np.float32)


# version 03 (final)
# speedup vs baseline: 1.1253x; 1.0954x over previous
import numpy as np
import ml_dtypes
import concourse.bass as bass
import concourse.mybir as mybir
from concourse.bass_utils import run_bass_kernel_spmd
from contextlib import ExitStack

F32 = mybir.dt.float32
BF16 = mybir.dt.bfloat16
I16 = mybir.dt.int16
AF = mybir.ActivationFunctionType
ALU = mybir.AluOpType
AX = mybir.AxisListType

MAGIC = 12582912.0
NEGINF = -1.0e30
D = 1024
NCH = 8
NT = 4096
SEQ = 2048
HID = 3072
NHC = 24
QMAXA = 32767.0
EPS_LN = 1e-5
EPS_Q = 1e-12

ENG_ATTR = {"T": "tensor", "V": "vector", "S": "scalar", "G": "gpsimd", "Y": "sync"}


class Eng:

    def __init__(self, raw, w):
        self._raw = raw
        self._w = w
        self.last = None

    def __getattr__(self, name):
        v = getattr(self._raw, name)
        if not callable(v):
            return v
        def call(*a, **kw):
            r = v(*a, **kw)
            if hasattr(r, "then_inc"):
                self.last = r
            return r
        return call


class Sched:

    def __init__(self, nc, block, es):
        self.nc, self.block, self.es = nc, block, es
        self.cnt = {}
        self.hnd = {}
        self.real_cnt = {}
        self.free_pool = []
        self.free_pool_dma = []
        self.free_pool_dg = []
        self.persist = set()
        self.phase_names = []
        self.waited = {}
        self.ni = {k: 0 for k in ENG_ATTR}

    def sem(self, name):
        if name not in self.hnd:
            if name.startswith("dG"):
                pool = self.free_pool_dg
            elif name.startswith("d"):
                pool = self.free_pool_dma
            else:
                pool = self.free_pool
            if pool:
                h = pool.pop()
            else:
                h = self.es.enter_context(self.nc.semaphore(f"sm{len(self.real_cnt)}"))
                self.real_cnt[id(h)] = 0
            self.hnd[name] = h
            self.cnt[name] = self.real_cnt[id(h)]
            if self.phase_names is not None:
                self.phase_names.append(name)
        return self.hnd[name]

    def persist_name(self, name):
        self.sem(name)
        self.persist.add(name)

    def end_phase(self):
        for name in self.phase_names:
            if name in self.persist or name not in self.hnd:
                continue
            h = self.hnd.pop(name)
            self.real_cnt[id(h)] = self.cnt.pop(name)
            if name.startswith("dG"):
                self.free_pool_dg.append(h)
            elif name.startswith("d"):
                self.free_pool_dma.append(h)
            else:
                self.free_pool.append(h)
            self.waited = {k: v for k, v in self.waited.items() if k[1] != name}
        self.phase_names = []

    def inc(self, instr, name, k=1):
        h = self.sem(name)
        instr.then_inc(h, k)
        self.cnt[name] += k
        self.real_cnt[id(h)] = self.cnt[name]
        return (name, self.cnt[name])

    def sig(self, e, name, k=1):
        assert e.last is not None, f"sig({name}) with no prior instruction"
        r = self.inc(e.last, name, k)
        e.last = None
        return r

    def fence(self, e):
        name = "fn" + e._w
        assert e.last is not None
        tok = self.inc(e.last, name)
        e.last = None
        e._raw.wait_ge(self.hnd[name], tok[1])
        self.waited[(e._w, name)] = tok[1]
        return tok

    def bump(self, e, name, k=1):
        h = self.sem(name)
        e._raw.sem_inc(h, k)
        self.cnt[name] += k
        self.real_cnt[id(h)] = self.cnt[name]
        return (name, self.cnt[name])

    def tok(self, name):
        self.sem(name)
        return (name, self.cnt[name])

    def wait(self, e, tok):
        if tok is None:
            return
        name, val = tok
        if val <= 0:
            return
        key = (e._w, name)
        if self.waited.get(key, -1) >= val:
            return
        e._raw.wait_ge(self.hnd[name], val)
        self.waited[key] = val

    def on(self, w, f):
        self.ni[w] += 1
        getattr(self.block, ENG_ATTR[w])(lambda raw: f(Eng(raw, w)))


def build(upto=7, dumps=(), stage=99):
    nc = bass.Bass("TRN2", target_bir_lowering=False, detect_race_conditions=False)

    xin = nc.dram_tensor("xT", [NCH, 128, NT], F32, kind="ExternalInput").ap()
    w_up_raw = nc.dram_tensor("w_up", [HID, D], F32, kind="ExternalInput").ap()
    w_gate_raw = nc.dram_tensor("w_gate", [HID, D], F32, kind="ExternalInput").ap()
    w_down_raw = nc.dram_tensor("w_down", [D, HID], F32, kind="ExternalInput").ap()
    l0_w = {p: nc.dram_tensor(f"l0_w{p}T", [D, D], F32, kind="ExternalInput").ap()
            for p in ("q", "k", "v", "o")}
    l1_w = {p: nc.dram_tensor(f"l1_w{p}T", [D, D], F32, kind="ExternalInput").ap()
            for p in ("q", "k", "v", "o")}
    l1_b = {p: nc.dram_tensor(f"l1_b{p}", [128, 8], F32, kind="ExternalInput").ap()
            for p in ("q", "k", "v", "o")}
    wfT = nc.dram_tensor("wfT", [D, D], F32, kind="ExternalInput").ap()
    iden32 = nc.dram_tensor("iden32", [128, 128], F32, kind="ExternalInput").ap()
    idenb = nc.dram_tensor("idenb", [128, 128], BF16, kind="ExternalInput").ap()
    ones32 = nc.dram_tensor("ones32", [128, 1], F32, kind="ExternalInput").ap()
    masks = nc.dram_tensor("masks", [4, 128, 128], F32, kind="ExternalInput").ap()

    xout = nc.dram_tensor("outT", [NCH, 128, NT], F32, kind="ExternalOutput").ap()

    xres = {i: nc.dram_tensor(f"x{i}", [NCH, 128, NT], F32).ap() for i in range(1, 6)}
    pk_ug = nc.dram_tensor("pk_ug", [2, 2, NHC, 128, 8 * 128], BF16).ap()
    pk_d = nc.dram_tensor("pk_d", [2, NCH, 128, 24 * 128], BF16).ap()
    upraw = nc.dram_tensor("upraw", [NHC, 128, NT], F32).ap()
    gateraw = nc.dram_tensor("gateraw", [NHC, 128, NT], F32).ap()
    m2i = nc.dram_tensor("m2i", [NHC, 128, NT], I16).ap()
    prodb = nc.dram_tensor("prodb", [NHC, 128, NT], F32).ap()
    mmb = nc.dram_tensor("mmb", [NHC, 128, NT], F32).ap()
    rows = nc.dram_tensor("rows", [8, NT], F32).ap()
    dbg_sc = nc.dram_tensor("dbg_sc", [128, 16], F32).ap()
    dbg_mx = nc.dram_tensor("dbg_mx", [128, 384], F32).ap()
    dbg2 = nc.dram_tensor("dbg2", [128, 8], F32).ap()
    avh = nc.dram_tensor("avh", [8, 128, SEQ], F32).ap()
    cc_in = [nc.dram_tensor(f"cc_in{i}", [1, 2], F32) for i in range(8)]
    cc_out = [nc.dram_tensor(f"cc_out{i}", [1, 2], F32) for i in range(8)]

    named = {"x1": (xres[1], [NCH, 128, NT], F32), "x2": (xres[2], [NCH, 128, NT], F32),
             "x3": (xres[3], [NCH, 128, NT], F32), "x4": (xres[4], [NCH, 128, NT], F32),
             "x5": (xres[5], [NCH, 128, NT], F32),
             "pk_ug": (pk_ug, [2, 2, NHC, 128, 8 * 128], BF16),
             "pk_d": (pk_d, [2, NCH, 128, 24 * 128], BF16),
             "upraw": (upraw, [NHC, 128, NT], F32),
             "gateraw": (gateraw, [NHC, 128, NT], F32),
             "prodb": (prodb, [NHC, 128, NT], F32),
             "mmb": (mmb, [NHC, 128, NT], F32),
             "rows": (rows, [8, NT], F32),
             "dbg_sc": (dbg_sc, [128, 16], F32),
             "dbg_mx": (dbg_mx, [128, 384], F32),
             "dbg2": (dbg2, [128, 8], F32),
             "m2i": (m2i, [NHC, 128, NT], I16),
             "avh": (avh, [8, 128, SEQ], F32)}
    dump_outs = {nm: nc.dram_tensor("dump_" + nm, named[nm][1], named[nm][2],
                                    kind="ExternalOutput").ap() for nm in dumps}

    with ExitStack() as es:
        block = es.enter_context(nc.Block())
        S = Sched(nc, block, es)
        for w in ENG_ATTR:
            S.persist_name("bar" + w)
            S.persist_name("fn" + w)
        S.persist_name("dconst")

        c_iden = es.enter_context(nc.sbuf_tensor("c_iden", [128, 128], F32))
        c_idenb = es.enter_context(nc.sbuf_tensor("c_idenb", [128, 128], BF16))
        c_ones = es.enter_context(nc.sbuf_tensor("c_ones", [128, 1], F32))
        c_masks = es.enter_context(nc.sbuf_tensor("c_masks", [128, 4 * 128], F32))
        c_b = {p: es.enter_context(nc.sbuf_tensor(f"c_b{p}", [128, 8], F32))
               for p in ("q", "k", "v", "o")}
        c_sc = es.enter_context(nc.sbuf_tensor("c_sc", [128, 16], F32))
        rb1 = es.enter_context(nc.sbuf_tensor("rb1", [1, 2 * 512], F32))
        rb2 = es.enter_context(nc.sbuf_tensor("rb2", [1, 2 * 512], F32))
        rb3 = es.enter_context(nc.sbuf_tensor("rb3", [1, 2 * 512], F32))
        r32 = es.enter_context(nc.sbuf_tensor("r32", [32, 4 * 128], F32))
        s11 = es.enter_context(nc.sbuf_tensor("s11", [1, 2], F32))

        def y0(yq):
            for dst, src in [(c_iden, iden32), (c_idenb, idenb), (c_ones, ones32)]:
                S.inc(yq.dma_start(out=dst[:], in_=src), "dconst", 16)
            msrc = bass.AP(masks.tensor, 0, [[128, 128], [128 * 128, 4], [1, 128]])
            S.inc(yq.dma_start(out=c_masks[:].rearrange("p (m f) -> p m f", m=4),
                               in_=msrc), "dconst", 16)
            for p in ("q", "k", "v", "o"):
                S.inc(yq.dma_start(out=c_b[p][:], in_=l1_b[p]), "dconst", 16)
        S.on("Y", y0)
        tok_consts = S.tok("dconst")

        def barrier():
            for w in ENG_ATTR:
                S.on(w, (lambda w: lambda e: S.bump(e, "bar" + w))(w))
            toks = [S.tok("bar" + w) for w in ENG_ATTR]
            for w in ENG_ATTR:
                def body(e, w=w):
                    for tk in toks:
                        S.wait(e, tk)
                S.on(w, body)

        def par_max(vec_ap, out11, in_tok, tag, psb):
            def t(te):
                S.wait(te, in_tok)
                te.transpose(psb[0:1, 0:128], vec_ap, c_iden[:])
                S.sig(te, "pmt" + tag)
            def v(ve):
                S.wait(ve, S.tok("pmt" + tag))
                ve.tensor_reduce(out11, psb[0:1, 0:128], AX.X, ALU.max)
                S.sig(ve, "pmv" + tag)
            S.on("T", t)
            S.on("V", v)
            return S.tok("pmv" + tag)

        def collective_max(rnd, srcs, dst_tile, ncols, in_tok, tag):
            def g(ge):
                S.wait(ge, in_tok)
                for i in range(2):
                    s_ = srcs[min(i, len(srcs) - 1)]
                    S.inc(ge.dma_start(out=cc_in[rnd][0:1, i:i + 1], in_=s_), "dG", 16)
                S.wait(ge, S.tok("dG"))
                S.inc(ge.collective_compute(
                    "AllReduce", ALU.max, replica_groups=[list(range(8))],
                    ins=[cc_in[rnd].ap().opt()], outs=[cc_out[rnd].ap().opt()]), "cc")
                S.wait(ge, S.tok("cc"))
                rep = bass.AP(cc_out[rnd].ap().tensor, 0, [[0, 128], [1, ncols]])
                S.inc(ge.dma_start(out=dst_tile, in_=rep), "dG", 16)
                S.wait(ge, S.tok("dG"))
                S.bump(ge, "ccb" + tag)
            S.on("G", g)
            return S.tok("ccb" + tag)

        def stats_pass(xsrc, t0, T, tag, ph, psg, psh):
            xa = ph.enter_context(nc.sbuf_tensor(f"st_x_{tag}", [128, 2 * 512], F32))
            sq = ph.enter_context(nc.sbuf_tensor(f"st_q_{tag}", [128, 2 * 512], F32))
            nblk = T // 512
            slot_tok = [None, None]
            for blk in range(nblk):
                tb = t0 + blk * 512
                bsl = blk % 2
                bss = slice(bsl * 512, bsl * 512 + 512)
                for c in range(NCH):
                    sl = (blk * NCH + c) % 2
                    ss = slice(sl * 512, sl * 512 + 512)
                    def y(yq, c=c, tb=tb, sl=sl, ss=ss, st_=slot_tok[sl]):
                        S.wait(yq, st_)
                        S.inc(yq.dma_start(out=xa[:, ss], in_=xsrc[c, :, tb:tb + 512]),
                              f"d{tag}x{sl}", 16)
                    def s_(se, sl=sl, ss=ss, st_=slot_tok[sl]):
                        S.wait(se, S.tok(f"d{tag}x{sl}"))
                        S.wait(se, st_)
                        se.activation(sq[:, ss], xa[:, ss], AF.Square)
                        S.sig(se, f"{tag}sq")
                    def t(te, c=c, ss=ss):
                        S.wait(te, S.tok(f"{tag}sq"))
                        if c == 0:
                            S.wait(te, S.tok(f"{tag}ev"))
                        te.matmul(psg[0:1, 0:512], c_ones[:], xa[:, ss],
                                  start=(c == 0), stop=(c == NCH - 1))
                        te.matmul(psh[0:1, 0:512], c_ones[:], sq[:, ss],
                                  start=(c == 0), stop=(c == NCH - 1))
                        S.sig(te, f"{tag}tc")
                    S.on("Y", y)
                    S.on("S", s_)
                    S.on("T", t)
                    slot_tok[sl] = S.tok(f"{tag}tc")
                def v(ve, bss=bss, bsl=bsl):
                    S.wait(ve, S.tok(f"{tag}tc"))
                    S.wait(ve, S.tok(f"d{tag}mr{bsl}"))
                    ve.tensor_scalar(rb1[0:1, bss], psg[0:1, 0:512], 1.0 / D, 0.0,
                                     ALU.mult, ALU.add)
                    ve.tensor_scalar(rb3[0:1, bss], psh[0:1, 0:512], 1.0 / D, 0.0,
                                     ALU.mult, ALU.add)
                    S.sig(ve, f"{tag}ev")
                    ve.tensor_tensor(rb2[0:1, bss], rb1[0:1, bss], rb1[0:1, bss],
                                     ALU.mult)
                    ve.tensor_tensor(rb3[0:1, bss], rb3[0:1, bss], rb2[0:1, bss],
                                     ALU.subtract)
                    ve.tensor_scalar(rb3[0:1, bss], rb3[0:1, bss], EPS_LN, 0.0,
                                     ALU.add, ALU.add)
                    S.sig(ve, f"{tag}vv")
                def s2_(se, bss=bss):
                    S.wait(se, S.tok(f"{tag}vv"))
                    se.activation(rb2[0:1, bss], rb3[0:1, bss], AF.Sqrt)
                    S.sig(se, f"{tag}sr")
                def v3(ve, bss=bss):
                    S.wait(ve, S.tok(f"{tag}sr"))
                    ve.reciprocal(rb3[0:1, bss], rb2[0:1, bss])
                    S.fence(ve)
                    ve.tensor_copy(rb2[0:1, bss.start:bss.start + 1],
                                   rb3[0:1, bss.start:bss.start + 1])
                    S.sig(ve, f"{tag}rr")
                def y2(yq, tb=tb, bss=bss, bsl=bsl):
                    S.wait(yq, S.tok(f"{tag}rr"))
                    S.inc(yq.dma_start(out=rows[0:1, tb:tb + 512], in_=rb1[0:1, bss]),
                          f"d{tag}mr{bsl}", 16)
                    S.inc(yq.dma_start(out=rows[1:2, tb:tb + 512], in_=rb3[0:1, bss]),
                          f"d{tag}mr{bsl}", 16)
                S.on("V", v)
                S.on("S", s2_)
                S.on("V", v3)
                S.on("Y", y2)
            def yfin(yq):
                S.wait(yq, S.tok(f"d{tag}mr0"))
                S.wait(yq, S.tok(f"d{tag}mr1"))
                S.bump(yq, f"{tag}done")
            S.on("Y", yfin)
            rtok = S.tok(f"{tag}done")
            return rtok, rtok

        def bcast_row(yq, dst, row_idx, t0, T, semname):
            src = bass.AP(rows.tensor, row_idx * NT + t0, [[0, 128], [1, T]])
            return S.inc(yq.dma_start(out=dst, in_=src), semname, 16)

        def emit_dequant(ph):
            PSD = {nm: ph.enter_context(nc.psum_tensor(f"psd_{nm}", [128, 512], F32))
                   for nm in "ab"}
            swr = ph.enter_context(nc.sbuf_tensor("dq_wr", [128, HID], F32))
            sbm = ph.enter_context(nc.sbuf_tensor("dq_bm", [128, 96], F32))
            sinv = ph.enter_context(nc.sbuf_tensor("dq_inv", [128, 96], F32))
            st_ = ph.enter_context(nc.sbuf_tensor("dq_t", [128, HID], F32))
            shi = ph.enter_context(nc.sbuf_tensor("dq_hi", [128, 2 * 512], BF16))
            slo = ph.enter_context(nc.sbuf_tensor("dq_lo", [128, 2 * 512], BF16))
            jobs = [(w_up_raw, D, NHC, ("ug", 0)), (w_gate_raw, D, NHC, ("ug", 1)),
                    (w_down_raw, HID, NCH, ("d", 0))]
            tr_done = None
            v_done = None
            for raw, K, OC, dest in jobs:
                nb = K // 32
                nkc = K // 128
                for oc in range(OC):
                    def y(yq, raw=raw, oc=oc, K=K, v_done=v_done):
                        S.wait(yq, v_done)
                        S.inc(yq.dma_start(out=swr[:, 0:K],
                                           in_=raw[oc * 128:(oc + 1) * 128, :]), "dqx", 16)
                    def v(ve, K=K, nb=nb, tr_done=tr_done, wprev=v_done):
                        S.wait(ve, S.tok("dqx"))
                        S.wait(ve, tr_done)
                        S.wait(ve, wprev)
                        ve.tensor_reduce(sbm[:, 0:nb],
                                         swr[:, 0:K].rearrange("p (b k) -> p b k", k=32),
                                         AX.X, ALU.max, apply_absolute_value=True)
                        S.fence(ve)
                        ve.tensor_scalar(sbm[:, 0:nb], sbm[:, 0:nb], 1.0 / 7.0, EPS_Q,
                                         ALU.mult, ALU.max)
                        S.fence(ve)
                        ve.reciprocal(sinv[:, 0:nb], sbm[:, 0:nb])
                        S.fence(ve)
                        for b in range(nb):
                            ve.tensor_scalar(st_[:, b * 32:(b + 1) * 32],
                                             swr[:, b * 32:(b + 1) * 32],
                                             sinv[:, b:b + 1], MAGIC, ALU.mult, ALU.add)
                        S.fence(ve)
                        ve.tensor_scalar(st_[:, 0:K], st_[:, 0:K], MAGIC, 0.0,
                                         ALU.subtract, ALU.add)
                        for b in range(nb):
                            ve.tensor_scalar(st_[:, b * 32:(b + 1) * 32],
                                             st_[:, b * 32:(b + 1) * 32],
                                             sbm[:, b:b + 1], 0.0, ALU.mult, ALU.add)
                        S.fence(ve)
                        ve.tensor_copy(sbm[:, 0:1], sbm[:, 0:1])
                        S.sig(ve, "zqw")
                    S.on("Y", y)
                    S.on("V", v)
                    v_done = S.tok("zqw")
                    ng4 = (nkc + 3) // 4
                    for g4 in range(ng4):
                        ntile = min(4, nkc - g4 * 4)
                        pb = "ab"[g4 % 2]
                        off = 0 if pb == "a" else 512
                        def t(te, g4=g4, ntile=ntile, pb=pb, v_done=v_done):
                            S.wait(te, v_done)
                            S.wait(te, S.tok(f"zqpf{pb}"))
                            for i in range(ntile):
                                kc = g4 * 4 + i
                                te.transpose(PSD[pb][:, i * 128:(i + 1) * 128],
                                             st_[:, kc * 128:(kc + 1) * 128], c_iden[:])
                            S.sig(te, "zqt")
                        def sv(se, ntile=ntile, pb=pb, off=off, tcur=None):
                            S.wait(se, S.tok("zqt"))
                            S.wait(se, S.tok(f"zqhf{pb}"))
                            se.activation(shi[:, off:off + ntile * 128],
                                          PSD[pb][:, 0:ntile * 128], AF.Identity)
                            S.sig(se, f"zqh{pb}")
                        def vv(ve, ntile=ntile, pb=pb, off=off):
                            S.wait(ve, S.tok(f"zqh{pb}"))
                            ve.tensor_tensor(slo[:, off:off + ntile * 128],
                                             PSD[pb][:, 0:ntile * 128],
                                             shi[:, off:off + ntile * 128], ALU.subtract)
                            S.sig(ve, f"zql{pb}")
                        def g2(ge, g4=g4, ntile=ntile, pb=pb, off=off, dest=dest, oc=oc):
                            opb = "b" if pb == "a" else "a"
                            prev = S.tok(f"dGpk{opb}")
                            S.wait(ge, S.tok(f"zql{pb}"))
                            S.wait(ge, prev)
                            S.bump(ge, f"zqhf{opb}")
                            S.bump(ge, f"zqpf{opb}")
                            for i in range(ntile):
                                kc = g4 * 4 + i
                                for half, srcbuf in ((0, shi), (1, slo)):
                                    if dest[0] == "ug":
                                        base = (((dest[1] * 2 + half) * NHC + oc) * 128) * (8 * 128)
                                        dap = bass.AP(pk_ug.tensor, base + kc * 128,
                                                      [[8 * 128, 128], [1, 128]])
                                    else:
                                        base = ((half * NCH + oc) * 128) * (24 * 128)
                                        dap = bass.AP(pk_d.tensor, base + kc * 128,
                                                      [[24 * 128, 128], [1, 128]])
                                    S.inc(ge.dma_start(
                                        out=dap, in_=srcbuf[:, off + i * 128:off + (i + 1) * 128]),
                                        f"dGpk{pb}", 16)
                        S.on("T", t)
                        S.on("S", sv)
                        S.on("V", vv)
                        S.on("G", g2)
                    tr_done = S.tok("zqt")
            def gend(ge):
                S.wait(ge, S.tok("dGpka"))
                S.wait(ge, S.tok("dGpkb"))
            S.on("G", gend)

        def emit_ffn0(xsrc, xdst, rbase, P, ph):
            mb = ph.enter_context(nc.sbuf_tensor(f"f{P}_mb", [128, 2048], F32))
            rb = ph.enter_context(nc.sbuf_tensor(f"f{P}_rb", [128, 2048], F32))
            xt = ph.enter_context(nc.sbuf_tensor(f"f{P}_xt", [128, 2 * 2048], F32))
            t1 = ph.enter_context(nc.sbuf_tensor(f"f{P}_t1", [128, 2048], F32))
            mxs = ph.enter_context(nc.sbuf_tensor(f"f{P}_mx", [128, 18], F32))
            mxs2 = ph.enter_context(nc.sbuf_tensor(f"f{P}_mx2", [128, 384], F32))
            umx = ph.enter_context(nc.sbuf_tensor(f"f{P}_umx", [128, 2], F32))
            gv = ph.enter_context(nc.sbuf_tensor(f"f{P}_gv", [128, 2], F32))

            stsub = ExitStack()
            psg = stsub.enter_context(nc.psum_tensor(f"f{P}_psg", [128, 512], F32))
            psh = stsub.enter_context(nc.psum_tensor(f"f{P}_psh", [128, 512], F32))
            _, rtok = stats_pass(xsrc, 0, NT, f"{P}st", ph, psg, psh)
            stsub.close()
            PSF = {nm: ph.enter_context(nc.psum_tensor(f"f{P}_psb_{nm}", [128, 512], F32))
                   for nm in "abcdefgh"}
            if stage < 2:
                return

            tg = f"{P}L"
            xslot = [None, None]
            for half in range(2):
                tb = half * 2048
                bused = S.tok(f"{tg}v")
                def y(yq, tb=tb, bused=bused):
                    S.wait(yq, rtok)
                    S.wait(yq, bused)
                    bcast_row(yq, mb[:, :], 0, tb, 2048, f"d{tg}b")
                    bcast_row(yq, rb[:, :], 1, tb, 2048, f"d{tg}b")
                S.on("Y", y)
                btok = S.tok(f"d{tg}b")
                for c in range(NCH):
                    sl = c % 2
                    ss = slice(sl * 2048, sl * 2048 + 2048)
                    def y2(yq, c=c, tb=tb, sl=sl, ss=ss, xs=xslot[sl]):
                        S.wait(yq, xs)
                        S.inc(yq.dma_start(out=xt[:, ss], in_=xsrc[c, :, tb:tb + 2048]),
                              f"d{tg}x{sl}", 16)
                    def v(ve, c=c, half=half, sl=sl, ss=ss, btok=btok):
                        S.wait(ve, btok)
                        S.wait(ve, S.tok(f"d{tg}x{sl}"))
                        ve.tensor_tensor(t1[:], xt[:, ss], mb[:], ALU.subtract)
                        S.sig(ve, f"{tg}xf")
                        ve.tensor_tensor(t1[:], t1[:], rb[:], ALU.mult)
                        ve.tensor_reduce(mxs[:, half * NCH + c:half * NCH + c + 1],
                                         t1[:], AX.X, ALU.max, apply_absolute_value=True)
                        S.sig(ve, f"{tg}v")
                    S.on("Y", y2)
                    S.on("V", v)
                    xslot[sl] = S.tok(f"{tg}xf")
            def vm(ve):
                S.wait(ve, S.tok(f"{tg}v"))
                ve.tensor_reduce(mxs[:, 16:17], mxs[:, 0:16], AX.X, ALU.max)
                S.sig(ve, f"{tg}m")
            S.on("V", vm)
            pt = par_max(mxs[:, 16:17], s11[0:1, 0:1], S.tok(f"{tg}m"), tg, PSF["g"])
            ct1 = collective_max(rbase + 0, [s11[0:1, 0:1]], gv[:, 0:1], 1, pt, tg)
            def ya0(yq):
                S.wait(yq, ct1)
                mr_src = bass.AP(rows.tensor, 0 * NT, [[128, 32], [1, 128]])
                rr_src = bass.AP(rows.tensor, 1 * NT, [[128, 32], [1, 128]])
                S.inc(yq.dma_start(out=r32[0:32, 0:128], in_=mr_src), f"d{tg}mr32", 16)
                S.inc(yq.dma_start(out=r32[0:32, 128:256], in_=rr_src), f"d{tg}mr32", 16)
            def va(ve):
                S.wait(ve, ct1)
                S.wait(ve, S.tok(f"d{tg}mr32"))
                ve.tensor_scalar(c_sc[:, 0:1], gv[:, 0:1], 1.0 / QMAXA, EPS_Q,
                                 ALU.mult, ALU.max)
                S.fence(ve)
                ve.reciprocal(c_sc[:, 1:2], c_sc[:, 0:1])
                S.fence(ve)
                ve.tensor_scalar(r32[0:32, 256:384], r32[0:32, 128:256],
                                 c_sc[0:32, 1:2], 0.0, ALU.mult, ALU.add)
                S.fence(ve)
                ve.tensor_tensor(r32[0:32, 384:512], r32[0:32, 0:128],
                                 r32[0:32, 256:384], ALU.mult)
                S.sig(ve, f"{tg}ab")
            def ya(yq):
                S.wait(yq, S.tok(f"{tg}ab"))
                a_dst = bass.AP(rows.tensor, 2 * NT, [[128, 32], [1, 128]])
                b_dst = bass.AP(rows.tensor, 3 * NT, [[128, 32], [1, 128]])
                S.inc(yq.dma_start(out=a_dst, in_=r32[0:32, 256:384]), f"d{tg}ab", 16)
                S.inc(yq.dma_start(out=b_dst, in_=r32[0:32, 384:512]), f"d{tg}ab", 16)
            S.on("Y", ya0)
            S.on("V", va)
            S.on("Y", ya)
            abtok = S.tok(f"d{tg}ab")

            if stage < 3:
                return
            qt = f"{P}Q"
            ph1 = ExitStack()
            kh = ph1.enter_context(nc.sbuf_tensor(f"f{P}_kh", [128, NCH * 2048], BF16))
            kl = ph1.enter_context(nc.sbuf_tensor(f"f{P}_kl", [128, NCH * 2048], BF16))
            wsl = [ph1.enter_context(nc.sbuf_tensor(f"f{P}_w{i}", [128, 1024], BF16))
                   for i in range(8)]
            evb = [ph1.enter_context(nc.sbuf_tensor(f"f{P}_ev{i}", [128, 512], F32))
                   for i in range(4)]
            UPB, GAB = "abcd", "efgh"
            xslot2 = [None, None]
            wslot_free = [None, None]
            bank_free = {b: None for b in "abcdefgh"}
            ev_free = [None, None]
            mm_half_tok = None
            for half in range(2):
                tb = half * 2048
                bused = S.tok(f"{qt}k")
                def y(yq, tb=tb, bused=bused):
                    S.wait(yq, abtok)
                    S.wait(yq, bused)
                    bcast_row(yq, mb[:, :], 2, tb, 2048, f"d{qt}b")
                    bcast_row(yq, rb[:, :], 3, tb, 2048, f"d{qt}b")
                S.on("Y", y)
                btok = S.tok(f"d{qt}b")
                for c in range(NCH):
                    sl = c % 2
                    ss = slice(sl * 2048, sl * 2048 + 2048)
                    def y2(yq, c=c, tb=tb, sl=sl, ss=ss, xs=xslot2[sl]):
                        S.wait(yq, xs)
                        S.inc(yq.dma_start(out=xt[:, ss],
                                           in_=xsrc[c, :, tb:tb + 2048]), f"d{qt}x{sl}", 16)
                    def v(ve, c=c, sl=sl, ss=ss, half=half, btok=btok,
                          mht=mm_half_tok):
                        S.wait(ve, btok)
                        S.wait(ve, S.tok(f"d{qt}x{sl}"))
                        if half == 1:
                            S.wait(ve, mht)
                        ve.tensor_tensor(t1[:], xt[:, ss], mb[:], ALU.mult)
                        S.sig(ve, f"{qt}xf")
                        ve.tensor_tensor(t1[:], t1[:], rb[:], ALU.subtract)
                        ve.tensor_scalar(t1[:], t1[:], MAGIC, MAGIC, ALU.add, ALU.subtract)
                        cs = slice(c * 2048, (c + 1) * 2048)
                        ve.tensor_copy(kh[:, cs], t1[:])
                        ve.tensor_tensor(kl[:, cs], t1[:], kh[:, cs], ALU.subtract)
                        S.sig(ve, f"{qt}k")
                    S.on("Y", y2)
                    S.on("V", v)
                    xslot2[sl] = S.tok(f"{qt}xf")
                ktok = S.tok(f"{qt}k")
                for hc in range(NHC):
                    ws = hc % 2
                    def yw(yq, hc=hc, ws=ws, wf=wslot_free[ws]):
                        S.wait(yq, wf)
                        for j, (var, hl) in enumerate(((0, 0), (0, 1), (1, 0), (1, 1))):
                            S.inc(yq.dma_start(out=wsl[ws * 4 + j][:],
                                               in_=pk_ug[var, hl, hc]), f"d{qt}w{ws}", 16)
                    S.on("Y", yw)
                    wtok = S.tok(f"d{qt}w{ws}")
                    for tt in range(4):
                        pu, pg = UPB[tt], GAB[tt]
                        def t(te, hc=hc, tt=tt, pu=pu, pg=pg, ws=ws, ktok=ktok, wtok=wtok,
                              bfu=bank_free[pu], bfg=bank_free[pg]):
                            S.wait(te, ktok)
                            S.wait(te, wtok)
                            S.wait(te, bfu)
                            S.wait(te, bfg)
                            for var, pb in ((0, pu), (1, pg)):
                                whi = wsl[ws * 4 + var * 2]
                                wlo = wsl[ws * 4 + var * 2 + 1]
                                for kc in range(NCH):
                                    rs = slice(kc * 2048 + tt * 512, kc * 2048 + tt * 512 + 512)
                                    wk = slice(kc * 128, (kc + 1) * 128)
                                    te.matmul(PSF[pb][:, 0:512], whi[:, wk], kh[:, rs],
                                              start=(kc == 0), stop=False)
                                    te.matmul(PSF[pb][:, 0:512], whi[:, wk], kl[:, rs],
                                              start=False, stop=False)
                                    te.matmul(PSF[pb][:, 0:512], wlo[:, wk], kh[:, rs],
                                              start=False, stop=(kc == NCH - 1))
                            S.sig(te, f"{qt}mm")
                        S.on("T", t)
                        mmtok = S.tok(f"{qt}mm")
                        idx = (half * NHC + hc) * 4 + tt
                        def v2(ve, tt=tt, pu=pu, pg=pg, mmtok=mmtok, ef=ev_free[tt % 2],
                               idx=idx):
                            S.wait(ve, mmtok)
                            S.wait(ve, ef)
                            e0, e1 = evb[tt % 2 * 2], evb[tt % 2 * 2 + 1]
                            ve.tensor_scalar(e0[:], PSF[pu][:, 0:512], c_sc[:, 0:1], 0.0,
                                             ALU.mult, ALU.add)
                            S.sig(ve, f"{qt}ec")
                            ve.tensor_scalar(e1[:], PSF[pg][:, 0:512], c_sc[:, 0:1], 0.0,
                                             ALU.mult, ALU.add)
                            S.sig(ve, f"{qt}ec")
                            ve.tensor_reduce(mxs2[:, idx:idx + 1], PSF[pu][:, 0:512], AX.X,
                                             ALU.max, apply_absolute_value=True)
                            S.sig(ve, f"{qt}pv")
                            ve.tensor_reduce(mxs2[:, 192 + idx:193 + idx], PSF[pg][:, 0:512],
                                             AX.X, ALU.max, apply_absolute_value=True)
                            S.sig(ve, f"{qt}pv")
                        S.on("V", v2)
                        pvtok = S.tok(f"{qt}pv")
                        bank_free[pu] = pvtok
                        bank_free[pg] = pvtok
                        ectok = S.tok(f"{qt}ec")
                        def y3(yq, hc=hc, tt=tt, half=half, ectok=ectok):
                            S.wait(yq, ectok)
                            e0, e1 = evb[tt % 2 * 2], evb[tt % 2 * 2 + 1]
                            off = half * 2048 + tt * 512
                            S.inc(yq.dma_start(out=upraw[hc, :, off:off + 512],
                                               in_=e0[:]), f"d{qt}s{tt % 2}", 16)
                            S.inc(yq.dma_start(out=gateraw[hc, :, off:off + 512],
                                               in_=e1[:]), f"d{qt}s{tt % 2}", 16)
                        S.on("Y", y3)
                        ev_free[tt % 2] = S.tok(f"d{qt}s{tt % 2}")
                    wslot_free[ws] = S.tok(f"{qt}mm")
                mm_half_tok = S.tok(f"{qt}mm")
            if stage < 4:
                return
            def yrs(yq):
                S.wait(yq, S.tok(f"d{qt}s0"))
                S.wait(yq, S.tok(f"d{qt}s1"))
                S.bump(yq, f"{qt}rawdone")
            S.on("Y", yrs)
            rawtok = S.tok(f"{qt}rawdone")
            allpv = S.tok(f"{qt}pv")
            def vfin(ve):
                S.wait(ve, allpv)
                ve.tensor_reduce(umx[:, 0:1], mxs2[:, 0:192], AX.X, ALU.max)
                S.sig(ve, f"{qt}fin")
                ve.tensor_reduce(umx[:, 1:2], mxs2[:, 192:384], AX.X, ALU.max)
                S.sig(ve, f"{qt}fin")
            S.on("V", vfin)
            fintok = S.tok(f"{qt}fin")
            def tq(te):
                S.wait(te, fintok)
                te.transpose(PSF["a"][0:1, 0:128], umx[:, 0:1], c_iden[:])
                te.transpose(PSF["a"][0:1, 128:256], umx[:, 1:2], c_iden[:])
                S.sig(te, f"{qt}tp")
            def vq2(ve):
                S.wait(ve, S.tok(f"{qt}tp"))
                ve.tensor_reduce(s11[0:1, 0:1], PSF["a"][0:1, 0:128], AX.X, ALU.max)
                S.sig(ve, f"{qt}pm")
                ve.tensor_reduce(s11[0:1, 1:2], PSF["a"][0:1, 128:256], AX.X, ALU.max)
                S.sig(ve, f"{qt}pm")
            S.on("T", tq)
            S.on("V", vq2)
            ct23 = collective_max(rbase + 1, [s11[0:1, 0:1], s11[0:1, 1:2]],
                                  gv[:, 0:2], 2, S.tok(f"{qt}pm"), qt)
            def vsc(ve):
                S.wait(ve, ct23)
                ve.tensor_scalar(mxs[:, 0:2], gv[:, 0:2], c_sc[:, 0:1], 0.0,
                                 ALU.mult, ALU.add)
                S.fence(ve)
                ve.tensor_scalar(c_sc[:, 4:6], mxs[:, 0:2], 1.0 / QMAXA, EPS_Q,
                                 ALU.mult, ALU.max)
                S.fence(ve)
                ve.reciprocal(c_sc[:, 2:3], c_sc[:, 4:5])
                ve.reciprocal(c_sc[:, 3:4], c_sc[:, 5:6])
                ve.memset(umx[:, 0:1], 0.0)
                S.fence(ve)
                ve.tensor_copy(c_sc[:, 14:15], c_sc[:, 2:3])
                S.sig(ve, f"{qt}sc")
            S.on("V", vsc)
            sctok = S.tok(f"{qt}sc")

            if stage < 5:
                return
            ph1.close()
            at = f"{P}A"
            a_up = ph.enter_context(nc.sbuf_tensor(f"f{P}_aup", [128, 2 * 512], F32))
            a_ga = ph.enter_context(nc.sbuf_tensor(f"f{P}_aga", [128, 2 * 512], F32))
            a_t = ph.enter_context(nc.sbuf_tensor(f"f{P}_atA", [128, 2 * 512], F32))
            a_t2 = ph.enter_context(nc.sbuf_tensor(f"f{P}_atB", [128, 2 * 512], F32))
            a_i16 = ph.enter_context(nc.sbuf_tensor(f"f{P}_i16", [128, 2 * 512], I16))
            a_pr = ph.enter_context(nc.sbuf_tensor(f"f{P}_apr", [128, 2 * 512], F32))
            ntile = NHC * NT // 512
            in_free = [None, None]
            out_free = [None, None]
            for j in range(ntile):
                hc, off = divmod(j * 512, NT)
                sl = j % 2
                ss = slice(sl * 512, sl * 512 + 512)
                def y(yq, hc=hc, off=off, sl=sl, ss=ss, if_=in_free[sl]):
                    S.wait(yq, sctok)
                    S.wait(yq, rawtok)
                    S.wait(yq, if_)
                    S.inc(yq.dma_start(out=a_up[:, ss], in_=upraw[hc, :, off:off + 512]),
                          f"d{at}i{sl}", 16)
                    S.inc(yq.dma_start(out=a_ga[:, ss], in_=gateraw[hc, :, off:off + 512]),
                          f"d{at}i{sl}", 16)
                def v(ve, sl=sl, ss=ss, of_=out_free[sl]):
                    S.wait(ve, S.tok(f"d{at}i{sl}"))
                    S.wait(ve, sctok)
                    S.wait(ve, of_)
                    ve.tensor_scalar(a_t[:, ss], a_up[:, ss], c_sc[:, 2:3], MAGIC,
                                     ALU.mult, ALU.add)
                    S.fence(ve)
                    ve.tensor_scalar(a_t[:, ss], a_t[:, ss], MAGIC, 0.0,
                                     ALU.subtract, ALU.add)
                    ve.tensor_copy(a_i16[:, ss], a_t[:, ss])
                    S.fence(ve)
                    ve.tensor_scalar(a_t[:, ss], a_ga[:, ss], c_sc[:, 3:4], MAGIC,
                                     ALU.mult, ALU.add)
                    iftok = S.sig(ve, f"{at}if")
                    S.wait(ve, iftok)
                    ve.tensor_scalar(a_t[:, ss], a_t[:, ss], MAGIC, 0.0,
                                     ALU.subtract, ALU.add)
                    S.sig(ve, f"{at}gq")
                def s(se, sl=sl, ss=ss):
                    S.wait(se, S.tok(f"{at}gq"))
                    se.activation(a_t2[:, ss], a_t[:, ss], AF.Sigmoid,
                                  scale=c_sc[:, 5:6])
                    S.sig(se, f"{at}sg")
                def v2(ve, sl=sl, ss=ss, j=j):
                    S.wait(ve, S.tok(f"{at}sg"))
                    ve.tensor_scalar(a_t2[:, ss], a_t2[:, ss], 65536.0, MAGIC,
                                     ALU.mult, ALU.add)
                    ve.tensor_scalar(a_t2[:, ss], a_t2[:, ss], MAGIC, 65535.0,
                                     ALU.subtract, ALU.min)
                    ve.tensor_scalar(a_t2[:, ss], a_t2[:, ss], 1.0 / 65536.0, 0.0,
                                     ALU.mult, ALU.add)
                    ve.tensor_tensor(a_pr[:, ss], a_t[:, ss], a_t2[:, ss], ALU.mult)
                    S.sig(ve, f"{at}pr")
                    ve.tensor_reduce(mxs2[:, j:j + 1], a_pr[:, ss], AX.X, ALU.max,
                                     apply_absolute_value=True)
                    S.sig(ve, f"{at}prm")
                def y2(yq, hc=hc, off=off, sl=sl, ss=ss):
                    S.wait(yq, S.tok(f"{at}pr"))
                    S.inc(yq.dma_start(out=m2i[hc, :, off:off + 512], in_=a_i16[:, ss]),
                          f"d{at}o{sl}", 16)
                    S.inc(yq.dma_start(out=prodb[hc, :, off:off + 512], in_=a_pr[:, ss]),
                          f"d{at}o{sl}", 16)
                S.on("Y", y)
                S.on("V", v)
                S.on("S", s)
                S.on("V", v2)
                S.on("Y", y2)
                in_free[sl] = S.tok(f"{at}if")
                out_free[sl] = S.tok(f"d{at}o{sl}")
            def vfin4(ve):
                S.wait(ve, S.tok(f"{at}prm"))
                ve.tensor_reduce(umx[:, 0:1], mxs2[:, 0:192], AX.X, ALU.max)
                S.sig(ve, f"{at}fin")
            S.on("V", vfin4)
            pt4 = par_max(umx[:, 0:1], s11[0:1, 0:1], S.tok(f"{at}fin"), at, PSF["g"])
            ct4 = collective_max(rbase + 2, [s11[0:1, 0:1]], gv[:, 0:1], 1, pt4, at)
            def vs4(ve):
                S.wait(ve, ct4)
                ve.tensor_scalar(mxs[:, 0:1], gv[:, 0:1], c_sc[:, 5:6], 0.0,
                                 ALU.mult, ALU.add)
                S.fence(ve)
                ve.tensor_scalar(c_sc[:, 6:7], mxs[:, 0:1], 1.0 / QMAXA, EPS_Q,
                                 ALU.mult, ALU.max)
                S.fence(ve)
                ve.reciprocal(c_sc[:, 13:14], c_sc[:, 6:7])
                S.fence(ve)
                ve.tensor_tensor(c_sc[:, 7:8], c_sc[:, 13:14], c_sc[:, 5:6], ALU.mult)
                S.sig(ve, f"{at}s4")
            S.on("V", vs4)
            s4tok = S.tok(f"{at}s4")
            def yod(yq):
                S.wait(yq, S.tok(f"d{at}o0"))
                S.wait(yq, S.tok(f"d{at}o1"))
                S.bump(yq, f"{at}odone")
            S.on("Y", yod)
            aodone = S.tok(f"{at}odone")

            if stage < 6:
                return
            bt = f"{P}B"
            in_free = [None, None]
            out_free = [None, None]
            for j in range(ntile):
                hc, off = divmod(j * 512, NT)
                sl = j % 2
                ss = slice(sl * 512, sl * 512 + 512)
                def y(yq, hc=hc, off=off, sl=sl, ss=ss, if_=in_free[sl]):
                    S.wait(yq, s4tok)
                    S.wait(yq, aodone)
                    S.wait(yq, if_)
                    S.inc(yq.dma_start(out=a_up[:, ss], in_=prodb[hc, :, off:off + 512]),
                          f"d{bt}i{sl}", 16)
                    S.inc(yq.dma_start(out=a_i16[:, ss], in_=m2i[hc, :, off:off + 512]),
                          f"d{bt}i{sl}", 16)
                def v(ve, sl=sl, ss=ss, of_=out_free[sl], j=j):
                    S.wait(ve, S.tok(f"d{bt}i{sl}"))
                    S.wait(ve, s4tok)
                    S.wait(ve, of_)
                    ve.tensor_scalar(a_t[:, ss], a_up[:, ss], c_sc[:, 7:8], MAGIC,
                                     ALU.mult, ALU.add)
                    S.fence(ve)
                    ve.tensor_scalar(a_t[:, ss], a_t[:, ss], MAGIC, 0.0,
                                     ALU.subtract, ALU.add)
                    ve.tensor_copy(a_t2[:, ss], a_i16[:, ss])
                    S.sig(ve, f"{bt}if")
                    ve.tensor_tensor(a_pr[:, ss], a_t[:, ss], a_t2[:, ss], ALU.mult)
                    S.sig(ve, f"{bt}pr")
                    ve.tensor_reduce(mxs2[:, j:j + 1], a_pr[:, ss], AX.X, ALU.max,
                                     apply_absolute_value=True)
                    S.sig(ve, f"{bt}prm")
                def y2(yq, hc=hc, off=off, sl=sl, ss=ss):
                    S.wait(yq, S.tok(f"{bt}pr"))
                    S.inc(yq.dma_start(out=mmb[hc, :, off:off + 512], in_=a_pr[:, ss]),
                          f"d{bt}o{sl}", 16)
                S.on("Y", y)
                S.on("V", v)
                S.on("Y", y2)
                in_free[sl] = S.tok(f"{bt}if")
                out_free[sl] = S.tok(f"d{bt}o{sl}")
            def vfin5(ve):
                S.wait(ve, S.tok(f"{bt}prm"))
                ve.tensor_reduce(umx[:, 0:1], mxs2[:, 0:192], AX.X, ALU.max)
                S.sig(ve, f"{bt}fin")
            S.on("V", vfin5)
            pt5 = par_max(umx[:, 0:1], s11[0:1, 0:1], S.tok(f"{bt}fin"), bt, PSF["g"])
            ct5 = collective_max(rbase + 3, [s11[0:1, 0:1]], gv[:, 0:1], 1, pt5, bt)
            def vs5(ve):
                S.wait(ve, ct5)
                ve.tensor_tensor(c_sc[:, 8:9], c_sc[:, 6:7], c_sc[:, 4:5], ALU.mult)
                S.fence(ve)
                ve.tensor_scalar(mxs[:, 0:1], gv[:, 0:1], c_sc[:, 8:9], 0.0,
                                 ALU.mult, ALU.add)
                S.fence(ve)
                ve.tensor_scalar(c_sc[:, 9:10], mxs[:, 0:1], 1.0 / QMAXA, EPS_Q,
                                 ALU.mult, ALU.max)
                S.fence(ve)
                ve.reciprocal(c_sc[:, 10:11], c_sc[:, 9:10])
                S.fence(ve)
                ve.tensor_tensor(c_sc[:, 11:12], c_sc[:, 8:9], c_sc[:, 10:11], ALU.mult)
                ve.tensor_scalar(c_sc[:, 12:13], c_sc[:, 9:10], 0.5, 0.0, ALU.mult, ALU.add)
                S.sig(ve, f"{bt}s5")
            S.on("V", vs5)
            s5tok = S.tok(f"{bt}s5")
            def yod2(yq):
                S.wait(yq, S.tok(f"d{bt}o0"))
                S.wait(yq, S.tok(f"d{bt}o1"))
                S.bump(yq, f"{bt}odone")
            S.on("Y", yod2)
            bodone = S.tok(f"{bt}odone")

            if stage < 7:
                return
            dt = f"{P}D"
            m5h = ph.enter_context(nc.sbuf_tensor(f"f{P}_m5h", [128, NHC * 512], BF16))
            m5l = ph.enter_context(nc.sbuf_tensor(f"f{P}_m5l", [128, NHC * 512], BF16))
            dwb = [ph.enter_context(nc.sbuf_tensor(f"f{P}_dw{i}", [128, 24 * 128], BF16))
                   for i in range(4)]
            in_free = [None, None]
            w_free = [None, None]
            bank_free = {"a": None, "b": None}
            x_free = {"a": None, "b": None}
            y_free = {"a": None, "b": None}
            mm_tt_tok = None
            for tt in range(NT // 512):
                off = tt * 512
                for hc in range(NHC):
                    sl = hc % 2
                    ss = slice(sl * 512, sl * 512 + 512)
                    def y(yq, hc=hc, off=off, sl=sl, ss=ss, if_=in_free[sl]):
                        S.wait(yq, s5tok)
                        S.wait(yq, bodone)
                        S.wait(yq, if_)
                        S.inc(yq.dma_start(out=a_up[:, ss], in_=mmb[hc, :, off:off + 512]),
                              f"d{dt}i{sl}", 16)
                    def v(ve, hc=hc, sl=sl, ss=ss, tt=tt, mtt=mm_tt_tok):
                        S.wait(ve, S.tok(f"d{dt}i{sl}"))
                        S.wait(ve, s5tok)
                        if tt > 0 and hc == 0:
                            S.wait(ve, mtt)
                        ve.tensor_scalar(a_t[:, ss], a_up[:, ss], c_sc[:, 11:12], MAGIC,
                                         ALU.mult, ALU.add)
                        iftok = S.sig(ve, f"{dt}if")
                        S.wait(ve, iftok)
                        ve.tensor_scalar(a_t[:, ss], a_t[:, ss], MAGIC, 0.0,
                                         ALU.subtract, ALU.add)
                        hs = slice(hc * 512, (hc + 1) * 512)
                        ve.tensor_copy(m5h[:, hs], a_t[:, ss])
                        ve.tensor_tensor(m5l[:, hs], a_t[:, ss], m5h[:, hs], ALU.subtract)
                        S.sig(ve, f"{dt}m5")
                    S.on("Y", y)
                    S.on("V", v)
                    in_free[sl] = S.tok(f"{dt}if")
                m5tok = S.tok(f"{dt}m5")
                for dc in range(NCH):
                    ws = dc % 2
                    pb = "ab"[dc % 2]
                    def yw(yq, dc=dc, ws=ws, wf=w_free[ws]):
                        S.wait(yq, wf)
                        for j, hl in enumerate((0, 1)):
                            S.inc(yq.dma_start(out=dwb[ws * 2 + j][:], in_=pk_d[hl, dc]),
                                  f"d{dt}w{ws}", 16)
                    S.on("Y", yw)
                    wtok = S.tok(f"d{dt}w{ws}")
                    def t(te, dc=dc, ws=ws, pb=pb, m5tok=m5tok, wtok=wtok,
                          bf=bank_free[pb]):
                        S.wait(te, m5tok)
                        S.wait(te, wtok)
                        S.wait(te, bf)
                        whi, wlo = dwb[ws * 2], dwb[ws * 2 + 1]
                        for kc in range(NHC):
                            ks = slice(kc * 512, kc * 512 + 512)
                            wk = slice(kc * 128, (kc + 1) * 128)
                            te.matmul(PSF[pb][:, 0:512], whi[:, wk], m5h[:, ks],
                                      start=(kc == 0), stop=False)
                            te.matmul(PSF[pb][:, 0:512], whi[:, wk], m5l[:, ks],
                                      start=False, stop=False)
                            te.matmul(PSF[pb][:, 0:512], wlo[:, wk], m5h[:, ks],
                                      start=False, stop=(kc == NHC - 1))
                        S.sig(te, f"{dt}mm")
                    S.on("T", t)
                    w_free[ws] = S.tok(f"{dt}mm")
                    mmtok = S.tok(f"{dt}mm")
                    def y2(yq, dc=dc, off=off, pb=pb, xf=x_free[pb]):
                        S.wait(yq, xf)
                        xoff = 0 if pb == "a" else 512
                        S.inc(yq.dma_start(out=xt[:, xoff:xoff + 512],
                                           in_=xsrc[dc, :, off:off + 512]), f"d{dt}x{pb}", 16)
                    def v2(ve, dc=dc, off=off, pb=pb, mmtok=mmtok, yf=y_free[pb]):
                        S.wait(ve, mmtok)
                        S.wait(ve, S.tok(f"d{dt}x{pb}"))
                        S.wait(ve, yf)
                        xoff = 0 if pb == "a" else 512
                        ysl = t1[:, 0:512] if pb == "a" else t1[:, 512:1024]
                        ve.tensor_scalar(ysl, PSF[pb][:, 0:512], c_sc[:, 12:13], 0.0,
                                         ALU.mult, ALU.add)
                        pvt = S.sig(ve, f"{dt}pv")
                        S.wait(ve, pvt)
                        ve.tensor_tensor(ysl, ysl, xt[:, xoff:xoff + 512], ALU.add)
                        S.sig(ve, f"{dt}yv")
                    S.on("Y", y2)
                    S.on("V", v2)
                    bank_free[pb] = S.tok(f"{dt}pv")
                    x_free[pb] = S.tok(f"{dt}yv")
                    yvtok = S.tok(f"{dt}yv")
                    def y3(yq, dc=dc, off=off, pb=pb, yvtok=yvtok):
                        S.wait(yq, yvtok)
                        ysl = t1[:, 0:512] if pb == "a" else t1[:, 512:1024]
                        S.inc(yq.dma_start(out=xdst[dc, :, off:off + 512], in_=ysl),
                              f"d{dt}o", 16)
                    S.on("Y", y3)
                    y_free[pb] = S.tok(f"d{dt}o")
                mm_tt_tok = S.tok(f"{dt}mm")
            def yend(yq):
                S.wait(yq, S.tok(f"d{dt}o"))
            S.on("Y", yend)


        def attn_tiles(layer, qi_g):
            if layer == 1 or qi_g <= 3:
                return [(0, 0, qi_g + 1)], {qi_g: 0}
            if qi_g == 4:
                return [(0, 0, 5)], {0: 2, 4: 0}
            return [(0, qi_g - 4, 5), (5, 0, 1)], {0: 1, 4: 0, 5: 3}

        def emit_attn(xsrc, xdst, layer, P, ph):
            L1 = (layer == 1)
            wsrc = l1_w if L1 else l0_w
            stsub = ExitStack()
            psg = stsub.enter_context(nc.psum_tensor(f"a{P}_psg", [128, 512], F32))
            psh = stsub.enter_context(nc.psum_tensor(f"a{P}_psh", [128, 512], F32))
            _, rtok = stats_pass(xsrc, 0, NT, f"{P}st", ph, psg, psh)
            stsub.close()
            ps_sc = ph.enter_context(nc.psum_tensor(f"a{P}_sc", [128, 2048], F32))
            ps_tp = ph.enter_context(nc.psum_tensor(f"a{P}_tp", [128, 512], F32))
            ps_av = ph.enter_context(nc.psum_tensor(f"a{P}_av", [128, 512], F32))
            pj = [ph.enter_context(nc.psum_tensor(f"a{P}_pj{i}", [128, 512], F32))
                  for i in range(2)]
            mb = ph.enter_context(nc.sbuf_tensor(f"a{P}_mb", [128, 2048], F32))
            rb = ph.enter_context(nc.sbuf_tensor(f"a{P}_rb", [128, 2048], F32))
            lnh = ph.enter_context(nc.sbuf_tensor(f"a{P}_lnh", [128, NCH * 2048], BF16))
            lnl = ph.enter_context(nc.sbuf_tensor(f"a{P}_lnl", [128, NCH * 2048], BF16))
            xt = ph.enter_context(nc.sbuf_tensor(f"a{P}_xt", [128, 2048], F32))
            st32 = ph.enter_context(nc.sbuf_tensor(f"a{P}_st", [128, 2048], F32))
            dall = ph.enter_context(nc.sbuf_tensor(f"a{P}_da", [128, 16], F32))
            biasv = ph.enter_context(nc.sbuf_tensor(f"a{P}_bv", [128, 8], F32))
            xprev = [None]
            for s_ in range(2):
                t0 = s_ * 2048
                Q = f"{P}s{s_}"
                prev_omm = S.tok(f"{P}s{s_-1}omm") if s_ else None
                def yb(yq, t0=t0, lc=S.tok(f"{P}alnc")):
                    S.wait(yq, rtok)
                    if lc[1] > 0:
                        S.wait(yq, lc)
                    bcast_row(yq, mb[:, :], 0, t0, 2048, f"d{Q}b")
                    bcast_row(yq, rb[:, :], 1, t0, 2048, f"d{Q}b")
                S.on("Y", yb)
                btok = S.tok(f"d{Q}b")
                for c in range(NCH):
                    def y2(yq, c=c, t0=t0, xs=xprev[0]):
                        S.wait(yq, xs)
                        S.inc(yq.dma_start(out=xt[:, :], in_=xsrc[c, :, t0:t0 + 2048]),
                              f"d{Q}x", 16)
                    def v(ve, c=c, btok=btok, xd=None):
                        S.wait(ve, btok)
                        S.wait(ve, S.tok(f"d{Q}x"))
                        ve.tensor_tensor(st32[:], xt[:, :], mb[:], ALU.subtract)
                        S.sig(ve, f"{P}axf")
                        ve.tensor_tensor(st32[:], st32[:], rb[:], ALU.mult)
                        cs = slice(c * 2048, (c + 1) * 2048)
                        ve.tensor_copy(lnh[:, cs], st32[:])
                        ve.tensor_tensor(lnl[:, cs], st32[:], lnh[:, cs], ALU.subtract)
                        S.sig(ve, f"{P}alnc")
                    S.on("Y", y2)
                    S.on("V", v)
                    xprev[0] = S.tok(f"{P}axf")
                lntok = S.tok(f"{P}alnc")
                subA = ExitStack()
                w32 = subA.enter_context(nc.sbuf_tensor(f"a{P}{s_}_w32", [128, 1024], F32))
                whi = subA.enter_context(nc.sbuf_tensor(f"a{P}{s_}_whi", [128, 1024], BF16))
                wlo = subA.enter_context(nc.sbuf_tensor(f"a{P}{s_}_wlo", [128, 1024], BF16))
                qh = subA.enter_context(nc.sbuf_tensor(f"a{P}{s_}_qh", [128, 2048], BF16))
                ql = subA.enter_context(nc.sbuf_tensor(f"a{P}{s_}_ql", [128, 2048], BF16))
                kh2 = subA.enter_context(nc.sbuf_tensor(f"a{P}{s_}_kh", [128, 2048], BF16))
                kl2 = subA.enter_context(nc.sbuf_tensor(f"a{P}{s_}_kl", [128, 2048], BF16))
                v32 = subA.enter_context(nc.sbuf_tensor(f"a{P}{s_}_v32", [128, 2048], F32))
                vT = subA.enter_context(nc.sbuf_tensor(f"a{P}{s_}_vT", [128, 2048], F32))
                p32 = subA.enter_context(nc.sbuf_tensor(f"a{P}{s_}_p32", [128, 2048], F32))
                pT = subA.enter_context(nc.sbuf_tensor(f"a{P}{s_}_pT", [128, 2048], F32))
                av2 = subA.enter_context(nc.sbuf_tensor(f"a{P}{s_}_av2", [64, 2048], F32))
                ivd = subA.enter_context(nc.sbuf_tensor(f"a{P}{s_}_ivd", [64, 2048], F32))
                exp_par = [None, None]
                exp_full_box = [None]
                for hg in range(8):
                    lastqk = S.tok(f"{Q}qk")
                    lastav = S.tok(f"{Q}avmm")
                    for pi, pname in enumerate("qkv"):
                        def yw(yq, pname=pname, hg=hg, pm=S.tok(f"{Q}pmm"),
                               prev_omm=prev_omm, first=(hg == 0 and pi == 0)):
                            S.wait(yq, pm)
                            if first and prev_omm is not None:
                                S.wait(yq, prev_omm)
                            src = bass.AP(wsrc[pname].tensor, hg * 128,
                                          [[1024, 128], [128 * 1024, 8], [1, 128]])
                            S.inc(yq.dma_start(
                                out=w32[:].rearrange("p (k f) -> p k f", k=8), in_=src),
                                f"d{Q}w", 16)
                        def vw(ve, wtok=None):
                            S.wait(ve, S.tok(f"d{Q}w"))
                            ve.tensor_copy(whi[:], w32[:])
                            ve.tensor_tensor(wlo[:], w32[:], whi[:], ALU.subtract)
                            S.sig(ve, f"{Q}wsp")
                        S.on("Y", yw)
                        S.on("V", vw)
                        wsptok = S.tok(f"{Q}wsp")
                        for tq in range(4):
                            pjb = pj[tq % 2]
                            def t(te, tq=tq, pjb=pjb, wsptok=wsptok, lntok=lntok,
                                  pf=S.tok(f"{Q}pev"), pi=pi, lastqk=lastqk, lastav=lastav):
                                S.wait(te, wsptok)
                                S.wait(te, lntok)
                                if pf[1] >= 2:
                                    S.wait(te, (pf[0], pf[1] - 1))
                                if pi < 2:
                                    S.wait(te, lastqk)
                                else:
                                    S.wait(te, lastav)
                                for kc in range(NCH):
                                    rs = slice(kc * 2048 + tq * 512, kc * 2048 + tq * 512 + 512)
                                    wk = slice(kc * 128, (kc + 1) * 128)
                                    te.matmul(pjb[:, 0:512], whi[:, wk], lnh[:, rs],
                                              start=(kc == 0), stop=False)
                                    te.matmul(pjb[:, 0:512], wlo[:, wk], lnh[:, rs],
                                              start=False, stop=False)
                                    te.matmul(pjb[:, 0:512], whi[:, wk], lnl[:, rs],
                                              start=False, stop=(kc == NCH - 1))
                                S.sig(te, f"{Q}pmm")
                            S.on("T", t)
                            mmtok = S.tok(f"{Q}pmm")
                            def sev(se, tq=tq, pjb=pjb, pname=pname, hg=hg, mmtok=mmtok,
                                    sf=S.tok(f"{Q}spl")):
                                S.wait(se, mmtok)
                                if pname != "v":
                                    S.wait(se, sf)
                                dst = (v32 if pname == "v" else st32)[:, tq * 512:(tq + 1) * 512]
                                if L1:
                                    se.activation(dst, pjb[:, 0:512], AF.Identity,
                                                  bias=c_b[pname][:, hg:hg + 1])
                                else:
                                    se.activation(dst, pjb[:, 0:512], AF.Identity)
                                S.sig(se, f"{Q}pev")
                            S.on("S", sev)
                        evtok = S.tok(f"{Q}pev")
                        if pname in ("q", "k"):
                            dsth, dstl = (qh, ql) if pname == "q" else (kh2, kl2)
                            def vsp(ve, dsth=dsth, dstl=dstl, evtok=evtok):
                                S.wait(ve, evtok)
                                ve.tensor_copy(dsth[:], st32[:])
                                ve.tensor_tensor(dstl[:], st32[:], dsth[:], ALU.subtract)
                                S.sig(ve, f"{Q}spl")
                            S.on("V", vsp)
                        else:
                            for g4 in range(4):
                                def tv(te, g4=g4, evtok=evtok, tf=S.tok(f"{Q}tpf")):
                                    S.wait(te, evtok)
                                    S.wait(te, tf)
                                    for i in range(4):
                                        kt = g4 * 4 + i
                                        te.transpose(ps_tp[:, i * 128:(i + 1) * 128],
                                                     v32[:, kt * 128:(kt + 1) * 128],
                                                     c_iden[:])
                                    S.sig(te, f"{Q}vtt")
                                def vv2(ve, g4=g4, vt=S.tok(f"{Q}vtt")):
                                    S.wait(ve, (f"{Q}vtt", vt[1] + 1))
                                    ve.tensor_copy(vT[:, g4 * 512:(g4 + 1) * 512],
                                                   ps_tp[:, 0:512])
                                    S.sig(ve, f"{Q}tpf")
                                S.on("T", tv)
                                S.on("V", vv2)
                    vttok = S.tok(f"{Q}tpf")
                    for h in range(2):
                        hofs = 64 * h
                        avwr = S.tok(f"d{Q}avh")
                        def mk_qi(qi_g):
                            runs, mtiles = attn_tiles(layer, qi_g)
                            ntot = runs[-1][0] + runs[-1][2]
                            par = qi_g % 2
                            full = ntot > 8
                            return dict(runs=runs, mtiles=mtiles, ntot=ntot,
                                        par=par, full=full,
                                        base=0 if full else 8 * par)
                        ctx = {}

                        def emitA(qi_g):
                            c = ctx[qi_g]
                            runs, ntot = c["runs"], c["ntot"]
                            base, par = c["base"], c["par"]
                            if c["full"]:
                                scfree = (exp_par[0], exp_par[1], exp_full_box[0])
                            else:
                                scfree = (exp_par[par], exp_full_box[0])
                            qsl = slice(qi_g * 128, (qi_g + 1) * 128)
                            def tqk(te, runs=runs, qsl=qsl, hofs=hofs, base=base,
                                    scfree=scfree, vttok=vttok,
                                    lastsp=S.tok(f"{Q}spl")):
                                S.wait(te, lastsp)
                                S.wait(te, vttok)
                                for tk in scfree:
                                    S.wait(te, tk)
                                for (slot0, kt0, n) in runs:
                                    for pss in range(3):
                                        lh = (qh, ql, qh)[pss]
                                        rh = (kh2, kh2, kl2)[pss]
                                        lt = lh[hofs:hofs + 64, qsl]
                                        done = 0
                                        while done < n:
                                            pos = base + slot0 + done
                                            take = min(4 - pos % 4, n - done)
                                            te.matmul(
                                                ps_sc[:, pos * 128:(pos + take) * 128],
                                                lt,
                                                rh[hofs:hofs + 64,
                                                   (kt0 + done) * 128:(kt0 + done + take) * 128],
                                                start=(pss == 0), stop=(pss == 2))
                                            done += take
                                S.sig(te, f"{Q}qk")
                            S.on("T", tqk)
                            qktok = S.tok(f"{Q}qk")
                            def vmx(ve, mtiles=c["mtiles"], ntot=ntot, base=base,
                                    par=par, qktok=qktok, bvfree=exp_par[par]):
                                S.wait(ve, qktok)
                                S.wait(ve, bvfree)
                                for slot, mi in mtiles.items():
                                    cs = slice((base + slot) * 128, (base + slot + 1) * 128)
                                    ve.tensor_tensor(ps_sc[:, cs], ps_sc[:, cs],
                                                     c_masks[:, mi * 128:(mi + 1) * 128],
                                                     ALU.add)
                                S.fence(ve)
                                ve.tensor_reduce(biasv[:, 4 * par:4 * par + 1],
                                                 ps_sc[:, base * 128:(base + ntot) * 128],
                                                 AX.X, ALU.max)
                                S.fence(ve)
                                ve.tensor_scalar(biasv[:, 4 * par + 1:4 * par + 2],
                                                 biasv[:, 4 * par:4 * par + 1], -0.125, 0.0,
                                                 ALU.mult, ALU.add)
                                S.sig(ve, f"{Q}bv")
                            S.on("V", vmx)
                            bvtok = S.tok(f"{Q}bv")
                            if c["full"] and qi_g >= 1:
                                pn = (ctx[qi_g - 1]["ntot"] + 3) // 4
                                ttprev = (f"{Q}ptt", S.tok(f"{Q}ptt")[1] + pn)
                            else:
                                ttprev = S.tok(f"{Q}ptt")
                            def sexp(se, ntot=ntot, base=base, par=par, qi_g=qi_g,
                                     bvtok=bvtok, ttprev=ttprev):
                                S.wait(se, bvtok)
                                S.wait(se, ttprev)
                                cs = slice(base * 128, (base + ntot) * 128)
                                se.activation(p32[:, cs], ps_sc[:, cs],
                                              AF.Exp, bias=biasv[:, 4 * par + 1:4 * par + 2],
                                              scale=0.125,
                                              accum_out=dall[:, qi_g:qi_g + 1])
                                S.sig(se, f"{Q}exp")
                            S.on("S", sexp)
                            c["exptok"] = S.tok(f"{Q}exp")
                            if c["full"]:
                                exp_full_box[0] = exp_par[0] = exp_par[1] = c["exptok"]
                            else:
                                exp_par[par] = c["exptok"]

                        def emitB(qi_g):
                            c = ctx[qi_g]
                            runs, ntot, base = c["runs"], c["ntot"], c["base"]
                            exptok = c["exptok"]
                            for g0 in range(0, ntot, 4):
                                nt_ = min(4, ntot - g0)
                                def tpp(te, g0=g0, nt_=nt_, base=base, exptok=exptok,
                                        tf=S.tok(f"{Q}tpf")):
                                    S.wait(te, exptok)
                                    S.wait(te, tf)
                                    for i in range(nt_):
                                        slot = base + g0 + i
                                        te.transpose(ps_tp[:, i * 128:(i + 1) * 128],
                                                     p32[:, slot * 128:(slot + 1) * 128],
                                                     c_iden[:])
                                    S.sig(te, f"{Q}ptt")
                                def vpp(ve, g0=g0, nt_=nt_, base=base,
                                        tt=S.tok(f"{Q}ptt")):
                                    S.wait(ve, (f"{Q}ptt", tt[1] + 1))
                                    ve.tensor_copy(
                                        pT[:, (base + g0) * 128:(base + g0 + nt_) * 128],
                                        ps_tp[:, 0:nt_ * 128])
                                    S.sig(ve, f"{Q}tpf")
                                S.on("T", tpp)
                                S.on("V", vpp)
                            pttok = S.tok(f"{Q}tpf")
                            def tav(te, hofs=hofs, runs=runs, ntot=ntot, base=base,
                                    pttok=pttok, avev=S.tok(f"{Q}avev")):
                                S.wait(te, pttok)
                                S.wait(te, avev)
                                j = 0
                                for (slot0, kt0, n) in runs:
                                    for i in range(n):
                                        slot, kt = base + slot0 + i, kt0 + i
                                        te.matmul(ps_av[0:64, 0:128],
                                                  vT[:, kt * 128 + hofs:kt * 128 + hofs + 64],
                                                  pT[:, slot * 128:(slot + 1) * 128],
                                                  start=(j == 0), stop=(j == ntot - 1))
                                        j += 1
                                S.sig(te, f"{Q}avmm")
                            S.on("T", tav)
                            avtok = S.tok(f"{Q}avmm")
                            def vav(ve, qi_g=qi_g, avtok=avtok, avwr=avwr):
                                S.wait(ve, avtok)
                                if qi_g == 0:
                                    S.wait(ve, avwr)
                                ve.tensor_copy(av2[0:64, qi_g * 128:(qi_g + 1) * 128],
                                               ps_av[0:64, 0:128])
                                S.sig(ve, f"{Q}avev")
                            S.on("V", vav)

                        for qi_g in (0, 1):
                            ctx[qi_g] = mk_qi(qi_g)
                            emitA(qi_g)
                        for qi_g in range(16):
                            emitB(qi_g)
                            if qi_g + 2 < 16:
                                ctx[qi_g + 2] = mk_qi(qi_g + 2)
                                emitA(qi_g + 2)
                        exptok = S.tok(f"{Q}exp")
                        def tdt(te, exptok=exptok, tf=S.tok(f"{Q}tpf")):
                            S.wait(te, exptok)
                            S.wait(te, tf)
                            te.transpose(ps_tp[0:16, 0:128], dall[:, 0:16], c_iden[:])
                            S.sig(te, f"{Q}dtt")
                        def vdt(ve, dt=S.tok(f"{Q}dtt")):
                            S.wait(ve, (f"{Q}dtt", dt[1] + 1))
                            ve.tensor_copy(st32[0:16, 0:128], ps_tp[0:16, 0:128])
                            ftok = S.sig(ve, f"{Q}tpf")
                            S.wait(ve, ftok)
                            ve.reciprocal(st32[0:16, 128:256], st32[0:16, 0:128])
                            S.sig(ve, f"{Q}div")
                        def ydt(yq, h=h, dv=S.tok(f"{Q}div")):
                            S.wait(yq, (f"{Q}div", dv[1] + 1))
                            dst = bass.AP(rows.tensor, (4 + h) * NT, [[128, 16], [1, 128]])
                            S.inc(yq.dma_start(out=dst, in_=st32[0:16, 128:256]),
                                  f"d{Q}iv", 16)
                        S.on("T", tdt)
                        S.on("V", vdt)
                        S.on("Y", ydt)
                        ivtok = S.tok(f"d{Q}iv")
                        def ybc(yq, h=h, ivtok=ivtok, avn=S.tok(f"{Q}avn")):
                            S.wait(yq, ivtok)
                            if avn[1] > 0:
                                S.wait(yq, avn)
                            src = bass.AP(rows.tensor, (4 + h) * NT, [[0, 64], [1, 2048]])
                            S.inc(yq.dma_start(out=ivd[0:64, :], in_=src),
                                  f"d{Q}ivb", 16)
                        def vnr(ve, bt_=S.tok(f"d{Q}ivb"), ave=S.tok(f"{Q}avev")):
                            S.wait(ve, (f"d{Q}ivb", bt_[1] + 16))
                            S.wait(ve, ave)
                            ve.tensor_tensor(av2[0:64, :], av2[0:64, :], ivd[0:64, :],
                                             ALU.mult)
                            S.sig(ve, f"{Q}avn")
                        def yst(yq, hg=hg, h=h, avn=S.tok(f"{Q}avn")):
                            S.wait(yq, (f"{Q}avn", avn[1] + 1))
                            S.inc(yq.dma_start(out=avh[hg, 64 * h:64 * h + 64, :],
                                               in_=av2[0:64, :]), f"d{Q}avh", 16)
                        S.on("Y", ybc)
                        S.on("V", vnr)
                        S.on("Y", yst)
                subA.close()
                avhtok = S.tok(f"d{Q}avh")
                subB = ExitStack()
                wo32 = subB.enter_context(nc.sbuf_tensor(f"a{P}{s_}_wo", [128, 1024], F32))
                woh = subB.enter_context(nc.sbuf_tensor(f"a{P}{s_}_woh", [128, 1024], BF16))
                wol = subB.enter_context(nc.sbuf_tensor(f"a{P}{s_}_wol", [128, 1024], BF16))
                avh_t = subB.enter_context(nc.sbuf_tensor(f"a{P}{s_}_avht", [128, 32 * 512], BF16))
                avl_t = subB.enter_context(nc.sbuf_tensor(f"a{P}{s_}_avlt", [128, 32 * 512], BF16))
                avt = subB.enter_context(nc.sbuf_tensor(f"a{P}{s_}_avt", [128, 2 * 512], F32))
                xr = subB.enter_context(nc.sbuf_tensor(f"a{P}{s_}_xr", [128, 2 * 512], F32))
                yo = subB.enter_context(nc.sbuf_tensor(f"a{P}{s_}_yo", [128, 2 * 512], F32))
                for hg in range(8):
                    for tt in range(4):
                        sl = (hg * 4 + tt) % 2
                        def ya(yq, hg=hg, tt=tt, sl=sl, avhtok=avhtok,
                               af=S.tok(f"{Q}avsp")):
                            S.wait(yq, avhtok)
                            if af[1] >= 2:
                                S.wait(yq, (af[0], af[1] - 1))
                            S.inc(yq.dma_start(out=avt[:, sl * 512:(sl + 1) * 512],
                                               in_=avh[hg, :, tt * 512:(tt + 1) * 512]),
                                  f"d{Q}av{sl}", 16)
                        def vsp2(ve, hg=hg, tt=tt, sl=sl):
                            S.wait(ve, S.tok(f"d{Q}av{sl}"))
                            hs = slice((hg * 4 + tt) * 512, (hg * 4 + tt + 1) * 512)
                            ve.tensor_copy(avh_t[:, hs], avt[:, sl * 512:(sl + 1) * 512])
                            ve.tensor_tensor(avl_t[:, hs], avt[:, sl * 512:(sl + 1) * 512],
                                             avh_t[:, hs], ALU.subtract)
                            S.sig(ve, f"{Q}avsp")
                        S.on("Y", ya)
                        S.on("V", vsp2)
                asptok = S.tok(f"{Q}avsp")
                for oc in range(NCH):
                    def ywo(yq, oc=oc, ow=S.tok(f"{Q}omm")):
                        S.wait(yq, ow)
                        src = bass.AP(wsrc["o"].tensor, oc * 128,
                                      [[1024, 128], [128 * 1024, 8], [1, 128]])
                        S.inc(yq.dma_start(
                            out=wo32[:].rearrange("p (k f) -> p k f", k=8), in_=src),
                            f"d{Q}wo", 16)
                    def vwo(ve):
                        S.wait(ve, S.tok(f"d{Q}wo"))
                        ve.tensor_copy(woh[:], wo32[:])
                        ve.tensor_tensor(wol[:], wo32[:], woh[:], ALU.subtract)
                        S.sig(ve, f"{Q}wosp")
                    S.on("Y", ywo)
                    S.on("V", vwo)
                    wotok = S.tok(f"{Q}wosp")
                    for tt in range(4):
                        it = oc * 4 + tt
                        pjb = pj[it % 2]
                        def t2(te, oc=oc, tt=tt, pjb=pjb, asptok=asptok, wotok=wotok,
                               of=S.tok(f"{Q}oev")):
                            S.wait(te, asptok)
                            S.wait(te, wotok)
                            if of[1] >= 2:
                                S.wait(te, (of[0], of[1] - 1))
                            for hg in range(NCH):
                                hs = slice((hg * 4 + tt) * 512, (hg * 4 + tt + 1) * 512)
                                wk = slice(hg * 128, (hg + 1) * 128)
                                te.matmul(pjb[:, 0:512], woh[:, wk], avh_t[:, hs],
                                          start=(hg == 0), stop=False)
                                te.matmul(pjb[:, 0:512], woh[:, wk], avl_t[:, hs],
                                          start=False, stop=False)
                                te.matmul(pjb[:, 0:512], wol[:, wk], avh_t[:, hs],
                                          start=False, stop=(hg == NCH - 1))
                            S.sig(te, f"{Q}omm")
                        S.on("T", t2)
                        otok = S.tok(f"{Q}omm")
                        def so(se, it=it, oc=oc, pjb=pjb, otok=otok,
                               syo=S.tok(f"d{Q}yo")):
                            S.wait(se, otok)
                            if syo[1] >= 32:
                                S.wait(se, (syo[0], syo[1] - 16))
                            dst = yo[:, (it % 2) * 512:(it % 2) * 512 + 512]
                            if L1:
                                se.activation(dst, pjb[:, 0:512], AF.Identity,
                                              bias=c_b["o"][:, oc:oc + 1])
                            else:
                                se.activation(dst, pjb[:, 0:512], AF.Identity)
                            S.sig(se, f"{Q}oev")
                        def yx(yq, it=it, oc=oc, tt=tt, t0=t0, xf=S.tok(f"{Q}yadd")):
                            if xf[1] >= 2:
                                S.wait(yq, (xf[0], xf[1] - 1))
                            S.inc(yq.dma_start(
                                out=xr[:, (it % 2) * 512:(it % 2) * 512 + 512],
                                in_=xsrc[oc, :, t0 + tt * 512:t0 + (tt + 1) * 512]),
                                f"d{Q}xr", 16)
                        def vy(ve, it=it, oevt=S.tok(f"{Q}oev"), syo=S.tok(f"d{Q}yo")):
                            S.wait(ve, (f"{Q}oev", oevt[1] + 1))
                            S.wait(ve, S.tok(f"d{Q}xr"))
                            if syo[1] >= 32:
                                S.wait(ve, (syo[0], syo[1] - 16))
                            sl2 = (it % 2) * 512
                            ve.tensor_tensor(yo[:, sl2:sl2 + 512], yo[:, sl2:sl2 + 512],
                                             xr[:, sl2:sl2 + 512], ALU.add)
                            S.sig(ve, f"{Q}yadd")
                        def ys(yq, it=it, oc=oc, tt=tt, t0=t0, ya_=S.tok(f"{Q}yadd")):
                            S.wait(yq, (f"{Q}yadd", ya_[1] + 1))
                            sl2 = (it % 2) * 512
                            S.inc(yq.dma_start(
                                out=xdst[oc, :, t0 + tt * 512:t0 + (tt + 1) * 512],
                                in_=yo[:, sl2:sl2 + 512]), f"d{Q}yo", 16)
                        S.on("S", so)
                        S.on("Y", yx)
                        S.on("V", vy)
                        S.on("Y", ys)
                subB.close()
                def yend(yq, Q=Q):
                    S.wait(yq, S.tok(f"d{Q}yo"))
                S.on("Y", yend)

        def emit_ffn1(xsrc, xdst, P, ph):
            stsub = ExitStack()
            psg = stsub.enter_context(nc.psum_tensor(f"g{P}_psg", [128, 512], F32))
            psh = stsub.enter_context(nc.psum_tensor(f"g{P}_psh", [128, 512], F32))
            _, rtok = stats_pass(xsrc, 0, NT, f"{P}st", ph, psg, psh)
            stsub.close()
            pj = [ph.enter_context(nc.psum_tensor(f"g{P}_pj{i}", [128, 512], F32))
                  for i in range(2)]
            sw = ph.enter_context(nc.sbuf_tensor(f"g{P}_sw", [128, NCH * 1024], BF16))
            swr = ph.enter_context(nc.sbuf_tensor(f"g{P}_swr", [128, 2 * 1024], F32))
            mb = ph.enter_context(nc.sbuf_tensor(f"g{P}_mb", [128, 2048], F32))
            rb = ph.enter_context(nc.sbuf_tensor(f"g{P}_rb", [128, 2048], F32))
            lnh = ph.enter_context(nc.sbuf_tensor(f"g{P}_lnh", [128, NCH * 2048], BF16))
            lnl = ph.enter_context(nc.sbuf_tensor(f"g{P}_lnl", [128, NCH * 2048], BF16))
            xt = ph.enter_context(nc.sbuf_tensor(f"g{P}_xt", [128, 2 * 2048], F32))
            st32 = ph.enter_context(nc.sbuf_tensor(f"g{P}_st", [128, 2048], F32))
            xr = ph.enter_context(nc.sbuf_tensor(f"g{P}_xr", [128, 2 * 512], F32))
            yo = ph.enter_context(nc.sbuf_tensor(f"g{P}_yo", [128, 2 * 512], F32))
            for kc in range(NCH):
                sl = kc % 2
                def yw(yq, kc=kc, sl=sl, sg=S.tok(f"{P}sg")):
                    if sg[1] >= 2:
                        S.wait(yq, (sg[0], sg[1] - 1))
                    S.inc(yq.dma_start(out=swr[:, sl * 1024:(sl + 1) * 1024],
                                       in_=wfT[kc * 128:(kc + 1) * 128, :]), f"d{P}w{sl}", 16)
                def ssg(se, kc=kc, sl=sl):
                    S.wait(se, S.tok(f"d{P}w{sl}"))
                    se.activation(sw[:, kc * 1024:(kc + 1) * 1024],
                                  swr[:, sl * 1024:(sl + 1) * 1024], AF.Sign)
                    S.sig(se, f"{P}sg")
                S.on("Y", yw)
                S.on("S", ssg)
            swtok = S.tok(f"{P}sg")
            for half in range(2):
                t0 = half * 2048
                Q = f"{P}h{half}"
                def yb(yq, t0=t0, lc=S.tok(f"{P}lnc")):
                    S.wait(yq, rtok)
                    if lc[1] > 0:
                        S.wait(yq, lc)
                    bcast_row(yq, mb[:, :], 0, t0, 2048, f"d{Q}b")
                    bcast_row(yq, rb[:, :], 1, t0, 2048, f"d{Q}b")
                S.on("Y", yb)
                btok = S.tok(f"d{Q}b")
                xslot = [None, None]
                for c in range(NCH):
                    sl = c % 2
                    ss = slice(sl * 2048, sl * 2048 + 2048)
                    def y2(yq, c=c, t0=t0, ss=ss, xs=xslot[sl]):
                        S.wait(yq, xs)
                        S.inc(yq.dma_start(out=xt[:, ss], in_=xsrc[c, :, t0:t0 + 2048]),
                              f"d{Q}x{sl}", 16)
                    def v(ve, c=c, sl=sl, ss=ss, btok=btok, mmprev=S.tok(f"{P}fmm")):
                        S.wait(ve, btok)
                        S.wait(ve, S.tok(f"d{Q}x{sl}"))
                        S.wait(ve, mmprev)
                        ve.tensor_tensor(st32[:], xt[:, ss], mb[:], ALU.subtract)
                        S.sig(ve, f"{Q}xf")
                        ve.tensor_tensor(st32[:], st32[:], rb[:], ALU.mult)
                        cs = slice(c * 2048, (c + 1) * 2048)
                        ve.tensor_copy(lnh[:, cs], st32[:])
                        ve.tensor_tensor(lnl[:, cs], st32[:], lnh[:, cs], ALU.subtract)
                        S.sig(ve, f"{P}lnc")
                    S.on("Y", y2)
                    S.on("V", v)
                    xslot[sl] = S.tok(f"{Q}xf")
                lntok = S.tok(f"{P}lnc")
                for oc in range(NCH):
                    for tq in range(4):
                        pjb = pj[tq % 2]
                        def t(te, oc=oc, tq=tq, pjb=pjb, lntok=lntok, swtok=swtok,
                              of=S.tok(f"{Q}fev")):
                            S.wait(te, lntok)
                            S.wait(te, swtok)
                            if of[1] >= 2:
                                S.wait(te, (of[0], of[1] - 1))
                            for kc in range(NCH):
                                rs = slice(kc * 2048 + tq * 512, kc * 2048 + tq * 512 + 512)
                                wk = slice(kc * 1024 + oc * 128, kc * 1024 + (oc + 1) * 128)
                                te.matmul(pjb[:, 0:512], sw[:, wk], lnh[:, rs],
                                          start=(kc == 0), stop=False)
                                te.matmul(pjb[:, 0:512], sw[:, wk], lnl[:, rs],
                                          start=False, stop=(kc == NCH - 1))
                            S.sig(te, f"{P}fmm")
                        S.on("T", t)
                        mmtok = S.tok(f"{P}fmm")
                        def yx(yq, oc=oc, tq=tq, t0=t0, xf=S.tok(f"{Q}yadd")):
                            if xf[1] >= 2:
                                S.wait(yq, (xf[0], xf[1] - 1))
                            S.inc(yq.dma_start(
                                out=xr[:, (tq % 2) * 512:(tq % 2) * 512 + 512],
                                in_=xsrc[oc, :, t0 + tq * 512:t0 + (tq + 1) * 512]),
                                f"d{Q}xr", 16)
                        def vy(ve, oc=oc, tq=tq, pjb=pjb, mmtok=mmtok,
                               yst_=S.tok(f"d{Q}yo")):
                            S.wait(ve, mmtok)
                            S.wait(ve, S.tok(f"d{Q}xr"))
                            if yst_[1] >= 32:
                                S.wait(ve, (yst_[0], yst_[1] - 16))
                            sl2 = (tq % 2) * 512
                            ve.tensor_scalar(yo[:, sl2:sl2 + 512], pjb[:, 0:512], 0.5, 0.0,
                                             ALU.mult, ALU.add)
                            S.sig(ve, f"{Q}fev")
                            ve.tensor_tensor(yo[:, sl2:sl2 + 512], yo[:, sl2:sl2 + 512],
                                             xr[:, sl2:sl2 + 512], ALU.add)
                            S.sig(ve, f"{Q}yadd")
                        def ys(yq, oc=oc, tq=tq, t0=t0, ya_=S.tok(f"{Q}yadd")):
                            S.wait(yq, (f"{Q}yadd", ya_[1] + 1))
                            sl2 = (tq % 2) * 512
                            S.inc(yq.dma_start(
                                out=xdst[oc, :, t0 + tq * 512:t0 + (tq + 1) * 512],
                                in_=yo[:, sl2:sl2 + 512]), f"d{Q}yo", 16)
                        S.on("Y", yx)
                        S.on("V", vy)
                        S.on("Y", ys)
                def yend(yq, Q=Q):
                    S.wait(yq, S.tok(f"d{Q}yo"))
                S.on("Y", yend)

        def run_phase(fn, *args):
            with ExitStack() as ph:
                fn(*args, ph)
            barrier()
            S.end_phase()

        run_phase(emit_dequant)
        seqp = [(emit_ffn0, (xin, 0, "A")), (emit_attn, (None, 0, "M")),
                (emit_ffn0, (None, 4, "B")), (emit_ffn1, (None, "C")),
                (emit_attn, (None, 1, "N")), (emit_ffn1, (None, "E"))]
        srcs = [xin, xres[1], xres[2], xres[3], xres[4], xres[5]]
        nph = min(upto, 6)
        for i in range(nph):
            fn, args = seqp[i]
            src = srcs[i]
            dst = xout if i == nph - 1 else srcs[i + 1]
            if fn is emit_ffn0:
                run_phase(fn, src, dst, args[1], args[2])
            elif fn is emit_attn:
                run_phase(fn, src, dst, args[1], args[2])
            else:
                run_phase(fn, src, dst, args[1])

        if dump_outs:
            barrier()
            def ydump(yq):
                for nm, dst in dump_outs.items():
                    S.inc(yq.dma_start(out=dst, in_=named[nm][0]), "ddump", 16)
                S.wait(yq, S.tok("ddump"))
            S.on("Y", ydump)
        barrier()

    print("instr-sections per engine:", S.ni)
    return nc



def _mask_tiles():
    r = np.arange(128)[:, None]
    c = np.arange(128)[None, :]
    vis_d = c <= r
    vis_e = c >= r + 1
    vis_es = vis_e | (c < 4)
    vis_s = np.broadcast_to(c < 4, (128, 128))
    m = np.zeros((4, 128, 128), np.float32)
    for i, vis in enumerate((vis_d, vis_e, vis_es, vis_s)):
        m[i] = np.where(vis, 0.0, NEGINF).astype(np.float32)
    return m


def _prep_inputs(inputs):
    common = {
        "w_up": np.ascontiguousarray(np.asarray(inputs["L0_ffn_w_up"], np.float32)),
        "w_gate": np.ascontiguousarray(np.asarray(inputs["L0_ffn_w_gate"], np.float32)),
        "w_down": np.ascontiguousarray(np.asarray(inputs["L0_ffn_w_down"], np.float32)),
        "l0_wqT": np.ascontiguousarray(np.asarray(inputs["L0_attn_Wq"], np.float32).T),
        "l0_wkT": np.ascontiguousarray(np.asarray(inputs["L0_attn_Wk"], np.float32).T),
        "l0_wvT": np.ascontiguousarray(np.asarray(inputs["L0_attn_Wv"], np.float32).T),
        "l0_woT": np.ascontiguousarray(np.asarray(inputs["L0_attn_Wo"], np.float32).T),
        "l1_wqT": np.ascontiguousarray(
            np.transpose(np.asarray(inputs["L1_attn_W_Q"], np.float32), (1, 0, 2)).reshape(D, D)),
        "l1_wkT": np.ascontiguousarray(
            np.transpose(np.asarray(inputs["L1_attn_W_K"], np.float32), (1, 0, 2)).reshape(D, D)),
        "l1_wvT": np.ascontiguousarray(
            np.transpose(np.asarray(inputs["L1_attn_W_V"], np.float32), (1, 0, 2)).reshape(D, D)),
        "l1_woT": np.ascontiguousarray(np.asarray(inputs["L1_attn_W_O"], np.float32)),
        "wfT": np.ascontiguousarray(np.asarray(inputs["L1_ffn_weight"], np.float32).T),
        "iden32": np.eye(128, dtype=np.float32),
        "idenb": np.eye(128, dtype=ml_dtypes.bfloat16),
        "ones32": np.ones((128, 1), np.float32),
        "masks": _mask_tiles(),
    }
    for p, key in (("q", "L1_attn_b_Q"), ("k", "L1_attn_b_K"), ("v", "L1_attn_b_V")):
        b = np.asarray(inputs[key], np.float32).reshape(16, 64)
        bb = np.zeros((128, 8), np.float32)
        for hg in range(8):
            bb[0:64, hg] = b[2 * hg]
            bb[64:128, hg] = b[2 * hg + 1]
        common[f"l1_b{p}"] = bb
    bo = np.asarray(inputs["L1_attn_b_O"], np.float32).reshape(8, 128)
    common["l1_bo"] = np.ascontiguousarray(bo.T)
    x = np.asarray(inputs["L0_attn_x"], np.float32)
    in_maps = []
    for core in range(8):
        xc = x[2 * core:2 * core + 2]
        xt = np.transpose(xc, (2, 0, 1)).reshape(D, NT)
        m = dict(common)
        m["xT"] = np.ascontiguousarray(xt.reshape(NCH, 128, NT))
        in_maps.append(m)
    return in_maps


def kernel(**inputs):
    in_maps = _prep_inputs(inputs)
    nc = build(upto=7)
    res = run_bass_kernel_spmd(nc, in_maps, core_ids=list(range(8)))
    outs = []
    for core in range(8):
        o = np.asarray(res.results[core]["outT"]).reshape(D, 2, SEQ)
        outs.append(np.transpose(o, (1, 2, 0)))
    return np.ascontiguousarray(np.concatenate(outs, axis=0)).astype(np.float32)


# version 06
# speedup vs baseline: 1.1265x; 1.0011x over previous
import numpy as np
import ml_dtypes
import concourse.bass as bass
import concourse.mybir as mybir
from concourse.bass_utils import run_bass_kernel_spmd
from contextlib import ExitStack

F32 = mybir.dt.float32
BF16 = mybir.dt.bfloat16
I16 = mybir.dt.int16
AF = mybir.ActivationFunctionType
ALU = mybir.AluOpType
AX = mybir.AxisListType

MAGIC = 12582912.0
NEGINF = -1.0e30
D = 1024
NCH = 8
NT = 4096
SEQ = 2048
HID = 3072
NHC = 24
QMAXA = 32767.0
EPS_LN = 1e-5
EPS_Q = 1e-12

ENG_ATTR = {"T": "tensor", "V": "vector", "S": "scalar", "G": "gpsimd", "Y": "sync"}


class Eng:

    def __init__(self, raw, w):
        self._raw = raw
        self._w = w
        self.last = None

    def __getattr__(self, name):
        v = getattr(self._raw, name)
        if not callable(v):
            return v
        def call(*a, **kw):
            r = v(*a, **kw)
            if hasattr(r, "then_inc"):
                self.last = r
            return r
        return call


class Sched:

    def __init__(self, nc, block, es):
        self.nc, self.block, self.es = nc, block, es
        self.cnt = {}
        self.hnd = {}
        self.real_cnt = {}
        self.free_pool = []
        self.free_pool_dma = []
        self.free_pool_dg = []
        self.persist = set()
        self.phase_names = []
        self.waited = {}
        self.ni = {k: 0 for k in ENG_ATTR}

    def sem(self, name):
        if name not in self.hnd:
            if name.startswith("dG"):
                pool = self.free_pool_dg
            elif name.startswith("d"):
                pool = self.free_pool_dma
            else:
                pool = self.free_pool
            if pool:
                h = pool.pop()
            else:
                h = self.es.enter_context(self.nc.semaphore(f"sm{len(self.real_cnt)}"))
                self.real_cnt[id(h)] = 0
            self.hnd[name] = h
            self.cnt[name] = self.real_cnt[id(h)]
            if self.phase_names is not None:
                self.phase_names.append(name)
        return self.hnd[name]

    def persist_name(self, name):
        self.sem(name)
        self.persist.add(name)

    def end_phase(self):
        for name in self.phase_names:
            if name in self.persist or name not in self.hnd:
                continue
            h = self.hnd.pop(name)
            self.real_cnt[id(h)] = self.cnt.pop(name)
            if name.startswith("dG"):
                self.free_pool_dg.append(h)
            elif name.startswith("d"):
                self.free_pool_dma.append(h)
            else:
                self.free_pool.append(h)
            self.waited = {k: v for k, v in self.waited.items() if k[1] != name}
        self.phase_names = []

    def inc(self, instr, name, k=1):
        h = self.sem(name)
        instr.then_inc(h, k)
        self.cnt[name] += k
        self.real_cnt[id(h)] = self.cnt[name]
        return (name, self.cnt[name])

    def sig(self, e, name, k=1):
        assert e.last is not None, f"sig({name}) with no prior instruction"
        r = self.inc(e.last, name, k)
        e.last = None
        return r

    def fence(self, e):
        name = "fn" + e._w
        assert e.last is not None
        tok = self.inc(e.last, name)
        e.last = None
        e._raw.wait_ge(self.hnd[name], tok[1])
        self.waited[(e._w, name)] = tok[1]
        return tok

    def bump(self, e, name, k=1):
        h = self.sem(name)
        e._raw.sem_inc(h, k)
        self.cnt[name] += k
        self.real_cnt[id(h)] = self.cnt[name]
        return (name, self.cnt[name])

    def tok(self, name):
        self.sem(name)
        return (name, self.cnt[name])

    def wait(self, e, tok):
        if tok is None:
            return
        name, val = tok
        if val <= 0:
            return
        key = (e._w, name)
        if self.waited.get(key, -1) >= val:
            return
        e._raw.wait_ge(self.hnd[name], val)
        self.waited[key] = val

    def on(self, w, f):
        self.ni[w] += 1
        getattr(self.block, ENG_ATTR[w])(lambda raw: f(Eng(raw, w)))


def build(upto=7, dumps=(), stage=99):
    nc = bass.Bass("TRN2", target_bir_lowering=False, detect_race_conditions=False)

    xin = nc.dram_tensor("xT", [NCH, 128, NT], F32, kind="ExternalInput").ap()
    w_up_raw = nc.dram_tensor("w_up", [HID, D], F32, kind="ExternalInput").ap()
    w_gate_raw = nc.dram_tensor("w_gate", [HID, D], F32, kind="ExternalInput").ap()
    w_down_raw = nc.dram_tensor("w_down", [D, HID], F32, kind="ExternalInput").ap()
    l0_w = {p: nc.dram_tensor(f"l0_w{p}T", [D, D], F32, kind="ExternalInput").ap()
            for p in ("q", "k", "v", "o")}
    l1_w = {p: nc.dram_tensor(f"l1_w{p}T", [D, D], F32, kind="ExternalInput").ap()
            for p in ("q", "k", "v", "o")}
    l1_b = {p: nc.dram_tensor(f"l1_b{p}", [128, 8], F32, kind="ExternalInput").ap()
            for p in ("q", "k", "v", "o")}
    wfT = nc.dram_tensor("wfT", [D, D], F32, kind="ExternalInput").ap()
    iden32 = nc.dram_tensor("iden32", [128, 128], F32, kind="ExternalInput").ap()
    idenb = nc.dram_tensor("idenb", [128, 128], BF16, kind="ExternalInput").ap()
    ones32 = nc.dram_tensor("ones32", [128, 1], F32, kind="ExternalInput").ap()
    masks = nc.dram_tensor("masks", [4, 128, 128], F32, kind="ExternalInput").ap()

    xout = nc.dram_tensor("outT", [NCH, 128, NT], F32, kind="ExternalOutput").ap()

    xres = {i: nc.dram_tensor(f"x{i}", [NCH, 128, NT], F32).ap() for i in range(1, 6)}
    pk_ug = nc.dram_tensor("pk_ug", [2, 2, NHC, 128, 8 * 128], BF16).ap()
    pk_d = nc.dram_tensor("pk_d", [2, NCH, 128, 24 * 128], BF16).ap()
    upraw = nc.dram_tensor("upraw", [NHC, 128, NT], F32).ap()
    gateraw = nc.dram_tensor("gateraw", [NHC, 128, NT], F32).ap()
    m2i = nc.dram_tensor("m2i", [NHC, 128, NT], I16).ap()
    prodb = nc.dram_tensor("prodb", [NHC, 128, NT], F32).ap()
    mmb = nc.dram_tensor("mmb", [NHC, 128, NT], F32).ap()
    rows = nc.dram_tensor("rows", [8, NT], F32).ap()
    dbg_sc = nc.dram_tensor("dbg_sc", [128, 16], F32).ap()
    dbg_mx = nc.dram_tensor("dbg_mx", [128, 384], F32).ap()
    dbg2 = nc.dram_tensor("dbg2", [128, 8], F32).ap()
    avh = nc.dram_tensor("avh", [8, 128, SEQ], F32).ap()
    cc_in = [nc.dram_tensor(f"cc_in{i}", [1, 2], F32) for i in range(8)]
    cc_out = [nc.dram_tensor(f"cc_out{i}", [1, 2], F32) for i in range(8)]

    named = {"x1": (xres[1], [NCH, 128, NT], F32), "x2": (xres[2], [NCH, 128, NT], F32),
             "x3": (xres[3], [NCH, 128, NT], F32), "x4": (xres[4], [NCH, 128, NT], F32),
             "x5": (xres[5], [NCH, 128, NT], F32),
             "pk_ug": (pk_ug, [2, 2, NHC, 128, 8 * 128], BF16),
             "pk_d": (pk_d, [2, NCH, 128, 24 * 128], BF16),
             "upraw": (upraw, [NHC, 128, NT], F32),
             "gateraw": (gateraw, [NHC, 128, NT], F32),
             "prodb": (prodb, [NHC, 128, NT], F32),
             "mmb": (mmb, [NHC, 128, NT], F32),
             "rows": (rows, [8, NT], F32),
             "dbg_sc": (dbg_sc, [128, 16], F32),
             "dbg_mx": (dbg_mx, [128, 384], F32),
             "dbg2": (dbg2, [128, 8], F32),
             "m2i": (m2i, [NHC, 128, NT], I16),
             "avh": (avh, [8, 128, SEQ], F32)}
    dump_outs = {nm: nc.dram_tensor("dump_" + nm, named[nm][1], named[nm][2],
                                    kind="ExternalOutput").ap() for nm in dumps}

    with ExitStack() as es:
        block = es.enter_context(nc.Block())
        S = Sched(nc, block, es)
        for w in ENG_ATTR:
            S.persist_name("bar" + w)
            S.persist_name("fn" + w)
        S.persist_name("dconst")

        c_iden = es.enter_context(nc.sbuf_tensor("c_iden", [128, 128], F32))
        c_idenb = es.enter_context(nc.sbuf_tensor("c_idenb", [128, 128], BF16))
        c_ones = es.enter_context(nc.sbuf_tensor("c_ones", [128, 1], F32))
        c_masks = es.enter_context(nc.sbuf_tensor("c_masks", [128, 4 * 128], F32))
        c_b = {p: es.enter_context(nc.sbuf_tensor(f"c_b{p}", [128, 8], F32))
               for p in ("q", "k", "v", "o")}
        c_sc = es.enter_context(nc.sbuf_tensor("c_sc", [128, 16], F32))
        rb1 = es.enter_context(nc.sbuf_tensor("rb1", [1, 2 * 512], F32))
        rb2 = es.enter_context(nc.sbuf_tensor("rb2", [1, 2 * 512], F32))
        rb3 = es.enter_context(nc.sbuf_tensor("rb3", [1, 2 * 512], F32))
        r32 = es.enter_context(nc.sbuf_tensor("r32", [32, 4 * 128], F32))
        s11 = es.enter_context(nc.sbuf_tensor("s11", [1, 2], F32))

        def y0(yq):
            for dst, src in [(c_iden, iden32), (c_idenb, idenb), (c_ones, ones32)]:
                S.inc(yq.dma_start(out=dst[:], in_=src), "dconst", 16)
            msrc = bass.AP(masks.tensor, 0, [[128, 128], [128 * 128, 4], [1, 128]])
            S.inc(yq.dma_start(out=c_masks[:].rearrange("p (m f) -> p m f", m=4),
                               in_=msrc), "dconst", 16)
            for p in ("q", "k", "v", "o"):
                S.inc(yq.dma_start(out=c_b[p][:], in_=l1_b[p]), "dconst", 16)
        S.on("Y", y0)
        tok_consts = S.tok("dconst")

        def barrier():
            for w in ENG_ATTR:
                S.on(w, (lambda w: lambda e: S.bump(e, "bar" + w))(w))
            toks = [S.tok("bar" + w) for w in ENG_ATTR]
            for w in ENG_ATTR:
                def body(e, w=w):
                    for tk in toks:
                        S.wait(e, tk)
                S.on(w, body)

        def par_max(vec_ap, out11, in_tok, tag, psb):
            def t(te):
                S.wait(te, in_tok)
                te.transpose(psb[0:1, 0:128], vec_ap, c_iden[:])
                S.sig(te, "pmt" + tag)
            def v(ve):
                S.wait(ve, S.tok("pmt" + tag))
                ve.tensor_reduce(out11, psb[0:1, 0:128], AX.X, ALU.max)
                S.sig(ve, "pmv" + tag)
            S.on("T", t)
            S.on("V", v)
            return S.tok("pmv" + tag)

        def collective_max(rnd, srcs, dst_tile, ncols, in_tok, tag):
            def g(ge):
                S.wait(ge, in_tok)
                for i in range(2):
                    s_ = srcs[min(i, len(srcs) - 1)]
                    S.inc(ge.dma_start(out=cc_in[rnd][0:1, i:i + 1], in_=s_), "dG", 16)
                S.wait(ge, S.tok("dG"))
                S.inc(ge.collective_compute(
                    "AllReduce", ALU.max, replica_groups=[list(range(8))],
                    ins=[cc_in[rnd].ap().opt()], outs=[cc_out[rnd].ap().opt()]), "cc")
                S.wait(ge, S.tok("cc"))
                rep = bass.AP(cc_out[rnd].ap().tensor, 0, [[0, 128], [1, ncols]])
                S.inc(ge.dma_start(out=dst_tile, in_=rep), "dG", 16)
                S.wait(ge, S.tok("dG"))
                S.bump(ge, "ccb" + tag)
            S.on("G", g)
            return S.tok("ccb" + tag)

        def stats_pass(xsrc, t0, T, tag, ph, psg, psh):
            xa = ph.enter_context(nc.sbuf_tensor(f"st_x_{tag}", [128, 2 * 512], F32))
            sq = ph.enter_context(nc.sbuf_tensor(f"st_q_{tag}", [128, 2 * 512], F32))
            nblk = T // 512
            slot_tok = [None, None]
            for blk in range(nblk):
                tb = t0 + blk * 512
                bsl = blk % 2
                bss = slice(bsl * 512, bsl * 512 + 512)
                for c in range(NCH):
                    sl = (blk * NCH + c) % 2
                    ss = slice(sl * 512, sl * 512 + 512)
                    def y(yq, c=c, tb=tb, sl=sl, ss=ss, st_=slot_tok[sl]):
                        S.wait(yq, st_)
                        S.inc(yq.dma_start(out=xa[:, ss], in_=xsrc[c, :, tb:tb + 512]),
                              f"d{tag}x{sl}", 16)
                    def s_(se, sl=sl, ss=ss, st_=slot_tok[sl]):
                        S.wait(se, S.tok(f"d{tag}x{sl}"))
                        S.wait(se, st_)
                        se.activation(sq[:, ss], xa[:, ss], AF.Square)
                        S.sig(se, f"{tag}sq")
                    def t(te, c=c, ss=ss):
                        S.wait(te, S.tok(f"{tag}sq"))
                        if c == 0:
                            S.wait(te, S.tok(f"{tag}ev"))
                        te.matmul(psg[0:1, 0:512], c_ones[:], xa[:, ss],
                                  start=(c == 0), stop=(c == NCH - 1))
                        te.matmul(psh[0:1, 0:512], c_ones[:], sq[:, ss],
                                  start=(c == 0), stop=(c == NCH - 1))
                        S.sig(te, f"{tag}tc")
                    S.on("Y", y)
                    S.on("S", s_)
                    S.on("T", t)
                    slot_tok[sl] = S.tok(f"{tag}tc")
                def v(ve, bss=bss, bsl=bsl):
                    S.wait(ve, S.tok(f"{tag}tc"))
                    S.wait(ve, S.tok(f"d{tag}mr{bsl}"))
                    ve.tensor_scalar(rb1[0:1, bss], psg[0:1, 0:512], 1.0 / D, 0.0,
                                     ALU.mult, ALU.add)
                    ve.tensor_scalar(rb3[0:1, bss], psh[0:1, 0:512], 1.0 / D, 0.0,
                                     ALU.mult, ALU.add)
                    S.sig(ve, f"{tag}ev")
                    ve.tensor_tensor(rb2[0:1, bss], rb1[0:1, bss], rb1[0:1, bss],
                                     ALU.mult)
                    ve.tensor_tensor(rb3[0:1, bss], rb3[0:1, bss], rb2[0:1, bss],
                                     ALU.subtract)
                    ve.tensor_scalar(rb3[0:1, bss], rb3[0:1, bss], EPS_LN, 0.0,
                                     ALU.add, ALU.add)
                    S.sig(ve, f"{tag}vv")
                def s2_(se, bss=bss):
                    S.wait(se, S.tok(f"{tag}vv"))
                    se.activation(rb2[0:1, bss], rb3[0:1, bss], AF.Sqrt)
                    S.sig(se, f"{tag}sr")
                def v3(ve, bss=bss):
                    S.wait(ve, S.tok(f"{tag}sr"))
                    ve.reciprocal(rb3[0:1, bss], rb2[0:1, bss])
                    S.fence(ve)
                    ve.tensor_copy(rb2[0:1, bss.start:bss.start + 1],
                                   rb3[0:1, bss.start:bss.start + 1])
                    S.sig(ve, f"{tag}rr")
                def y2(yq, tb=tb, bss=bss, bsl=bsl):
                    S.wait(yq, S.tok(f"{tag}rr"))
                    S.inc(yq.dma_start(out=rows[0:1, tb:tb + 512], in_=rb1[0:1, bss]),
                          f"d{tag}mr{bsl}", 16)
                    S.inc(yq.dma_start(out=rows[1:2, tb:tb + 512], in_=rb3[0:1, bss]),
                          f"d{tag}mr{bsl}", 16)
                S.on("V", v)
                S.on("S", s2_)
                S.on("V", v3)
                S.on("Y", y2)
            def yfin(yq):
                S.wait(yq, S.tok(f"d{tag}mr0"))
                S.wait(yq, S.tok(f"d{tag}mr1"))
                S.bump(yq, f"{tag}done")
            S.on("Y", yfin)
            rtok = S.tok(f"{tag}done")
            return rtok, rtok

        def bcast_row(yq, dst, row_idx, t0, T, semname):
            src = bass.AP(rows.tensor, row_idx * NT + t0, [[0, 128], [1, T]])
            return S.inc(yq.dma_start(out=dst, in_=src), semname, 16)

        def emit_dequant(ph):
            PSD = {nm: ph.enter_context(nc.psum_tensor(f"psd_{nm}", [128, 512], F32))
                   for nm in "ab"}
            swr = ph.enter_context(nc.sbuf_tensor("dq_wr", [128, HID], F32))
            sbm = ph.enter_context(nc.sbuf_tensor("dq_bm", [128, 96], F32))
            sinv = ph.enter_context(nc.sbuf_tensor("dq_inv", [128, 96], F32))
            st_ = ph.enter_context(nc.sbuf_tensor("dq_t", [128, HID], F32))
            shi = ph.enter_context(nc.sbuf_tensor("dq_hi", [128, 2 * 512], BF16))
            slo = ph.enter_context(nc.sbuf_tensor("dq_lo", [128, 2 * 512], BF16))
            jobs = [(w_up_raw, D, NHC, ("ug", 0)), (w_gate_raw, D, NHC, ("ug", 1)),
                    (w_down_raw, HID, NCH, ("d", 0))]
            tr_done = None
            v_done = None
            for raw, K, OC, dest in jobs:
                nb = K // 32
                nkc = K // 128
                for oc in range(OC):
                    def y(yq, raw=raw, oc=oc, K=K, v_done=v_done):
                        S.wait(yq, v_done)
                        S.inc(yq.dma_start(out=swr[:, 0:K],
                                           in_=raw[oc * 128:(oc + 1) * 128, :]), "dqx", 16)
                    def v(ve, K=K, nb=nb, tr_done=tr_done, wprev=v_done):
                        S.wait(ve, S.tok("dqx"))
                        S.wait(ve, tr_done)
                        S.wait(ve, wprev)
                        ve.tensor_reduce(sbm[:, 0:nb],
                                         swr[:, 0:K].rearrange("p (b k) -> p b k", k=32),
                                         AX.X, ALU.max, apply_absolute_value=True)
                        S.fence(ve)
                        ve.tensor_scalar(sbm[:, 0:nb], sbm[:, 0:nb], 1.0 / 7.0, EPS_Q,
                                         ALU.mult, ALU.max)
                        S.fence(ve)
                        ve.reciprocal(sinv[:, 0:nb], sbm[:, 0:nb])
                        S.fence(ve)
                        for b in range(nb):
                            ve.tensor_scalar(st_[:, b * 32:(b + 1) * 32],
                                             swr[:, b * 32:(b + 1) * 32],
                                             sinv[:, b:b + 1], MAGIC, ALU.mult, ALU.add)
                        S.fence(ve)
                        ve.tensor_scalar(st_[:, 0:K], st_[:, 0:K], MAGIC, 0.0,
                                         ALU.subtract, ALU.add)
                        for b in range(nb):
                            ve.tensor_scalar(st_[:, b * 32:(b + 1) * 32],
                                             st_[:, b * 32:(b + 1) * 32],
                                             sbm[:, b:b + 1], 0.0, ALU.mult, ALU.add)
                        S.fence(ve)
                        ve.tensor_copy(sbm[:, 0:1], sbm[:, 0:1])
                        S.sig(ve, "zqw")
                    S.on("Y", y)
                    S.on("V", v)
                    v_done = S.tok("zqw")
                    ng4 = (nkc + 3) // 4
                    for g4 in range(ng4):
                        ntile = min(4, nkc - g4 * 4)
                        pb = "ab"[g4 % 2]
                        off = 0 if pb == "a" else 512
                        def t(te, g4=g4, ntile=ntile, pb=pb, v_done=v_done):
                            S.wait(te, v_done)
                            S.wait(te, S.tok(f"zqpf{pb}"))
                            for i in range(ntile):
                                kc = g4 * 4 + i
                                te.transpose(PSD[pb][:, i * 128:(i + 1) * 128],
                                             st_[:, kc * 128:(kc + 1) * 128], c_iden[:])
                            S.sig(te, "zqt")
                        def sv(se, ntile=ntile, pb=pb, off=off, tcur=None):
                            S.wait(se, S.tok("zqt"))
                            S.wait(se, S.tok(f"zqhf{pb}"))
                            se.activation(shi[:, off:off + ntile * 128],
                                          PSD[pb][:, 0:ntile * 128], AF.Identity)
                            S.sig(se, f"zqh{pb}")
                        def vv(ve, ntile=ntile, pb=pb, off=off):
                            S.wait(ve, S.tok(f"zqh{pb}"))
                            ve.tensor_tensor(slo[:, off:off + ntile * 128],
                                             PSD[pb][:, 0:ntile * 128],
                                             shi[:, off:off + ntile * 128], ALU.subtract)
                            S.sig(ve, f"zql{pb}")
                        def g2(ge, g4=g4, ntile=ntile, pb=pb, off=off, dest=dest, oc=oc):
                            opb = "b" if pb == "a" else "a"
                            prev = S.tok(f"dGpk{opb}")
                            S.wait(ge, S.tok(f"zql{pb}"))
                            S.wait(ge, prev)
                            S.bump(ge, f"zqhf{opb}")
                            S.bump(ge, f"zqpf{opb}")
                            for i in range(ntile):
                                kc = g4 * 4 + i
                                for half, srcbuf in ((0, shi), (1, slo)):
                                    if dest[0] == "ug":
                                        base = (((dest[1] * 2 + half) * NHC + oc) * 128) * (8 * 128)
                                        dap = bass.AP(pk_ug.tensor, base + kc * 128,
                                                      [[8 * 128, 128], [1, 128]])
                                    else:
                                        base = ((half * NCH + oc) * 128) * (24 * 128)
                                        dap = bass.AP(pk_d.tensor, base + kc * 128,
                                                      [[24 * 128, 128], [1, 128]])
                                    S.inc(ge.dma_start(
                                        out=dap, in_=srcbuf[:, off + i * 128:off + (i + 1) * 128]),
                                        f"dGpk{pb}", 16)
                        S.on("T", t)
                        S.on("S", sv)
                        S.on("V", vv)
                        S.on("G", g2)
                    tr_done = S.tok("zqt")
            def gend(ge):
                S.wait(ge, S.tok("dGpka"))
                S.wait(ge, S.tok("dGpkb"))
            S.on("G", gend)

        def emit_ffn0(xsrc, xdst, rbase, P, ph):
            mb = ph.enter_context(nc.sbuf_tensor(f"f{P}_mb", [128, 2048], F32))
            rb = ph.enter_context(nc.sbuf_tensor(f"f{P}_rb", [128, 2048], F32))
            xt = ph.enter_context(nc.sbuf_tensor(f"f{P}_xt", [128, 2 * 2048], F32))
            t1 = ph.enter_context(nc.sbuf_tensor(f"f{P}_t1", [128, 2048], F32))
            mxs = ph.enter_context(nc.sbuf_tensor(f"f{P}_mx", [128, 18], F32))
            mxs2 = ph.enter_context(nc.sbuf_tensor(f"f{P}_mx2", [128, 384], F32))
            umx = ph.enter_context(nc.sbuf_tensor(f"f{P}_umx", [128, 2], F32))
            gv = ph.enter_context(nc.sbuf_tensor(f"f{P}_gv", [128, 2], F32))

            stsub = ExitStack()
            psg = stsub.enter_context(nc.psum_tensor(f"f{P}_psg", [128, 512], F32))
            psh = stsub.enter_context(nc.psum_tensor(f"f{P}_psh", [128, 512], F32))
            _, rtok = stats_pass(xsrc, 0, NT, f"{P}st", ph, psg, psh)
            stsub.close()
            PSF = {nm: ph.enter_context(nc.psum_tensor(f"f{P}_psb_{nm}", [128, 512], F32))
                   for nm in "abcdefgh"}
            if stage < 2:
                return

            tg = f"{P}L"
            xslot = [None, None]
            for half in range(2):
                tb = half * 2048
                bused = S.tok(f"{tg}v")
                def y(yq, tb=tb, bused=bused):
                    S.wait(yq, rtok)
                    S.wait(yq, bused)
                    bcast_row(yq, mb[:, :], 0, tb, 2048, f"d{tg}b")
                    bcast_row(yq, rb[:, :], 1, tb, 2048, f"d{tg}b")
                S.on("Y", y)
                btok = S.tok(f"d{tg}b")
                for c in range(NCH):
                    sl = c % 2
                    ss = slice(sl * 2048, sl * 2048 + 2048)
                    def y2(yq, c=c, tb=tb, sl=sl, ss=ss, xs=xslot[sl]):
                        S.wait(yq, xs)
                        S.inc(yq.dma_start(out=xt[:, ss], in_=xsrc[c, :, tb:tb + 2048]),
                              f"d{tg}x{sl}", 16)
                    def v(ve, c=c, half=half, sl=sl, ss=ss, btok=btok):
                        S.wait(ve, btok)
                        S.wait(ve, S.tok(f"d{tg}x{sl}"))
                        ve.tensor_tensor(t1[:], xt[:, ss], mb[:], ALU.subtract)
                        S.sig(ve, f"{tg}xf")
                        ve.tensor_tensor(t1[:], t1[:], rb[:], ALU.mult)
                        ve.tensor_reduce(mxs[:, half * NCH + c:half * NCH + c + 1],
                                         t1[:], AX.X, ALU.max, apply_absolute_value=True)
                        S.sig(ve, f"{tg}v")
                    S.on("Y", y2)
                    S.on("V", v)
                    xslot[sl] = S.tok(f"{tg}xf")
            def vm(ve):
                S.wait(ve, S.tok(f"{tg}v"))
                ve.tensor_reduce(mxs[:, 16:17], mxs[:, 0:16], AX.X, ALU.max)
                S.sig(ve, f"{tg}m")
            S.on("V", vm)
            pt = par_max(mxs[:, 16:17], s11[0:1, 0:1], S.tok(f"{tg}m"), tg, PSF["g"])
            ct1 = collective_max(rbase + 0, [s11[0:1, 0:1]], gv[:, 0:1], 1, pt, tg)
            def ya0(yq):
                S.wait(yq, ct1)
                mr_src = bass.AP(rows.tensor, 0 * NT, [[128, 32], [1, 128]])
                rr_src = bass.AP(rows.tensor, 1 * NT, [[128, 32], [1, 128]])
                S.inc(yq.dma_start(out=r32[0:32, 0:128], in_=mr_src), f"d{tg}mr32", 16)
                S.inc(yq.dma_start(out=r32[0:32, 128:256], in_=rr_src), f"d{tg}mr32", 16)
            def va(ve):
                S.wait(ve, ct1)
                S.wait(ve, S.tok(f"d{tg}mr32"))
                ve.tensor_scalar(c_sc[:, 0:1], gv[:, 0:1], 1.0 / QMAXA, EPS_Q,
                                 ALU.mult, ALU.max)
                S.fence(ve)
                ve.reciprocal(c_sc[:, 1:2], c_sc[:, 0:1])
                S.fence(ve)
                ve.tensor_scalar(r32[0:32, 256:384], r32[0:32, 128:256],
                                 c_sc[0:32, 1:2], 0.0, ALU.mult, ALU.add)
                S.fence(ve)
                ve.tensor_tensor(r32[0:32, 384:512], r32[0:32, 0:128],
                                 r32[0:32, 256:384], ALU.mult)
                S.sig(ve, f"{tg}ab")
            def ya(yq):
                S.wait(yq, S.tok(f"{tg}ab"))
                a_dst = bass.AP(rows.tensor, 2 * NT, [[128, 32], [1, 128]])
                b_dst = bass.AP(rows.tensor, 3 * NT, [[128, 32], [1, 128]])
                S.inc(yq.dma_start(out=a_dst, in_=r32[0:32, 256:384]), f"d{tg}ab", 16)
                S.inc(yq.dma_start(out=b_dst, in_=r32[0:32, 384:512]), f"d{tg}ab", 16)
            S.on("Y", ya0)
            S.on("V", va)
            S.on("Y", ya)
            abtok = S.tok(f"d{tg}ab")

            if stage < 3:
                return
            qt = f"{P}Q"
            ph1 = ExitStack()
            kh = ph1.enter_context(nc.sbuf_tensor(f"f{P}_kh", [128, NCH * 2048], BF16))
            kl = ph1.enter_context(nc.sbuf_tensor(f"f{P}_kl", [128, NCH * 2048], BF16))
            wsl = [ph1.enter_context(nc.sbuf_tensor(f"f{P}_w{i}", [128, 1024], BF16))
                   for i in range(8)]
            evb = [ph1.enter_context(nc.sbuf_tensor(f"f{P}_ev{i}", [128, 512], F32))
                   for i in range(4)]
            UPB, GAB = "abcd", "efgh"
            xslot2 = [None, None]
            wslot_free = [None, None]
            bank_free = {b: None for b in "abcdefgh"}
            ev_free = [None, None]
            mm_half_tok = None
            for half in range(2):
                tb = half * 2048
                bused = S.tok(f"{qt}k")
                def y(yq, tb=tb, bused=bused):
                    S.wait(yq, abtok)
                    S.wait(yq, bused)
                    bcast_row(yq, mb[:, :], 2, tb, 2048, f"d{qt}b")
                    bcast_row(yq, rb[:, :], 3, tb, 2048, f"d{qt}b")
                S.on("Y", y)
                btok = S.tok(f"d{qt}b")
                for c in range(NCH):
                    sl = c % 2
                    ss = slice(sl * 2048, sl * 2048 + 2048)
                    def y2(yq, c=c, tb=tb, sl=sl, ss=ss, xs=xslot2[sl]):
                        S.wait(yq, xs)
                        S.inc(yq.dma_start(out=xt[:, ss],
                                           in_=xsrc[c, :, tb:tb + 2048]), f"d{qt}x{sl}", 16)
                    def v(ve, c=c, sl=sl, ss=ss, half=half, btok=btok,
                          mht=mm_half_tok):
                        S.wait(ve, btok)
                        S.wait(ve, S.tok(f"d{qt}x{sl}"))
                        if half == 1:
                            S.wait(ve, mht)
                        ve.tensor_tensor(t1[:], xt[:, ss], mb[:], ALU.mult)
                        S.sig(ve, f"{qt}xf")
                        ve.tensor_tensor(t1[:], t1[:], rb[:], ALU.subtract)
                        ve.tensor_scalar(t1[:], t1[:], MAGIC, MAGIC, ALU.add, ALU.subtract)
                        cs = slice(c * 2048, (c + 1) * 2048)
                        ve.tensor_copy(kh[:, cs], t1[:])
                        ve.tensor_tensor(kl[:, cs], t1[:], kh[:, cs], ALU.subtract)
                        S.sig(ve, f"{qt}k")
                    S.on("Y", y2)
                    S.on("V", v)
                    xslot2[sl] = S.tok(f"{qt}xf")
                ktok = S.tok(f"{qt}k")
                for hc in range(NHC):
                    ws = hc % 2
                    def yw(yq, hc=hc, ws=ws, wf=wslot_free[ws]):
                        S.wait(yq, wf)
                        for j, (var, hl) in enumerate(((0, 0), (0, 1), (1, 0), (1, 1))):
                            S.inc(yq.dma_start(out=wsl[ws * 4 + j][:],
                                               in_=pk_ug[var, hl, hc]), f"d{qt}w{ws}", 16)
                    S.on("Y", yw)
                    wtok = S.tok(f"d{qt}w{ws}")
                    for tt in range(4):
                        pu, pg = UPB[tt], GAB[tt]
                        def t(te, hc=hc, tt=tt, pu=pu, pg=pg, ws=ws, ktok=ktok, wtok=wtok,
                              bfu=bank_free[pu], bfg=bank_free[pg]):
                            S.wait(te, ktok)
                            S.wait(te, wtok)
                            S.wait(te, bfu)
                            S.wait(te, bfg)
                            for var, pb in ((0, pu), (1, pg)):
                                whi = wsl[ws * 4 + var * 2]
                                wlo = wsl[ws * 4 + var * 2 + 1]
                                for kc in range(NCH):
                                    rs = slice(kc * 2048 + tt * 512, kc * 2048 + tt * 512 + 512)
                                    wk = slice(kc * 128, (kc + 1) * 128)
                                    te.matmul(PSF[pb][:, 0:512], whi[:, wk], kh[:, rs],
                                              start=(kc == 0), stop=False)
                                    te.matmul(PSF[pb][:, 0:512], whi[:, wk], kl[:, rs],
                                              start=False, stop=False)
                                    te.matmul(PSF[pb][:, 0:512], wlo[:, wk], kh[:, rs],
                                              start=False, stop=(kc == NCH - 1))
                            S.sig(te, f"{qt}mm")
                        S.on("T", t)
                        mmtok = S.tok(f"{qt}mm")
                        idx = (half * NHC + hc) * 4 + tt
                        def v2(ve, tt=tt, pu=pu, pg=pg, mmtok=mmtok, ef=ev_free[tt % 2],
                               idx=idx):
                            S.wait(ve, mmtok)
                            S.wait(ve, ef)
                            e0, e1 = evb[tt % 2 * 2], evb[tt % 2 * 2 + 1]
                            ve.tensor_scalar(e0[:], PSF[pu][:, 0:512], c_sc[:, 0:1], 0.0,
                                             ALU.mult, ALU.add)
                            S.sig(ve, f"{qt}ec")
                            ve.tensor_scalar(e1[:], PSF[pg][:, 0:512], c_sc[:, 0:1], 0.0,
                                             ALU.mult, ALU.add)
                            S.sig(ve, f"{qt}ec")
                            ve.tensor_reduce(mxs2[:, idx:idx + 1], PSF[pu][:, 0:512], AX.X,
                                             ALU.max, apply_absolute_value=True)
                            S.sig(ve, f"{qt}pv")
                            ve.tensor_reduce(mxs2[:, 192 + idx:193 + idx], PSF[pg][:, 0:512],
                                             AX.X, ALU.max, apply_absolute_value=True)
                            S.sig(ve, f"{qt}pv")
                        S.on("V", v2)
                        pvtok = S.tok(f"{qt}pv")
                        bank_free[pu] = pvtok
                        bank_free[pg] = pvtok
                        ectok = S.tok(f"{qt}ec")
                        def y3(yq, hc=hc, tt=tt, half=half, ectok=ectok):
                            S.wait(yq, ectok)
                            e0, e1 = evb[tt % 2 * 2], evb[tt % 2 * 2 + 1]
                            off = half * 2048 + tt * 512
                            S.inc(yq.dma_start(out=upraw[hc, :, off:off + 512],
                                               in_=e0[:]), f"d{qt}s{tt % 2}", 16)
                            S.inc(yq.dma_start(out=gateraw[hc, :, off:off + 512],
                                               in_=e1[:]), f"d{qt}s{tt % 2}", 16)
                        S.on("Y", y3)
                        ev_free[tt % 2] = S.tok(f"d{qt}s{tt % 2}")
                    wslot_free[ws] = S.tok(f"{qt}mm")
                mm_half_tok = S.tok(f"{qt}mm")
            if stage < 4:
                return
            def yrs(yq):
                S.wait(yq, S.tok(f"d{qt}s0"))
                S.wait(yq, S.tok(f"d{qt}s1"))
                S.bump(yq, f"{qt}rawdone")
            S.on("Y", yrs)
            rawtok = S.tok(f"{qt}rawdone")
            allpv = S.tok(f"{qt}pv")
            def vfin(ve):
                S.wait(ve, allpv)
                ve.tensor_reduce(umx[:, 0:1], mxs2[:, 0:192], AX.X, ALU.max)
                S.sig(ve, f"{qt}fin")
                ve.tensor_reduce(umx[:, 1:2], mxs2[:, 192:384], AX.X, ALU.max)
                S.sig(ve, f"{qt}fin")
            S.on("V", vfin)
            fintok = S.tok(f"{qt}fin")
            def tq(te):
                S.wait(te, fintok)
                te.transpose(PSF["a"][0:1, 0:128], umx[:, 0:1], c_iden[:])
                te.transpose(PSF["a"][0:1, 128:256], umx[:, 1:2], c_iden[:])
                S.sig(te, f"{qt}tp")
            def vq2(ve):
                S.wait(ve, S.tok(f"{qt}tp"))
                ve.tensor_reduce(s11[0:1, 0:1], PSF["a"][0:1, 0:128], AX.X, ALU.max)
                S.sig(ve, f"{qt}pm")
                ve.tensor_reduce(s11[0:1, 1:2], PSF["a"][0:1, 128:256], AX.X, ALU.max)
                S.sig(ve, f"{qt}pm")
            S.on("T", tq)
            S.on("V", vq2)
            ct23 = collective_max(rbase + 1, [s11[0:1, 0:1], s11[0:1, 1:2]],
                                  gv[:, 0:2], 2, S.tok(f"{qt}pm"), qt)
            def vsc(ve):
                S.wait(ve, ct23)
                ve.tensor_scalar(mxs[:, 0:2], gv[:, 0:2], c_sc[:, 0:1], 0.0,
                                 ALU.mult, ALU.add)
                S.fence(ve)
                ve.tensor_scalar(c_sc[:, 4:6], mxs[:, 0:2], 1.0 / QMAXA, EPS_Q,
                                 ALU.mult, ALU.max)
                S.fence(ve)
                ve.reciprocal(c_sc[:, 2:3], c_sc[:, 4:5])
                ve.reciprocal(c_sc[:, 3:4], c_sc[:, 5:6])
                ve.memset(umx[:, 0:1], 0.0)
                S.fence(ve)
                ve.tensor_copy(c_sc[:, 14:15], c_sc[:, 2:3])
                S.sig(ve, f"{qt}sc")
            S.on("V", vsc)
            sctok = S.tok(f"{qt}sc")

            if stage < 5:
                return
            ph1.close()
            at = f"{P}A"
            a_up = ph.enter_context(nc.sbuf_tensor(f"f{P}_aup", [128, 4 * 512], F32))
            a_ga = ph.enter_context(nc.sbuf_tensor(f"f{P}_aga", [128, 4 * 512], F32))
            a_t = ph.enter_context(nc.sbuf_tensor(f"f{P}_atA", [128, 4 * 512], F32))
            a_t2 = ph.enter_context(nc.sbuf_tensor(f"f{P}_atB", [128, 4 * 512], F32))
            a_i16 = ph.enter_context(nc.sbuf_tensor(f"f{P}_i16", [128, 4 * 512], I16))
            a_pr = ph.enter_context(nc.sbuf_tensor(f"f{P}_apr", [128, 4 * 512], F32))
            ntile = NHC * NT // 512
            in_free = [None] * 4
            out_free = [None] * 4
            for j in range(ntile):
                hc, off = divmod(j * 512, NT)
                sl = j % 4
                ss = slice(sl * 512, sl * 512 + 512)
                def y(yq, hc=hc, off=off, sl=sl, ss=ss, if_=in_free[sl]):
                    S.wait(yq, sctok)
                    S.wait(yq, rawtok)
                    S.wait(yq, if_)
                    S.inc(yq.dma_start(out=a_up[:, ss], in_=upraw[hc, :, off:off + 512]),
                          f"d{at}i", 16)
                    S.inc(yq.dma_start(out=a_ga[:, ss], in_=gateraw[hc, :, off:off + 512]),
                          f"d{at}i", 16)
                def v(ve, sl=sl, ss=ss, of_=out_free[sl]):
                    S.wait(ve, S.tok(f"d{at}i"))
                    S.wait(ve, sctok)
                    S.wait(ve, of_)
                    ve.tensor_scalar(a_t[:, ss], a_up[:, ss], c_sc[:, 2:3], MAGIC,
                                     ALU.mult, ALU.add)
                    S.fence(ve)
                    ve.tensor_scalar(a_t[:, ss], a_t[:, ss], MAGIC, 0.0,
                                     ALU.subtract, ALU.add)
                    ve.tensor_copy(a_i16[:, ss], a_t[:, ss])
                    S.fence(ve)
                    ve.tensor_scalar(a_t[:, ss], a_ga[:, ss], c_sc[:, 3:4], MAGIC,
                                     ALU.mult, ALU.add)
                    iftok = S.sig(ve, f"{at}if")
                    S.wait(ve, iftok)
                    ve.tensor_scalar(a_t[:, ss], a_t[:, ss], MAGIC, 0.0,
                                     ALU.subtract, ALU.add)
                    S.sig(ve, f"{at}gq")
                def s(se, sl=sl, ss=ss):
                    S.wait(se, S.tok(f"{at}gq"))
                    se.activation(a_t2[:, ss], a_t[:, ss], AF.Sigmoid,
                                  scale=c_sc[:, 5:6])
                    S.sig(se, f"{at}sg")
                def v2(ve, sl=sl, ss=ss, j=j):
                    S.wait(ve, S.tok(f"{at}sg"))
                    ve.tensor_scalar(a_t2[:, ss], a_t2[:, ss], 65536.0, MAGIC,
                                     ALU.mult, ALU.add)
                    ve.tensor_scalar(a_t2[:, ss], a_t2[:, ss], MAGIC, 65535.0,
                                     ALU.subtract, ALU.min)
                    ve.tensor_scalar(a_t2[:, ss], a_t2[:, ss], 1.0 / 65536.0, 0.0,
                                     ALU.mult, ALU.add)
                    ve.tensor_tensor(a_pr[:, ss], a_t[:, ss], a_t2[:, ss], ALU.mult)
                    S.sig(ve, f"{at}pr")
                    ve.tensor_reduce(mxs2[:, j:j + 1], a_pr[:, ss], AX.X, ALU.max,
                                     apply_absolute_value=True)
                    S.sig(ve, f"{at}prm")
                def y2(yq, hc=hc, off=off, sl=sl, ss=ss):
                    S.wait(yq, S.tok(f"{at}pr"))
                    S.inc(yq.dma_start(out=m2i[hc, :, off:off + 512], in_=a_i16[:, ss]),
                          f"d{at}o", 16)
                    S.inc(yq.dma_start(out=prodb[hc, :, off:off + 512], in_=a_pr[:, ss]),
                          f"d{at}o", 16)
                S.on("Y", y)
                S.on("V", v)
                S.on("S", s)
                S.on("V", v2)
                S.on("Y", y2)
                in_free[sl] = S.tok(f"{at}if")
                out_free[sl] = S.tok(f"d{at}o")
            def vfin4(ve):
                S.wait(ve, S.tok(f"{at}prm"))
                ve.tensor_reduce(umx[:, 0:1], mxs2[:, 0:192], AX.X, ALU.max)
                S.sig(ve, f"{at}fin")
            S.on("V", vfin4)
            pt4 = par_max(umx[:, 0:1], s11[0:1, 0:1], S.tok(f"{at}fin"), at, PSF["g"])
            ct4 = collective_max(rbase + 2, [s11[0:1, 0:1]], gv[:, 0:1], 1, pt4, at)
            def vs4(ve):
                S.wait(ve, ct4)
                ve.tensor_scalar(mxs[:, 0:1], gv[:, 0:1], c_sc[:, 5:6], 0.0,
                                 ALU.mult, ALU.add)
                S.fence(ve)
                ve.tensor_scalar(c_sc[:, 6:7], mxs[:, 0:1], 1.0 / QMAXA, EPS_Q,
                                 ALU.mult, ALU.max)
                S.fence(ve)
                ve.reciprocal(c_sc[:, 13:14], c_sc[:, 6:7])
                S.fence(ve)
                ve.tensor_tensor(c_sc[:, 7:8], c_sc[:, 13:14], c_sc[:, 5:6], ALU.mult)
                S.sig(ve, f"{at}s4")
            S.on("V", vs4)
            s4tok = S.tok(f"{at}s4")
            def yod(yq):
                S.wait(yq, S.tok(f"d{at}o"))
                S.bump(yq, f"{at}odone")
            S.on("Y", yod)
            aodone = S.tok(f"{at}odone")

            if stage < 6:
                return
            bt = f"{P}B"
            in_free = [None] * 4
            out_free = [None] * 4
            for j in range(ntile):
                hc, off = divmod(j * 512, NT)
                sl = j % 4
                ss = slice(sl * 512, sl * 512 + 512)
                def y(yq, hc=hc, off=off, sl=sl, ss=ss, if_=in_free[sl]):
                    S.wait(yq, s4tok)
                    S.wait(yq, aodone)
                    S.wait(yq, if_)
                    S.inc(yq.dma_start(out=a_up[:, ss], in_=prodb[hc, :, off:off + 512]),
                          f"d{bt}i", 16)
                    S.inc(yq.dma_start(out=a_i16[:, ss], in_=m2i[hc, :, off:off + 512]),
                          f"d{bt}i", 16)
                def v(ve, sl=sl, ss=ss, of_=out_free[sl], j=j):
                    S.wait(ve, S.tok(f"d{bt}i"))
                    S.wait(ve, s4tok)
                    S.wait(ve, of_)
                    ve.tensor_scalar(a_t[:, ss], a_up[:, ss], c_sc[:, 7:8], MAGIC,
                                     ALU.mult, ALU.add)
                    S.fence(ve)
                    ve.tensor_scalar(a_t[:, ss], a_t[:, ss], MAGIC, 0.0,
                                     ALU.subtract, ALU.add)
                    ve.tensor_copy(a_t2[:, ss], a_i16[:, ss])
                    S.sig(ve, f"{bt}if")
                    ve.tensor_tensor(a_pr[:, ss], a_t[:, ss], a_t2[:, ss], ALU.mult)
                    S.sig(ve, f"{bt}pr")
                    ve.tensor_reduce(mxs2[:, j:j + 1], a_pr[:, ss], AX.X, ALU.max,
                                     apply_absolute_value=True)
                    S.sig(ve, f"{bt}prm")
                def y2(yq, hc=hc, off=off, sl=sl, ss=ss):
                    S.wait(yq, S.tok(f"{bt}pr"))
                    S.inc(yq.dma_start(out=mmb[hc, :, off:off + 512], in_=a_pr[:, ss]),
                          f"d{bt}o", 16)
                S.on("Y", y)
                S.on("V", v)
                S.on("Y", y2)
                in_free[sl] = S.tok(f"{bt}if")
                out_free[sl] = S.tok(f"d{bt}o")
            def vfin5(ve):
                S.wait(ve, S.tok(f"{bt}prm"))
                ve.tensor_reduce(umx[:, 0:1], mxs2[:, 0:192], AX.X, ALU.max)
                S.sig(ve, f"{bt}fin")
            S.on("V", vfin5)
            pt5 = par_max(umx[:, 0:1], s11[0:1, 0:1], S.tok(f"{bt}fin"), bt, PSF["g"])
            ct5 = collective_max(rbase + 3, [s11[0:1, 0:1]], gv[:, 0:1], 1, pt5, bt)
            def vs5(ve):
                S.wait(ve, ct5)
                ve.tensor_tensor(c_sc[:, 8:9], c_sc[:, 6:7], c_sc[:, 4:5], ALU.mult)
                S.fence(ve)
                ve.tensor_scalar(mxs[:, 0:1], gv[:, 0:1], c_sc[:, 8:9], 0.0,
                                 ALU.mult, ALU.add)
                S.fence(ve)
                ve.tensor_scalar(c_sc[:, 9:10], mxs[:, 0:1], 1.0 / QMAXA, EPS_Q,
                                 ALU.mult, ALU.max)
                S.fence(ve)
                ve.reciprocal(c_sc[:, 10:11], c_sc[:, 9:10])
                S.fence(ve)
                ve.tensor_tensor(c_sc[:, 11:12], c_sc[:, 8:9], c_sc[:, 10:11], ALU.mult)
                ve.tensor_scalar(c_sc[:, 12:13], c_sc[:, 9:10], 0.5, 0.0, ALU.mult, ALU.add)
                S.sig(ve, f"{bt}s5")
            S.on("V", vs5)
            s5tok = S.tok(f"{bt}s5")
            def yod2(yq):
                S.wait(yq, S.tok(f"d{bt}o"))
                S.bump(yq, f"{bt}odone")
            S.on("Y", yod2)
            bodone = S.tok(f"{bt}odone")

            if stage < 7:
                return
            dt = f"{P}D"
            in_free = [None, None]
            m5h = ph.enter_context(nc.sbuf_tensor(f"f{P}_m5h", [128, NHC * 512], BF16))
            m5l = ph.enter_context(nc.sbuf_tensor(f"f{P}_m5l", [128, NHC * 512], BF16))
            dwb = [ph.enter_context(nc.sbuf_tensor(f"f{P}_dw{i}", [128, 24 * 128], BF16))
                   for i in range(4)]
            w_free = [None, None]
            bank_free = {"a": None, "b": None}
            x_free = {"a": None, "b": None}
            y_free = {"a": None, "b": None}
            mm_tt_tok = None
            for tt in range(NT // 512):
                off = tt * 512
                for hc in range(NHC):
                    sl = hc % 2
                    ss = slice(sl * 512, sl * 512 + 512)
                    def y(yq, hc=hc, off=off, sl=sl, ss=ss, if_=in_free[sl]):
                        S.wait(yq, s5tok)
                        S.wait(yq, bodone)
                        S.wait(yq, if_)
                        S.inc(yq.dma_start(out=a_up[:, ss], in_=mmb[hc, :, off:off + 512]),
                              f"d{dt}i{sl}", 16)
                    def v(ve, hc=hc, sl=sl, ss=ss, tt=tt, mtt=mm_tt_tok):
                        S.wait(ve, S.tok(f"d{dt}i{sl}"))
                        S.wait(ve, s5tok)
                        if tt > 0 and hc == 0:
                            S.wait(ve, mtt)
                        ve.tensor_scalar(a_t[:, ss], a_up[:, ss], c_sc[:, 11:12], MAGIC,
                                         ALU.mult, ALU.add)
                        iftok = S.sig(ve, f"{dt}if")
                        S.wait(ve, iftok)
                        ve.tensor_scalar(a_t[:, ss], a_t[:, ss], MAGIC, 0.0,
                                         ALU.subtract, ALU.add)
                        hs = slice(hc * 512, (hc + 1) * 512)
                        ve.tensor_copy(m5h[:, hs], a_t[:, ss])
                        ve.tensor_tensor(m5l[:, hs], a_t[:, ss], m5h[:, hs], ALU.subtract)
                        S.sig(ve, f"{dt}m5")
                    S.on("Y", y)
                    S.on("V", v)
                    in_free[sl] = S.tok(f"{dt}if")
                m5tok = S.tok(f"{dt}m5")
                for dc in range(NCH):
                    ws = dc % 2
                    pb = "ab"[dc % 2]
                    def yw(yq, dc=dc, ws=ws, wf=w_free[ws]):
                        S.wait(yq, wf)
                        for j, hl in enumerate((0, 1)):
                            S.inc(yq.dma_start(out=dwb[ws * 2 + j][:], in_=pk_d[hl, dc]),
                                  f"d{dt}w{ws}", 16)
                    S.on("Y", yw)
                    wtok = S.tok(f"d{dt}w{ws}")
                    def t(te, dc=dc, ws=ws, pb=pb, m5tok=m5tok, wtok=wtok,
                          bf=bank_free[pb]):
                        S.wait(te, m5tok)
                        S.wait(te, wtok)
                        S.wait(te, bf)
                        whi, wlo = dwb[ws * 2], dwb[ws * 2 + 1]
                        for kc in range(NHC):
                            ks = slice(kc * 512, kc * 512 + 512)
                            wk = slice(kc * 128, (kc + 1) * 128)
                            te.matmul(PSF[pb][:, 0:512], whi[:, wk], m5h[:, ks],
                                      start=(kc == 0), stop=False)
                            te.matmul(PSF[pb][:, 0:512], whi[:, wk], m5l[:, ks],
                                      start=False, stop=False)
                            te.matmul(PSF[pb][:, 0:512], wlo[:, wk], m5h[:, ks],
                                      start=False, stop=(kc == NHC - 1))
                        S.sig(te, f"{dt}mm")
                    S.on("T", t)
                    w_free[ws] = S.tok(f"{dt}mm")
                    mmtok = S.tok(f"{dt}mm")
                    def y2(yq, dc=dc, off=off, pb=pb, xf=x_free[pb]):
                        S.wait(yq, xf)
                        xoff = 0 if pb == "a" else 512
                        S.inc(yq.dma_start(out=xt[:, xoff:xoff + 512],
                                           in_=xsrc[dc, :, off:off + 512]), f"d{dt}x{pb}", 16)
                    def v2(ve, dc=dc, off=off, pb=pb, mmtok=mmtok, yf=y_free[pb]):
                        S.wait(ve, mmtok)
                        S.wait(ve, S.tok(f"d{dt}x{pb}"))
                        S.wait(ve, yf)
                        xoff = 0 if pb == "a" else 512
                        ysl = t1[:, 0:512] if pb == "a" else t1[:, 512:1024]
                        ve.tensor_scalar(ysl, PSF[pb][:, 0:512], c_sc[:, 12:13], 0.0,
                                         ALU.mult, ALU.add)
                        pvt = S.sig(ve, f"{dt}pv")
                        S.wait(ve, pvt)
                        ve.tensor_tensor(ysl, ysl, xt[:, xoff:xoff + 512], ALU.add)
                        S.sig(ve, f"{dt}yv")
                    S.on("Y", y2)
                    S.on("V", v2)
                    bank_free[pb] = S.tok(f"{dt}pv")
                    x_free[pb] = S.tok(f"{dt}yv")
                    yvtok = S.tok(f"{dt}yv")
                    def y3(yq, dc=dc, off=off, pb=pb, yvtok=yvtok):
                        S.wait(yq, yvtok)
                        ysl = t1[:, 0:512] if pb == "a" else t1[:, 512:1024]
                        S.inc(yq.dma_start(out=xdst[dc, :, off:off + 512], in_=ysl),
                              f"d{dt}o", 16)
                    S.on("Y", y3)
                    y_free[pb] = S.tok(f"d{dt}o")
                mm_tt_tok = S.tok(f"{dt}mm")
            def yend(yq):
                S.wait(yq, S.tok(f"d{dt}o"))
            S.on("Y", yend)


        def attn_tiles(layer, qi_g):
            if layer == 1 or qi_g <= 3:
                return [(0, 0, qi_g + 1)], {qi_g: 0}
            if qi_g == 4:
                return [(0, 0, 5)], {0: 2, 4: 0}
            return [(0, qi_g - 4, 5), (5, 0, 1)], {0: 1, 4: 0, 5: 3}

        def emit_attn(xsrc, xdst, layer, P, ph):
            L1 = (layer == 1)
            wsrc = l1_w if L1 else l0_w
            stsub = ExitStack()
            psg = stsub.enter_context(nc.psum_tensor(f"a{P}_psg", [128, 512], F32))
            psh = stsub.enter_context(nc.psum_tensor(f"a{P}_psh", [128, 512], F32))
            _, rtok = stats_pass(xsrc, 0, NT, f"{P}st", ph, psg, psh)
            stsub.close()
            ps_sc = ph.enter_context(nc.psum_tensor(f"a{P}_sc", [128, 2048], F32))
            ps_tp = ph.enter_context(nc.psum_tensor(f"a{P}_tp", [128, 512], F32))
            ps_av = ph.enter_context(nc.psum_tensor(f"a{P}_av", [128, 512], F32))
            pj = [ph.enter_context(nc.psum_tensor(f"a{P}_pj{i}", [128, 512], F32))
                  for i in range(2)]
            mb = ph.enter_context(nc.sbuf_tensor(f"a{P}_mb", [128, 2048], F32))
            rb = ph.enter_context(nc.sbuf_tensor(f"a{P}_rb", [128, 2048], F32))
            lnh = ph.enter_context(nc.sbuf_tensor(f"a{P}_lnh", [128, NCH * 2048], BF16))
            lnl = ph.enter_context(nc.sbuf_tensor(f"a{P}_lnl", [128, NCH * 2048], BF16))
            xt = ph.enter_context(nc.sbuf_tensor(f"a{P}_xt", [128, 2048], F32))
            st32 = ph.enter_context(nc.sbuf_tensor(f"a{P}_st", [128, 2048], F32))
            dall = ph.enter_context(nc.sbuf_tensor(f"a{P}_da", [128, 16], F32))
            biasv = ph.enter_context(nc.sbuf_tensor(f"a{P}_bv", [128, 8], F32))
            xprev = [None]
            for s_ in range(2):
                t0 = s_ * 2048
                Q = f"{P}s{s_}"
                prev_omm = S.tok(f"{P}s{s_-1}omm") if s_ else None
                def yb(yq, t0=t0, lc=S.tok(f"{P}alnc")):
                    S.wait(yq, rtok)
                    if lc[1] > 0:
                        S.wait(yq, lc)
                    bcast_row(yq, mb[:, :], 0, t0, 2048, f"d{Q}b")
                    bcast_row(yq, rb[:, :], 1, t0, 2048, f"d{Q}b")
                S.on("Y", yb)
                btok = S.tok(f"d{Q}b")
                for c in range(NCH):
                    def y2(yq, c=c, t0=t0, xs=xprev[0]):
                        S.wait(yq, xs)
                        S.inc(yq.dma_start(out=xt[:, :], in_=xsrc[c, :, t0:t0 + 2048]),
                              f"d{Q}x", 16)
                    def v(ve, c=c, btok=btok, xd=None):
                        S.wait(ve, btok)
                        S.wait(ve, S.tok(f"d{Q}x"))
                        ve.tensor_tensor(st32[:], xt[:, :], mb[:], ALU.subtract)
                        S.sig(ve, f"{P}axf")
                        ve.tensor_tensor(st32[:], st32[:], rb[:], ALU.mult)
                        cs = slice(c * 2048, (c + 1) * 2048)
                        ve.tensor_copy(lnh[:, cs], st32[:])
                        ve.tensor_tensor(lnl[:, cs], st32[:], lnh[:, cs], ALU.subtract)
                        S.sig(ve, f"{P}alnc")
                    S.on("Y", y2)
                    S.on("V", v)
                    xprev[0] = S.tok(f"{P}axf")
                lntok = S.tok(f"{P}alnc")
                subA = ExitStack()
                w32 = subA.enter_context(nc.sbuf_tensor(f"a{P}{s_}_w32", [128, 1024], F32))
                whi = subA.enter_context(nc.sbuf_tensor(f"a{P}{s_}_whi", [128, 1024], BF16))
                wlo = subA.enter_context(nc.sbuf_tensor(f"a{P}{s_}_wlo", [128, 1024], BF16))
                qh = subA.enter_context(nc.sbuf_tensor(f"a{P}{s_}_qh", [128, 2048], BF16))
                ql = subA.enter_context(nc.sbuf_tensor(f"a{P}{s_}_ql", [128, 2048], BF16))
                kh2 = subA.enter_context(nc.sbuf_tensor(f"a{P}{s_}_kh", [128, 2048], BF16))
                kl2 = subA.enter_context(nc.sbuf_tensor(f"a{P}{s_}_kl", [128, 2048], BF16))
                v32 = subA.enter_context(nc.sbuf_tensor(f"a{P}{s_}_v32", [128, 2048], F32))
                vT = subA.enter_context(nc.sbuf_tensor(f"a{P}{s_}_vT", [128, 2048], F32))
                p32 = subA.enter_context(nc.sbuf_tensor(f"a{P}{s_}_p32", [128, 2048], F32))
                pT = subA.enter_context(nc.sbuf_tensor(f"a{P}{s_}_pT", [128, 2048], F32))
                av2 = subA.enter_context(nc.sbuf_tensor(f"a{P}{s_}_av2", [64, 2048], F32))
                ivd = subA.enter_context(nc.sbuf_tensor(f"a{P}{s_}_ivd", [64, 2048], F32))
                exp_par = [None, None]
                exp_full_box = [None]
                for hg in range(8):
                    lastqk = S.tok(f"{Q}qk")
                    lastav = S.tok(f"{Q}avmm")
                    for pi, pname in enumerate("qkv"):
                        def yw(yq, pname=pname, hg=hg, pm=S.tok(f"{Q}pmm"),
                               prev_omm=prev_omm, first=(hg == 0 and pi == 0)):
                            S.wait(yq, pm)
                            if first and prev_omm is not None:
                                S.wait(yq, prev_omm)
                            src = bass.AP(wsrc[pname].tensor, hg * 128,
                                          [[1024, 128], [128 * 1024, 8], [1, 128]])
                            S.inc(yq.dma_start(
                                out=w32[:].rearrange("p (k f) -> p k f", k=8), in_=src),
                                f"d{Q}w", 16)
                        def vw(ve, wtok=None):
                            S.wait(ve, S.tok(f"d{Q}w"))
                            ve.tensor_copy(whi[:], w32[:])
                            ve.tensor_tensor(wlo[:], w32[:], whi[:], ALU.subtract)
                            S.sig(ve, f"{Q}wsp")
                        S.on("Y", yw)
                        S.on("V", vw)
                        wsptok = S.tok(f"{Q}wsp")
                        for tq in range(4):
                            pjb = pj[tq % 2]
                            def t(te, tq=tq, pjb=pjb, wsptok=wsptok, lntok=lntok,
                                  pf=S.tok(f"{Q}pev"), pi=pi, lastqk=lastqk, lastav=lastav):
                                S.wait(te, wsptok)
                                S.wait(te, lntok)
                                if pf[1] >= 2:
                                    S.wait(te, (pf[0], pf[1] - 1))
                                if pi < 2:
                                    S.wait(te, lastqk)
                                else:
                                    S.wait(te, lastav)
                                for kc in range(NCH):
                                    rs = slice(kc * 2048 + tq * 512, kc * 2048 + tq * 512 + 512)
                                    wk = slice(kc * 128, (kc + 1) * 128)
                                    te.matmul(pjb[:, 0:512], whi[:, wk], lnh[:, rs],
                                              start=(kc == 0), stop=False)
                                    te.matmul(pjb[:, 0:512], wlo[:, wk], lnh[:, rs],
                                              start=False, stop=False)
                                    te.matmul(pjb[:, 0:512], whi[:, wk], lnl[:, rs],
                                              start=False, stop=(kc == NCH - 1))
                                S.sig(te, f"{Q}pmm")
                            S.on("T", t)
                            mmtok = S.tok(f"{Q}pmm")
                            def sev(se, tq=tq, pjb=pjb, pname=pname, hg=hg, mmtok=mmtok,
                                    sf=S.tok(f"{Q}spl")):
                                S.wait(se, mmtok)
                                if pname != "v":
                                    S.wait(se, sf)
                                dst = (v32 if pname == "v" else st32)[:, tq * 512:(tq + 1) * 512]
                                if L1:
                                    se.activation(dst, pjb[:, 0:512], AF.Identity,
                                                  bias=c_b[pname][:, hg:hg + 1])
                                else:
                                    se.activation(dst, pjb[:, 0:512], AF.Identity)
                                S.sig(se, f"{Q}pev")
                            S.on("S", sev)
                        evtok = S.tok(f"{Q}pev")
                        if pname in ("q", "k"):
                            dsth, dstl = (qh, ql) if pname == "q" else (kh2, kl2)
                            def vsp(ve, dsth=dsth, dstl=dstl, evtok=evtok):
                                S.wait(ve, evtok)
                                ve.tensor_copy(dsth[:], st32[:])
                                ve.tensor_tensor(dstl[:], st32[:], dsth[:], ALU.subtract)
                                S.sig(ve, f"{Q}spl")
                            S.on("V", vsp)
                        else:
                            for g4 in range(4):
                                def tv(te, g4=g4, evtok=evtok, tf=S.tok(f"{Q}tpf")):
                                    S.wait(te, evtok)
                                    S.wait(te, tf)
                                    for i in range(4):
                                        kt = g4 * 4 + i
                                        te.transpose(ps_tp[:, i * 128:(i + 1) * 128],
                                                     v32[:, kt * 128:(kt + 1) * 128],
                                                     c_iden[:])
                                    S.sig(te, f"{Q}vtt")
                                def vv2(ve, g4=g4, vt=S.tok(f"{Q}vtt")):
                                    S.wait(ve, (f"{Q}vtt", vt[1] + 1))
                                    ve.tensor_copy(vT[:, g4 * 512:(g4 + 1) * 512],
                                                   ps_tp[:, 0:512])
                                    S.sig(ve, f"{Q}tpf")
                                S.on("T", tv)
                                S.on("V", vv2)
                    vttok = S.tok(f"{Q}tpf")
                    for h in range(2):
                        hofs = 64 * h
                        avwr = S.tok(f"d{Q}avh")
                        def mk_qi(qi_g):
                            runs, mtiles = attn_tiles(layer, qi_g)
                            ntot = runs[-1][0] + runs[-1][2]
                            par = qi_g % 2
                            full = ntot > 8
                            return dict(runs=runs, mtiles=mtiles, ntot=ntot,
                                        par=par, full=full,
                                        base=0 if full else 8 * par)
                        ctx = {}

                        def emitA(qi_g):
                            c = ctx[qi_g]
                            runs, ntot = c["runs"], c["ntot"]
                            base, par = c["base"], c["par"]
                            if c["full"]:
                                scfree = (exp_par[0], exp_par[1], exp_full_box[0])
                            else:
                                scfree = (exp_par[par], exp_full_box[0])
                            qsl = slice(qi_g * 128, (qi_g + 1) * 128)
                            def tqk(te, runs=runs, qsl=qsl, hofs=hofs, base=base,
                                    scfree=scfree, vttok=vttok,
                                    lastsp=S.tok(f"{Q}spl")):
                                S.wait(te, lastsp)
                                S.wait(te, vttok)
                                for tk in scfree:
                                    S.wait(te, tk)
                                for (slot0, kt0, n) in runs:
                                    for pss in range(3):
                                        lh = (qh, ql, qh)[pss]
                                        rh = (kh2, kh2, kl2)[pss]
                                        lt = lh[hofs:hofs + 64, qsl]
                                        done = 0
                                        while done < n:
                                            pos = base + slot0 + done
                                            take = min(4 - pos % 4, n - done)
                                            te.matmul(
                                                ps_sc[:, pos * 128:(pos + take) * 128],
                                                lt,
                                                rh[hofs:hofs + 64,
                                                   (kt0 + done) * 128:(kt0 + done + take) * 128],
                                                start=(pss == 0), stop=(pss == 2))
                                            done += take
                                S.sig(te, f"{Q}qk")
                            S.on("T", tqk)
                            qktok = S.tok(f"{Q}qk")
                            def vmx(ve, mtiles=c["mtiles"], ntot=ntot, base=base,
                                    par=par, qktok=qktok, bvfree=exp_par[par]):
                                S.wait(ve, qktok)
                                S.wait(ve, bvfree)
                                for slot, mi in mtiles.items():
                                    cs = slice((base + slot) * 128, (base + slot + 1) * 128)
                                    ve.tensor_tensor(ps_sc[:, cs], ps_sc[:, cs],
                                                     c_masks[:, mi * 128:(mi + 1) * 128],
                                                     ALU.add)
                                S.fence(ve)
                                ve.tensor_reduce(biasv[:, 4 * par:4 * par + 1],
                                                 ps_sc[:, base * 128:(base + ntot) * 128],
                                                 AX.X, ALU.max)
                                S.fence(ve)
                                ve.tensor_scalar(biasv[:, 4 * par + 1:4 * par + 2],
                                                 biasv[:, 4 * par:4 * par + 1], -0.125, 0.0,
                                                 ALU.mult, ALU.add)
                                S.sig(ve, f"{Q}bv")
                            S.on("V", vmx)
                            bvtok = S.tok(f"{Q}bv")
                            if c["full"] and qi_g >= 1:
                                pn = (ctx[qi_g - 1]["ntot"] + 3) // 4
                                ttprev = (f"{Q}ptt", S.tok(f"{Q}ptt")[1] + pn)
                            else:
                                ttprev = S.tok(f"{Q}ptt")
                            def sexp(se, ntot=ntot, base=base, par=par, qi_g=qi_g,
                                     bvtok=bvtok, ttprev=ttprev):
                                S.wait(se, bvtok)
                                S.wait(se, ttprev)
                                cs = slice(base * 128, (base + ntot) * 128)
                                se.activation(p32[:, cs], ps_sc[:, cs],
                                              AF.Exp, bias=biasv[:, 4 * par + 1:4 * par + 2],
                                              scale=0.125,
                                              accum_out=dall[:, qi_g:qi_g + 1])
                                S.sig(se, f"{Q}exp")
                            S.on("S", sexp)
                            c["exptok"] = S.tok(f"{Q}exp")
                            if c["full"]:
                                exp_full_box[0] = exp_par[0] = exp_par[1] = c["exptok"]
                            else:
                                exp_par[par] = c["exptok"]

                        def emitB(qi_g):
                            c = ctx[qi_g]
                            runs, ntot, base = c["runs"], c["ntot"], c["base"]
                            exptok = c["exptok"]
                            for g0 in range(0, ntot, 4):
                                nt_ = min(4, ntot - g0)
                                def tpp(te, g0=g0, nt_=nt_, base=base, exptok=exptok,
                                        tf=S.tok(f"{Q}tpf")):
                                    S.wait(te, exptok)
                                    S.wait(te, tf)
                                    for i in range(nt_):
                                        slot = base + g0 + i
                                        te.transpose(ps_tp[:, i * 128:(i + 1) * 128],
                                                     p32[:, slot * 128:(slot + 1) * 128],
                                                     c_iden[:])
                                    S.sig(te, f"{Q}ptt")
                                def vpp(ve, g0=g0, nt_=nt_, base=base,
                                        tt=S.tok(f"{Q}ptt")):
                                    S.wait(ve, (f"{Q}ptt", tt[1] + 1))
                                    ve.tensor_copy(
                                        pT[:, (base + g0) * 128:(base + g0 + nt_) * 128],
                                        ps_tp[:, 0:nt_ * 128])
                                    S.sig(ve, f"{Q}tpf")
                                S.on("T", tpp)
                                S.on("V", vpp)
                            pttok = S.tok(f"{Q}tpf")
                            def tav(te, hofs=hofs, runs=runs, ntot=ntot, base=base,
                                    pttok=pttok, avev=S.tok(f"{Q}avev")):
                                S.wait(te, pttok)
                                S.wait(te, avev)
                                j = 0
                                for (slot0, kt0, n) in runs:
                                    for i in range(n):
                                        slot, kt = base + slot0 + i, kt0 + i
                                        te.matmul(ps_av[0:64, 0:128],
                                                  vT[:, kt * 128 + hofs:kt * 128 + hofs + 64],
                                                  pT[:, slot * 128:(slot + 1) * 128],
                                                  start=(j == 0), stop=(j == ntot - 1))
                                        j += 1
                                S.sig(te, f"{Q}avmm")
                            S.on("T", tav)
                            avtok = S.tok(f"{Q}avmm")
                            def vav(ve, qi_g=qi_g, avtok=avtok, avwr=avwr):
                                S.wait(ve, avtok)
                                if qi_g == 0:
                                    S.wait(ve, avwr)
                                ve.tensor_copy(av2[0:64, qi_g * 128:(qi_g + 1) * 128],
                                               ps_av[0:64, 0:128])
                                S.sig(ve, f"{Q}avev")
                            S.on("V", vav)

                        for qi_g in (0, 1):
                            ctx[qi_g] = mk_qi(qi_g)
                            emitA(qi_g)
                        for qi_g in range(16):
                            emitB(qi_g)
                            if qi_g + 2 < 16:
                                ctx[qi_g + 2] = mk_qi(qi_g + 2)
                                emitA(qi_g + 2)
                        exptok = S.tok(f"{Q}exp")
                        def tdt(te, exptok=exptok, tf=S.tok(f"{Q}tpf")):
                            S.wait(te, exptok)
                            S.wait(te, tf)
                            te.transpose(ps_tp[0:16, 0:128], dall[:, 0:16], c_iden[:])
                            S.sig(te, f"{Q}dtt")
                        def vdt(ve, dt=S.tok(f"{Q}dtt")):
                            S.wait(ve, (f"{Q}dtt", dt[1] + 1))
                            ve.tensor_copy(st32[0:16, 0:128], ps_tp[0:16, 0:128])
                            ftok = S.sig(ve, f"{Q}tpf")
                            S.wait(ve, ftok)
                            ve.reciprocal(st32[0:16, 128:256], st32[0:16, 0:128])
                            S.sig(ve, f"{Q}div")
                        def ydt(yq, h=h, dv=S.tok(f"{Q}div")):
                            S.wait(yq, (f"{Q}div", dv[1] + 1))
                            dst = bass.AP(rows.tensor, (4 + h) * NT, [[128, 16], [1, 128]])
                            S.inc(yq.dma_start(out=dst, in_=st32[0:16, 128:256]),
                                  f"d{Q}iv", 16)
                        S.on("T", tdt)
                        S.on("V", vdt)
                        S.on("Y", ydt)
                        ivtok = S.tok(f"d{Q}iv")
                        def ybc(yq, h=h, ivtok=ivtok, avn=S.tok(f"{Q}avn")):
                            S.wait(yq, ivtok)
                            if avn[1] > 0:
                                S.wait(yq, avn)
                            src = bass.AP(rows.tensor, (4 + h) * NT, [[0, 64], [1, 2048]])
                            S.inc(yq.dma_start(out=ivd[0:64, :], in_=src),
                                  f"d{Q}ivb", 16)
                        def vnr(ve, bt_=S.tok(f"d{Q}ivb"), ave=S.tok(f"{Q}avev")):
                            S.wait(ve, (f"d{Q}ivb", bt_[1] + 16))
                            S.wait(ve, ave)
                            ve.tensor_tensor(av2[0:64, :], av2[0:64, :], ivd[0:64, :],
                                             ALU.mult)
                            S.sig(ve, f"{Q}avn")
                        def yst(yq, hg=hg, h=h, avn=S.tok(f"{Q}avn")):
                            S.wait(yq, (f"{Q}avn", avn[1] + 1))
                            S.inc(yq.dma_start(out=avh[hg, 64 * h:64 * h + 64, :],
                                               in_=av2[0:64, :]), f"d{Q}avh", 16)
                        S.on("Y", ybc)
                        S.on("V", vnr)
                        S.on("Y", yst)
                subA.close()
                avhtok = S.tok(f"d{Q}avh")
                subB = ExitStack()
                wo32 = subB.enter_context(nc.sbuf_tensor(f"a{P}{s_}_wo", [128, 1024], F32))
                woh = subB.enter_context(nc.sbuf_tensor(f"a{P}{s_}_woh", [128, 1024], BF16))
                wol = subB.enter_context(nc.sbuf_tensor(f"a{P}{s_}_wol", [128, 1024], BF16))
                avh_t = subB.enter_context(nc.sbuf_tensor(f"a{P}{s_}_avht", [128, 32 * 512], BF16))
                avl_t = subB.enter_context(nc.sbuf_tensor(f"a{P}{s_}_avlt", [128, 32 * 512], BF16))
                avt = subB.enter_context(nc.sbuf_tensor(f"a{P}{s_}_avt", [128, 2 * 512], F32))
                xr = subB.enter_context(nc.sbuf_tensor(f"a{P}{s_}_xr", [128, 2 * 512], F32))
                yo = subB.enter_context(nc.sbuf_tensor(f"a{P}{s_}_yo", [128, 2 * 512], F32))
                for hg in range(8):
                    for tt in range(4):
                        sl = (hg * 4 + tt) % 2
                        def ya(yq, hg=hg, tt=tt, sl=sl, avhtok=avhtok,
                               af=S.tok(f"{Q}avsp")):
                            S.wait(yq, avhtok)
                            if af[1] >= 2:
                                S.wait(yq, (af[0], af[1] - 1))
                            S.inc(yq.dma_start(out=avt[:, sl * 512:(sl + 1) * 512],
                                               in_=avh[hg, :, tt * 512:(tt + 1) * 512]),
                                  f"d{Q}av{sl}", 16)
                        def vsp2(ve, hg=hg, tt=tt, sl=sl):
                            S.wait(ve, S.tok(f"d{Q}av{sl}"))
                            hs = slice((hg * 4 + tt) * 512, (hg * 4 + tt + 1) * 512)
                            ve.tensor_copy(avh_t[:, hs], avt[:, sl * 512:(sl + 1) * 512])
                            ve.tensor_tensor(avl_t[:, hs], avt[:, sl * 512:(sl + 1) * 512],
                                             avh_t[:, hs], ALU.subtract)
                            S.sig(ve, f"{Q}avsp")
                        S.on("Y", ya)
                        S.on("V", vsp2)
                asptok = S.tok(f"{Q}avsp")
                for oc in range(NCH):
                    def ywo(yq, oc=oc, ow=S.tok(f"{Q}omm")):
                        S.wait(yq, ow)
                        src = bass.AP(wsrc["o"].tensor, oc * 128,
                                      [[1024, 128], [128 * 1024, 8], [1, 128]])
                        S.inc(yq.dma_start(
                            out=wo32[:].rearrange("p (k f) -> p k f", k=8), in_=src),
                            f"d{Q}wo", 16)
                    def vwo(ve):
                        S.wait(ve, S.tok(f"d{Q}wo"))
                        ve.tensor_copy(woh[:], wo32[:])
                        ve.tensor_tensor(wol[:], wo32[:], woh[:], ALU.subtract)
                        S.sig(ve, f"{Q}wosp")
                    S.on("Y", ywo)
                    S.on("V", vwo)
                    wotok = S.tok(f"{Q}wosp")
                    for tt in range(4):
                        it = oc * 4 + tt
                        pjb = pj[it % 2]
                        def t2(te, oc=oc, tt=tt, pjb=pjb, asptok=asptok, wotok=wotok,
                               of=S.tok(f"{Q}oev")):
                            S.wait(te, asptok)
                            S.wait(te, wotok)
                            if of[1] >= 2:
                                S.wait(te, (of[0], of[1] - 1))
                            for hg in range(NCH):
                                hs = slice((hg * 4 + tt) * 512, (hg * 4 + tt + 1) * 512)
                                wk = slice(hg * 128, (hg + 1) * 128)
                                te.matmul(pjb[:, 0:512], woh[:, wk], avh_t[:, hs],
                                          start=(hg == 0), stop=False)
                                te.matmul(pjb[:, 0:512], woh[:, wk], avl_t[:, hs],
                                          start=False, stop=False)
                                te.matmul(pjb[:, 0:512], wol[:, wk], avh_t[:, hs],
                                          start=False, stop=(hg == NCH - 1))
                            S.sig(te, f"{Q}omm")
                        S.on("T", t2)
                        otok = S.tok(f"{Q}omm")
                        def so(se, it=it, oc=oc, pjb=pjb, otok=otok,
                               syo=S.tok(f"d{Q}yo")):
                            S.wait(se, otok)
                            if syo[1] >= 32:
                                S.wait(se, (syo[0], syo[1] - 16))
                            dst = yo[:, (it % 2) * 512:(it % 2) * 512 + 512]
                            if L1:
                                se.activation(dst, pjb[:, 0:512], AF.Identity,
                                              bias=c_b["o"][:, oc:oc + 1])
                            else:
                                se.activation(dst, pjb[:, 0:512], AF.Identity)
                            S.sig(se, f"{Q}oev")
                        def yx(yq, it=it, oc=oc, tt=tt, t0=t0, xf=S.tok(f"{Q}yadd")):
                            if xf[1] >= 2:
                                S.wait(yq, (xf[0], xf[1] - 1))
                            S.inc(yq.dma_start(
                                out=xr[:, (it % 2) * 512:(it % 2) * 512 + 512],
                                in_=xsrc[oc, :, t0 + tt * 512:t0 + (tt + 1) * 512]),
                                f"d{Q}xr", 16)
                        def vy(ve, it=it, oevt=S.tok(f"{Q}oev"), syo=S.tok(f"d{Q}yo")):
                            S.wait(ve, (f"{Q}oev", oevt[1] + 1))
                            S.wait(ve, S.tok(f"d{Q}xr"))
                            if syo[1] >= 32:
                                S.wait(ve, (syo[0], syo[1] - 16))
                            sl2 = (it % 2) * 512
                            ve.tensor_tensor(yo[:, sl2:sl2 + 512], yo[:, sl2:sl2 + 512],
                                             xr[:, sl2:sl2 + 512], ALU.add)
                            S.sig(ve, f"{Q}yadd")
                        def ys(yq, it=it, oc=oc, tt=tt, t0=t0, ya_=S.tok(f"{Q}yadd")):
                            S.wait(yq, (f"{Q}yadd", ya_[1] + 1))
                            sl2 = (it % 2) * 512
                            S.inc(yq.dma_start(
                                out=xdst[oc, :, t0 + tt * 512:t0 + (tt + 1) * 512],
                                in_=yo[:, sl2:sl2 + 512]), f"d{Q}yo", 16)
                        S.on("S", so)
                        S.on("Y", yx)
                        S.on("V", vy)
                        S.on("Y", ys)
                subB.close()
                def yend(yq, Q=Q):
                    S.wait(yq, S.tok(f"d{Q}yo"))
                S.on("Y", yend)

        def emit_ffn1(xsrc, xdst, P, ph):
            stsub = ExitStack()
            psg = stsub.enter_context(nc.psum_tensor(f"g{P}_psg", [128, 512], F32))
            psh = stsub.enter_context(nc.psum_tensor(f"g{P}_psh", [128, 512], F32))
            _, rtok = stats_pass(xsrc, 0, NT, f"{P}st", ph, psg, psh)
            stsub.close()
            pj = [ph.enter_context(nc.psum_tensor(f"g{P}_pj{i}", [128, 512], F32))
                  for i in range(2)]
            sw = ph.enter_context(nc.sbuf_tensor(f"g{P}_sw", [128, NCH * 1024], BF16))
            swr = ph.enter_context(nc.sbuf_tensor(f"g{P}_swr", [128, 2 * 1024], F32))
            mb = ph.enter_context(nc.sbuf_tensor(f"g{P}_mb", [128, 2048], F32))
            rb = ph.enter_context(nc.sbuf_tensor(f"g{P}_rb", [128, 2048], F32))
            lnh = ph.enter_context(nc.sbuf_tensor(f"g{P}_lnh", [128, NCH * 2048], BF16))
            lnl = ph.enter_context(nc.sbuf_tensor(f"g{P}_lnl", [128, NCH * 2048], BF16))
            xt = ph.enter_context(nc.sbuf_tensor(f"g{P}_xt", [128, 2 * 2048], F32))
            st32 = ph.enter_context(nc.sbuf_tensor(f"g{P}_st", [128, 2048], F32))
            xr = ph.enter_context(nc.sbuf_tensor(f"g{P}_xr", [128, 2 * 512], F32))
            yo = ph.enter_context(nc.sbuf_tensor(f"g{P}_yo", [128, 2 * 512], F32))
            for kc in range(NCH):
                sl = kc % 2
                def yw(yq, kc=kc, sl=sl, sg=S.tok(f"{P}sg")):
                    if sg[1] >= 2:
                        S.wait(yq, (sg[0], sg[1] - 1))
                    S.inc(yq.dma_start(out=swr[:, sl * 1024:(sl + 1) * 1024],
                                       in_=wfT[kc * 128:(kc + 1) * 128, :]), f"d{P}w{sl}", 16)
                def ssg(se, kc=kc, sl=sl):
                    S.wait(se, S.tok(f"d{P}w{sl}"))
                    se.activation(sw[:, kc * 1024:(kc + 1) * 1024],
                                  swr[:, sl * 1024:(sl + 1) * 1024], AF.Sign)
                    S.sig(se, f"{P}sg")
                S.on("Y", yw)
                S.on("S", ssg)
            swtok = S.tok(f"{P}sg")
            for half in range(2):
                t0 = half * 2048
                Q = f"{P}h{half}"
                def yb(yq, t0=t0, lc=S.tok(f"{P}lnc")):
                    S.wait(yq, rtok)
                    if lc[1] > 0:
                        S.wait(yq, lc)
                    bcast_row(yq, mb[:, :], 0, t0, 2048, f"d{Q}b")
                    bcast_row(yq, rb[:, :], 1, t0, 2048, f"d{Q}b")
                S.on("Y", yb)
                btok = S.tok(f"d{Q}b")
                xslot = [None, None]
                for c in range(NCH):
                    sl = c % 2
                    ss = slice(sl * 2048, sl * 2048 + 2048)
                    def y2(yq, c=c, t0=t0, ss=ss, xs=xslot[sl]):
                        S.wait(yq, xs)
                        S.inc(yq.dma_start(out=xt[:, ss], in_=xsrc[c, :, t0:t0 + 2048]),
                              f"d{Q}x{sl}", 16)
                    def v(ve, c=c, sl=sl, ss=ss, btok=btok, mmprev=S.tok(f"{P}fmm")):
                        S.wait(ve, btok)
                        S.wait(ve, S.tok(f"d{Q}x{sl}"))
                        S.wait(ve, mmprev)
                        ve.tensor_tensor(st32[:], xt[:, ss], mb[:], ALU.subtract)
                        S.sig(ve, f"{Q}xf")
                        ve.tensor_tensor(st32[:], st32[:], rb[:], ALU.mult)
                        cs = slice(c * 2048, (c + 1) * 2048)
                        ve.tensor_copy(lnh[:, cs], st32[:])
                        ve.tensor_tensor(lnl[:, cs], st32[:], lnh[:, cs], ALU.subtract)
                        S.sig(ve, f"{P}lnc")
                    S.on("Y", y2)
                    S.on("V", v)
                    xslot[sl] = S.tok(f"{Q}xf")
                lntok = S.tok(f"{P}lnc")
                for oc in range(NCH):
                    for tq in range(4):
                        pjb = pj[tq % 2]
                        def t(te, oc=oc, tq=tq, pjb=pjb, lntok=lntok, swtok=swtok,
                              of=S.tok(f"{Q}fev")):
                            S.wait(te, lntok)
                            S.wait(te, swtok)
                            if of[1] >= 2:
                                S.wait(te, (of[0], of[1] - 1))
                            for kc in range(NCH):
                                rs = slice(kc * 2048 + tq * 512, kc * 2048 + tq * 512 + 512)
                                wk = slice(kc * 1024 + oc * 128, kc * 1024 + (oc + 1) * 128)
                                te.matmul(pjb[:, 0:512], sw[:, wk], lnh[:, rs],
                                          start=(kc == 0), stop=False)
                                te.matmul(pjb[:, 0:512], sw[:, wk], lnl[:, rs],
                                          start=False, stop=(kc == NCH - 1))
                            S.sig(te, f"{P}fmm")
                        S.on("T", t)
                        mmtok = S.tok(f"{P}fmm")
                        def yx(yq, oc=oc, tq=tq, t0=t0, xf=S.tok(f"{Q}yadd")):
                            if xf[1] >= 2:
                                S.wait(yq, (xf[0], xf[1] - 1))
                            S.inc(yq.dma_start(
                                out=xr[:, (tq % 2) * 512:(tq % 2) * 512 + 512],
                                in_=xsrc[oc, :, t0 + tq * 512:t0 + (tq + 1) * 512]),
                                f"d{Q}xr", 16)
                        def vy(ve, oc=oc, tq=tq, pjb=pjb, mmtok=mmtok,
                               yst_=S.tok(f"d{Q}yo")):
                            S.wait(ve, mmtok)
                            S.wait(ve, S.tok(f"d{Q}xr"))
                            if yst_[1] >= 32:
                                S.wait(ve, (yst_[0], yst_[1] - 16))
                            sl2 = (tq % 2) * 512
                            ve.tensor_scalar(yo[:, sl2:sl2 + 512], pjb[:, 0:512], 0.5, 0.0,
                                             ALU.mult, ALU.add)
                            S.sig(ve, f"{Q}fev")
                            ve.tensor_tensor(yo[:, sl2:sl2 + 512], yo[:, sl2:sl2 + 512],
                                             xr[:, sl2:sl2 + 512], ALU.add)
                            S.sig(ve, f"{Q}yadd")
                        def ys(yq, oc=oc, tq=tq, t0=t0, ya_=S.tok(f"{Q}yadd")):
                            S.wait(yq, (f"{Q}yadd", ya_[1] + 1))
                            sl2 = (tq % 2) * 512
                            S.inc(yq.dma_start(
                                out=xdst[oc, :, t0 + tq * 512:t0 + (tq + 1) * 512],
                                in_=yo[:, sl2:sl2 + 512]), f"d{Q}yo", 16)
                        S.on("Y", yx)
                        S.on("V", vy)
                        S.on("Y", ys)
                def yend(yq, Q=Q):
                    S.wait(yq, S.tok(f"d{Q}yo"))
                S.on("Y", yend)

        def run_phase(fn, *args):
            with ExitStack() as ph:
                fn(*args, ph)
            barrier()
            S.end_phase()

        run_phase(emit_dequant)
        seqp = [(emit_ffn0, (xin, 0, "A")), (emit_attn, (None, 0, "M")),
                (emit_ffn0, (None, 4, "B")), (emit_ffn1, (None, "C")),
                (emit_attn, (None, 1, "N")), (emit_ffn1, (None, "E"))]
        srcs = [xin, xres[1], xres[2], xres[3], xres[4], xres[5]]
        nph = min(upto, 6)
        for i in range(nph):
            fn, args = seqp[i]
            src = srcs[i]
            dst = xout if i == nph - 1 else srcs[i + 1]
            if fn is emit_ffn0:
                run_phase(fn, src, dst, args[1], args[2])
            elif fn is emit_attn:
                run_phase(fn, src, dst, args[1], args[2])
            else:
                run_phase(fn, src, dst, args[1])

        if dump_outs:
            barrier()
            def ydump(yq):
                for nm, dst in dump_outs.items():
                    S.inc(yq.dma_start(out=dst, in_=named[nm][0]), "ddump", 16)
                S.wait(yq, S.tok("ddump"))
            S.on("Y", ydump)
        barrier()

    print("instr-sections per engine:", S.ni)
    return nc



def _mask_tiles():
    r = np.arange(128)[:, None]
    c = np.arange(128)[None, :]
    vis_d = c <= r
    vis_e = c >= r + 1
    vis_es = vis_e | (c < 4)
    vis_s = np.broadcast_to(c < 4, (128, 128))
    m = np.zeros((4, 128, 128), np.float32)
    for i, vis in enumerate((vis_d, vis_e, vis_es, vis_s)):
        m[i] = np.where(vis, 0.0, NEGINF).astype(np.float32)
    return m


def _prep_inputs(inputs):
    common = {
        "w_up": np.ascontiguousarray(np.asarray(inputs["L0_ffn_w_up"], np.float32)),
        "w_gate": np.ascontiguousarray(np.asarray(inputs["L0_ffn_w_gate"], np.float32)),
        "w_down": np.ascontiguousarray(np.asarray(inputs["L0_ffn_w_down"], np.float32)),
        "l0_wqT": np.ascontiguousarray(np.asarray(inputs["L0_attn_Wq"], np.float32).T),
        "l0_wkT": np.ascontiguousarray(np.asarray(inputs["L0_attn_Wk"], np.float32).T),
        "l0_wvT": np.ascontiguousarray(np.asarray(inputs["L0_attn_Wv"], np.float32).T),
        "l0_woT": np.ascontiguousarray(np.asarray(inputs["L0_attn_Wo"], np.float32).T),
        "l1_wqT": np.ascontiguousarray(
            np.transpose(np.asarray(inputs["L1_attn_W_Q"], np.float32), (1, 0, 2)).reshape(D, D)),
        "l1_wkT": np.ascontiguousarray(
            np.transpose(np.asarray(inputs["L1_attn_W_K"], np.float32), (1, 0, 2)).reshape(D, D)),
        "l1_wvT": np.ascontiguousarray(
            np.transpose(np.asarray(inputs["L1_attn_W_V"], np.float32), (1, 0, 2)).reshape(D, D)),
        "l1_woT": np.ascontiguousarray(np.asarray(inputs["L1_attn_W_O"], np.float32)),
        "wfT": np.ascontiguousarray(np.asarray(inputs["L1_ffn_weight"], np.float32).T),
        "iden32": np.eye(128, dtype=np.float32),
        "idenb": np.eye(128, dtype=ml_dtypes.bfloat16),
        "ones32": np.ones((128, 1), np.float32),
        "masks": _mask_tiles(),
    }
    for p, key in (("q", "L1_attn_b_Q"), ("k", "L1_attn_b_K"), ("v", "L1_attn_b_V")):
        b = np.asarray(inputs[key], np.float32).reshape(16, 64)
        bb = np.zeros((128, 8), np.float32)
        for hg in range(8):
            bb[0:64, hg] = b[2 * hg]
            bb[64:128, hg] = b[2 * hg + 1]
        common[f"l1_b{p}"] = bb
    bo = np.asarray(inputs["L1_attn_b_O"], np.float32).reshape(8, 128)
    common["l1_bo"] = np.ascontiguousarray(bo.T)
    x = np.asarray(inputs["L0_attn_x"], np.float32)
    in_maps = []
    for core in range(8):
        xc = x[2 * core:2 * core + 2]
        xt = np.transpose(xc, (2, 0, 1)).reshape(D, NT)
        m = dict(common)
        m["xT"] = np.ascontiguousarray(xt.reshape(NCH, 128, NT))
        in_maps.append(m)
    return in_maps


def kernel(**inputs):
    in_maps = _prep_inputs(inputs)
    nc = build(upto=7)
    res = run_bass_kernel_spmd(nc, in_maps, core_ids=list(range(8)))
    outs = []
    for core in range(8):
        o = np.asarray(res.results[core]["outT"]).reshape(D, 2, SEQ)
        outs.append(np.transpose(o, (1, 2, 0)))
    return np.ascontiguousarray(np.concatenate(outs, axis=0)).astype(np.float32)
